# version 22
# baseline (speedup 1.0000x reference)
import contextlib
import math
import numpy as np
import concourse.bass as bass
import concourse.mybir as mybir
from concourse.bass_utils import run_bass_kernel_spmd

F32 = mybir.dt.float32
BF16 = mybir.dt.bfloat16
I32 = mybir.dt.int32
AF = mybir.ActivationFunctionType
ALU = mybir.AluOpType
AX = mybir.AxisListType

ENGS = ("tensor", "vector", "scalar", "gpsimd", "sync")



class Buf:
    __slots__ = ("name", "last_w", "readers")

    def __init__(self, name=""):
        self.name = name
        self.last_w = None
        self.readers = {}


class KB:
    def __init__(self, nc, es):
        self.nc = nc
        self.es = es
        self.sems = {}
        self.cnt = {}
        self.known = {e: {} for e in ENGS}
        self.q = {e: [] for e in ENGS}
        for e in ENGS:
            self.mk_sem("e_" + e)

    def mk_sem(self, name):
        self.sems[name] = self.es.enter_context(self.nc.semaphore(name))
        self.cnt[name] = 0
        return name

    def wait(self, ename, sem_name, val):
        k = self.known[ename]
        if k.get(sem_name, 0) < val:
            sem = self.sems[sem_name]
            self.q[ename].append(lambda eng, s=sem, v=val: eng.wait_ge(s, v))
            k[sem_name] = val

    def _pre(self, ename, reads, writes, exempt_self_waw=False, dma_sem=None):
        need = {}

        def add(item, waw=False):
            if item is None:
                return
            sn, v = item
            if exempt_self_waw and sn == "e_" + ename:
                return
            if waw and dma_sem is not None and sn == dma_sem:
                return
            if "@" in sn:
                v = max(v, self.cnt[sn])
            if need.get(sn, 0) < v:
                need[sn] = v

        for b in reads:
            add(b.last_w)
        for b in writes:
            add(b.last_w, waw=True)
            for sn, v in b.readers.items():
                add((sn, v))
        for sn in sorted(need):
            self.wait(ename, sn, need[sn])

    def _post(self, ename, f, sem_name, inc, reads, writes):
        self.cnt[sem_name] += inc
        v = self.cnt[sem_name]
        sem = self.sems[sem_name]
        self.q[ename].append(lambda eng, s=sem, i=inc: f(eng).then_inc(s, i))
        for b in writes:
            b.last_w = (sem_name, v)
            b.readers = {}
        for b in reads:
            if b.readers.get(sem_name, 0) < v:
                b.readers[sem_name] = v

    def op(self, ename, f, reads=(), writes=(), accum=False):
        self._pre(ename, reads, writes, exempt_self_waw=accum)
        self._post(ename, f, "e_" + ename, 1, reads, writes)

    def mm(self, psum_buf, out_ap, lhsT_ap, rhs_ap, reads=(), start=True, stop=True):
        self.op("tensor",
                lambda t: t.matmul(out_ap, lhsT_ap, rhs_ap, start=start, stop=stop),
                reads=reads, writes=(psum_buf,), accum=True)

    def mm_group(self, psum_buf, emits, reads=()):
        self._pre("tensor", reads, (psum_buf,), exempt_self_waw=True)
        for f in emits[:-1]:
            self.q["tensor"].append(f)
        self._post("tensor", emits[-1], "e_tensor", 1, reads, (psum_buf,))

    def dma(self, ename, stream, out, in_, reads=(), writes=()):
        key = f"{stream}@{ename}"
        if key not in self.sems:
            self.mk_sem(key)
        self._pre(ename, reads, writes, dma_sem=key)
        self._post(ename, lambda eng, o=out, i=in_: eng.dma_start(out=o, in_=i),
                   key, 16, reads, writes)

    def fence(self, engines=ENGS):
        for e in engines:
            for sn in sorted(self.cnt):
                if self.cnt[sn]:
                    self.wait(e, sn, self.cnt[sn])

    def final_wait(self, ename, streams):
        for sn in sorted(self.cnt):
            if sn.split("@")[0] in streams and self.cnt[sn]:
                self.wait(ename, sn, self.cnt[sn])

    def finish(self):
        with self.nc.Block() as block:
            for ename in ENGS:
                queue = self.q[ename]
                if queue:
                    getattr(block, ename)(lambda eng, qq=queue: [c(eng) for c in qq])



D = 1024
HEADS = 4
HDK = 256
HDV = 512
DV = 2048
LR = 16
II = 2816
CH = 128
TC = 512
LN16 = math.log(1.0 / 16.0)
MAGIC = 12582912.0
EPS = 1e-5
NCORE = 8


def build(T=4096, ncore=NCORE):
    NTC = T // TC
    NSEG = TC // CH
    nc = bass.Bass("TRN2", target_bir_lowering=False)

    dr = lambda n, s, dt=F32: nc.dram_tensor(n, s, dt, kind="ExternalInput").ap()
    xf = dr("xf", [D, T])
    xt = dr("xt", [T, D])
    wq = dr("wq", [D, D])
    wk = dr("wk", [D, D])
    wv = dr("wv", [D, DV])
    wg = dr("wg", [D, DV])
    wgk1 = dr("wgk1", [D, LR])
    wgk2 = dr("wgk2", [LR, D])
    bgk = dr("bgk", [128, 8])
    wo = dr("wo", [DV, D])
    rmsw = dr("rmsw", [128, DV])
    wgo = dr("wgo", [II, D])
    wuo = dr("wuo", [II, D])
    wdo = dr("wdo", [D, II])
    wg_sl = dr("wg_sl", [II, 128])
    wu_sl = dr("wu_sl", [II, 128])
    wd_sl = dr("wd_sl", [128, II])
    clip = dr("clip", [1, 2])
    ident = dr("ident", [128, 128], BF16)
    mask = dr("mask", [128, 128])
    ones1 = dr("ones1", [1, 128])

    out = nc.dram_tensor("out", [T, D], F32, kind="ExternalOutput").ap()

    ds = lambda n, s, dt=F32: nc.dram_tensor(n, s, dt)
    d_o = ds("d_o", [T, DV], BF16)
    d_ss = ds("d_ss", [T // 128, 128, HEADS])
    d_y1 = ds("d_y1", [T, D])
    d_y1T = ds("d_y1T", [D, T])
    d_h = ds("d_h", [II, T])
    d_g_sh = ds("d_g_sh", [22, 128, 128], BF16)
    d_u_sh = ds("d_u_sh", [22, 128, 128], BF16)
    d_d_sh = ds("d_d_sh", [22, 128, 128], BF16)
    assert ncore == NCORE, "sharded weight-quant requires 8 cores"
    g_wgq = ds("g_wgq", [8, 22, 128, 128], BF16)
    g_wuq = ds("g_wuq", [8, 22, 128, 128], BF16)
    g_wdq = ds("g_wdq", [8, 22, 128, 128], BF16)
    d_sd = ds("d_sd", [D])
    d_mx = ds("d_mx", [128])
    d_sc1v = ds("d_sc1v", [8])
    d_sc2v = ds("d_sc2v", [8])
    d_sdxv = ds("d_sdxv", [D])
    cc1i = ds("cc1i", [8])
    cc1o = ds("cc1o", [8])
    cc2i = ds("cc2i", [8])
    cc2o = ds("cc2o", [8])

    es = contextlib.ExitStack()
    with es:
        kb = KB(nc, es)
        SB = lambda n, s, dt=F32: es.enter_context(nc.sbuf_tensor(n, s, dt))
        nb = [0]

        def mkbuf(name=None):
            nb[0] += 1
            return Buf(name or f"b{nb[0]}")

        pt = [es.enter_context(nc.psum_tensor(f"p{i}", [128, 512], F32))
              for i in range(8)]
        pb = [mkbuf(f"p{i}") for i in range(8)]

        s_id = SB("s_id", [128, 128], BF16)
        s_idf = SB("s_idf", [128, 128])
        s_mask = SB("s_mask", [128, 128])
        s_ones = SB("s_ones", [1, 128])
        s_clip = SB("s_clip", [1, 2])
        s_sig = SB("s_sig", [1, 2])
        s_sc1 = SB("s_sc1", [1, 8])
        s_sc2 = SB("s_sc2", [1, 8])
        s_cc8 = SB("s_cc8", [1, 8])
        s_mrow = SB("s_mrow", [1, 128])
        s_col = SB("s_col", [128, 8])
        s_gc = SB("s_gc", [128, 24])
        s_uc = SB("s_uc", [128, 24])
        s_dc = SB("s_dc", [128, 8])
        s_gcs = SB("s_gcs", [128, 24])
        s_ucs = SB("s_ucs", [128, 24])
        s_max1 = SB("s_max1", [128, 34])
        s_max2 = SB("s_max2", [128, 178])

        Bg = {n: mkbuf(n) for n in
              ("id mask ones clip sig sc1 sc2 cc8 mrow col gc uc dc gcs ucs "
               "sdxr sdx dump wgw max1 max2 dmx cc1i cc1o cc2i cc2o "
               "dsd dsc1v dsc2v dsdxv gsh ush dsh gwg gwu gwd "
               "do0 do1 dss dy10 dy11 dh0 dh1 dout").split()}
        dump_bufs = [mkbuf("dump0"), mkbuf("dump1")]
        dump_i = [0]

        def dumpbuf():
            dump_i[0] ^= 1
            return dump_bufs[dump_i[0]]

        cc_sem = kb.mk_sem("cc_done")

        kb.dma("sync", "cst", s_id[:, :], ident, writes=(Bg["id"],))
        kb.dma("sync", "cst", s_mask[:, :], mask, writes=(Bg["mask"],))
        kb.dma("sync", "cst", s_ones[0:1, :], ones1, writes=(Bg["ones"],))
        kb.dma("sync", "cst", s_clip[0:1, :], clip, writes=(Bg["clip"],))
        kb.op("scalar", lambda s: s.activation(s_sig[0:1, :], s_clip[0:1, :], AF.Sigmoid),
              reads=(Bg["clip"],), writes=(Bg["sig"],))
        kb.op("vector", lambda v: v.memset(s_col[:, 3:4], -MAGIC), writes=(Bg["col"],))
        idfb = mkbuf("idf")
        kb.op("scalar", lambda s: s.copy(s_idf[:, :], s_id[:, :]),
              reads=(Bg["id"],), writes=(idfb,))
        kb.op("vector", lambda v: v.memset(s_max1[:, :], 0.0), writes=(Bg["max1"],))
        kb.op("vector", lambda v: v.memset(s_max2[:, :], 0.0), writes=(Bg["max2"],))
        kb.op("vector", lambda v: v.memset(s_cc8[0:1, :], 0.0), writes=(Bg["cc8"],))


        esAB = contextlib.ExitStack()
        s_xf = esAB.enter_context(nc.sbuf_tensor(
            "s_xf", [128, 2 * 8 * TC], BF16))
        s_wg = esAB.enter_context(nc.sbuf_tensor(
            "s_wg", [128, 8 * DV], BF16))
        with contextlib.ExitStack() as esA:
            SA = lambda n, s, dt=F32: esA.enter_context(nc.sbuf_tensor(n, s, dt))
            s_wq = SA("s_wq", [128, 8 * D], BF16)
            s_wk = SA("s_wk", [128, 8 * D], BF16)
            s_wv = SA("s_wv", [128, 8 * DV], BF16)
            s_wgk1 = SA("s_wgk1", [128, 8 * LR], BF16)
            s_wgk2 = SA("s_wgk2", [16, D], BF16)
            s_bgk = SA("s_bgk", [128, 8])
            s_z = SA("s_z", [16, TC], BF16)
            s_sg = SA("s_sg", [128, 2 * TC])
            s_sg8 = SA("s_sg8", [128, 8 * TC], BF16)
            s_B = SA("s_B", [128, 8 * TC])
            s_carry = SA("s_carry", [128, 8])
            s_nbq = SA("s_nbq", [128, 8])
            s_pbc = SA("s_pbc", [128, 8])
            s_basep = SA("s_basep", [128, 8])
            s_resc = SA("s_resc", [128, 8])
            s_Ep = SA("s_Ep", [128, TC], BF16)
            s_Em = SA("s_Em", [128, TC], BF16)
            s_q = SA("s_q", [128, 8 * TC], BF16)
            s_kd = SA("s_kd", [128, 8 * TC], BF16)
            s_v = SA("s_v", [128, 2 * DV], BF16)
            s_At = SA("s_At", [128, 2 * 128], BF16)
            s_kg = SA("s_kg", [128, 4 * 128], BF16)
            s_S = SA("s_S", [128, HEADS * 2 * HDV], BF16)
            s_o = SA("s_o", [128, DV], BF16)
            s_ss = SA("s_ss", [128, NSEG * HEADS])
            s_dumpA = SA("s_dumpA", [128, 512], BF16)

            wunits = []
            HWI = 1408
            s_wsl = SA("s_wsl", [128, II])
            s_wq8 = SA("s_wq8", [128, II], BF16)
            s_evt = SA("s_evt", [128, 512], BF16)
            s_rcol = SA("s_rcol", [128, 48])
            wst_b = [mkbuf("wst0"), mkbuf("wst1")]
            wq8_b = mkbuf("wq8")
            evt_b = mkbuf("evt")
            rcol_b = mkbuf("rcol")

            def w_scales(worig, n_tiles, hwid, sc_sbuf, sc_buf, rbase,
                         srow_d=None, srow_buf=None):
                for j in range(n_tiles):
                    def u(j=j):
                        if hwid <= HWI:
                            sl = j % 2
                            off = sl * HWI
                            kb.dma("sync", f"ws{sl}", s_wsl[:, off:off + hwid],
                                   worig[128 * j:128 * (j + 1), :], writes=(wst_b[sl],))
                            kb.op("vector", lambda v, j=j, off=off, sl=sl: v.reduce_max(
                                sc_sbuf[:, j:j + 1], s_wsl[:, off:off + hwid],
                                AX.X, apply_absolute_value=True),
                                reads=(wst_b[sl],), writes=(sc_buf,))
                        else:
                            nh = (hwid + HWI - 1) // HWI
                            for hh in range(nh):
                                w0 = HWI * hh
                                wl = min(hwid, HWI * (hh + 1)) - w0
                                sl = hh % 2
                                off = sl * HWI
                                kb.dma("sync", f"ws{sl}", s_wsl[:, off:off + wl],
                                       worig[128 * j:128 * (j + 1), w0:w0 + wl],
                                       writes=(wst_b[sl],))
                                kb.op("vector", lambda v, wl=wl, hh=hh, off=off,
                                      sl=sl: v.reduce_max(
                                    s_rcol[:, 45 + hh:46 + hh], s_wsl[:, off:off + wl],
                                    AX.X, apply_absolute_value=True),
                                    reads=(wst_b[sl],), writes=(rcol_b,))
                            for hh in range(1, nh):
                                kb.op("vector", lambda v, hh=hh: v.tensor_tensor(
                                    s_rcol[:, 45:46], s_rcol[:, 45:46],
                                    s_rcol[:, 45 + hh:46 + hh], ALU.max),
                                    reads=(rcol_b,), writes=(rcol_b,))
                            kb.op("vector", lambda v, j=j: v.tensor_copy(
                                sc_sbuf[:, j:j + 1], s_rcol[:, 45:46]),
                                reads=(rcol_b,), writes=(sc_buf,))
                    wunits.append(u)

                def u_fin():
                    kb.op("vector", lambda v: v.tensor_scalar(
                        sc_sbuf[:, 0:n_tiles], sc_sbuf[:, 0:n_tiles],
                        1.0 / 127.0, None, ALU.mult),
                        reads=(sc_buf,), writes=(sc_buf,))
                    if rbase is not None:
                        rc = s_rcol[:, rbase:rbase + n_tiles]
                        kb.op("vector", lambda v: v.reciprocal(
                            rc, sc_sbuf[:, 0:n_tiles]), reads=(sc_buf,), writes=(rcol_b,))
                        kb.op("vector", lambda v: v.tensor_tensor(
                            s_wsl[:, 0:n_tiles], sc_sbuf[:, 0:n_tiles], rc, ALU.mult),
                            reads=(sc_buf, rcol_b), writes=(wst_b[0],))
                        kb.op("vector", lambda v: v.tensor_scalar(
                            s_wsl[:, 0:n_tiles], s_wsl[:, 0:n_tiles], 2.0, -1.0,
                            ALU.subtract, ALU.mult), reads=(wst_b[0],), writes=(wst_b[0],))
                        kb.op("vector", lambda v: v.tensor_tensor(
                            rc, rc, s_wsl[:, 0:n_tiles], ALU.mult),
                            reads=(wst_b[0], rcol_b), writes=(rcol_b,))
                    if srow_d is not None:
                        for j in range(n_tiles):
                            kb.dma("sync", "trow", srow_d.ap()[128 * j:128 * (j + 1)],
                                   sc_sbuf[:, j:j + 1], reads=(sc_buf,), writes=(srow_buf,))
                wunits.append(u_fin)

            def w_shard_quant(wsl_dram, mode, rbase, sh_dram, sh_buf, g_dram, g_buf,
                              own_scale=False):
                def u_load():
                    if mode == "gu":
                        kb.dma("sync", "wsl", s_wsl[:, :].rearrange(
                            "p (it h) -> p it h", h=128),
                            wsl_dram.rearrange("(it p) h -> p it h", p=128),
                            writes=(wst_b[0], wst_b[1]))
                    else:
                        kb.dma("sync", "wsl", s_wsl[:, :], wsl_dram,
                               writes=(wst_b[0], wst_b[1]))
                wunits.append(u_load)
                if own_scale:
                    def u_osc():
                        kb.op("vector", lambda v: v.reduce_max(
                            s_rcol[:, 44:45], s_wsl[:, :], AX.X,
                            apply_absolute_value=True),
                            reads=(wst_b[0], wst_b[1]), writes=(rcol_b,))
                        kb.op("vector", lambda v: v.tensor_scalar(
                            s_rcol[:, 44:45], s_rcol[:, 44:45], 1.0 / 127.0, None,
                            ALU.mult), reads=(rcol_b,), writes=(rcol_b,))
                        kb.op("vector", lambda v: v.reciprocal(
                            s_rcol[:, 46:47], s_rcol[:, 44:45]),
                            reads=(rcol_b,), writes=(rcol_b,))
                        kb.op("vector", lambda v: v.tensor_tensor(
                            s_rcol[:, 47:48], s_rcol[:, 44:45], s_rcol[:, 46:47],
                            ALU.mult), reads=(rcol_b,), writes=(rcol_b,))
                        kb.op("vector", lambda v: v.tensor_scalar(
                            s_rcol[:, 47:48], s_rcol[:, 47:48], 2.0, -1.0,
                            ALU.subtract, ALU.mult), reads=(rcol_b,), writes=(rcol_b,))
                        kb.op("vector", lambda v: v.tensor_tensor(
                            s_rcol[:, 46:47], s_rcol[:, 46:47], s_rcol[:, 47:48],
                            ALU.mult), reads=(rcol_b,), writes=(rcol_b,))
                    wunits.append(u_osc)

                def u_quant():
                    if mode == "gu":
                        rv = bass.AP(s_rcol, rbase, [[48, 128], [1, 22], [0, 128]])
                        kb.op("vector", lambda v, rv=rv: v.tensor_tensor(
                            s_wsl[:, :].rearrange("p (it h) -> p it h", h=128),
                            s_wsl[:, :].rearrange("p (it h) -> p it h", h=128),
                            rv, ALU.mult),
                            reads=(wst_b[0], wst_b[1], rcol_b),
                            writes=(wst_b[0], wst_b[1]))
                    else:
                        kb.op("vector", lambda v: v.tensor_scalar(
                            s_wsl[:, :], s_wsl[:, :], s_rcol[:, 46:47], None,
                            ALU.mult), reads=(wst_b[0], wst_b[1], rcol_b),
                            writes=(wst_b[0], wst_b[1]))
                    kb.op("vector", lambda v: v.tensor_scalar(
                        s_wsl[:, :], s_wsl[:, :], 127.0, -128.0,
                        ALU.min, ALU.max),
                        reads=(wst_b[0], wst_b[1]), writes=(wst_b[0], wst_b[1]))
                    kb.op("vector", lambda v: v.tensor_scalar(
                        s_wsl[:, :], s_wsl[:, :], MAGIC, MAGIC,
                        ALU.add, ALU.subtract),
                        reads=(wst_b[0], wst_b[1]), writes=(wst_b[0], wst_b[1]))
                    kb.op("scalar", lambda s: s.copy(s_wq8[:, :], s_wsl[:, :]),
                          reads=(wst_b[0], wst_b[1]), writes=(wq8_b,))
                wunits.append(u_quant)

                for it0 in range(0, 22, 4):
                    def u_tr(it0=it0):
                        nb4 = min(4, 22 - it0)
                        bk = 2 + (it0 // 4) % 2
                        for z in range(nb4):
                            it = it0 + z
                            kb.mm(pb[bk], pt[bk][:, 128 * z:128 * (z + 1)],
                                  s_wq8[:, 128 * it:128 * (it + 1)], s_id[:, :],
                                  reads=(wq8_b, Bg["id"]), start=True, stop=True)
                        kb.op("vector", lambda v, bk=bk, nb4=nb4: v.tensor_copy(
                            s_evt[:, 0:128 * nb4], pt[bk][:, 0:128 * nb4]),
                            reads=(pb[bk],), writes=(evt_b,))
                        kb.dma("sync", "wsp",
                               sh_dram.ap()[it0:it0 + nb4, :, :].rearrange(
                                   "it p c -> p it c"),
                               s_evt[:, 0:128 * nb4].rearrange(
                                   "p (it c) -> p it c", c=128),
                               reads=(evt_b,), writes=(sh_buf,))
                    wunits.append(u_tr)

                def u_ag():
                    kb._pre("gpsimd", (sh_buf,), (g_buf,))
                    kb._post("gpsimd", lambda g2: g2.collective_compute(
                        "AllGather", ALU.bypass,
                        replica_groups=[list(range(ncore))],
                        ins=[sh_dram.ap()], outs=[g_dram.ap()]),
                        "cc_done", 1, (sh_buf,), (g_buf,))
                wunits.append(u_ag)

            w_scales(wgo, 22, D, s_gc, Bg["gc"], 0)
            w_shard_quant(wg_sl, "gu", 0, d_g_sh, Bg["gsh"], g_wgq, Bg["gwg"])
            w_scales(wuo, 22, D, s_uc, Bg["uc"], 22)
            w_shard_quant(wu_sl, "gu", 22, d_u_sh, Bg["ush"], g_wuq, Bg["gwu"])
            w_scales(wdo, 8, II, s_dc, Bg["dc"], None, d_sd, Bg["dsd"])
            w_shard_quant(wd_sl, "dn", 46, d_d_sh, Bg["dsh"], g_wdq, Bg["gwd"],
                          own_scale=True)
            wu_per_iter = (len(wunits) + NTC - 1) // NTC




            wqb, wkb, wvb = mkbuf("wq"), mkbuf("wk"), mkbuf("wv")
            wgk1b, wgk2b, bgkb = mkbuf(), mkbuf(), mkbuf()
            xfb = [mkbuf("xf0"), mkbuf("xf1")]
            zb = mkbuf()
            sgb = [mkbuf(), mkbuf()]
            sg8b = mkbuf("sg8")
            Bb = [mkbuf(f"B{d}") for d in range(8)]
            carryb = mkbuf("carry")
            nbqb, pbcb = mkbuf(), mkbuf()
            basepb, rescb = mkbuf("basep"), mkbuf("resc")
            Epb = [mkbuf(), mkbuf()]
            Emb = [mkbuf(), mkbuf()]
            qb = [mkbuf(f"q{d}") for d in range(8)]
            kdb = [mkbuf(f"kd{d}") for d in range(8)]
            vb = [mkbuf("v0"), mkbuf("v1")]
            Atb = [mkbuf(), mkbuf()]
            kgb = [mkbuf(), mkbuf()]
            Sb = [mkbuf(f"S{h}") for h in range(HEADS)]
            ob = [mkbuf("o0"), mkbuf("o1")]
            ssb = mkbuf("ss")

            kb.dma("gpsimd", "wA", s_wq[:, :].rearrange("p (k t) -> p k t", t=D),
                   wq.rearrange("(k p) t -> p k t", p=128), writes=(wqb,))
            kb.dma("gpsimd", "wA", s_wk[:, :].rearrange("p (k t) -> p k t", t=D),
                   wk.rearrange("(k p) t -> p k t", p=128), writes=(wkb,))
            kb.dma("gpsimd", "wA", s_wv[:, :].rearrange("p (k t) -> p k t", t=DV),
                   wv.rearrange("(k p) t -> p k t", p=128), writes=(wvb,))
            kb.dma("gpsimd", "wA", s_wgk1[:, :].rearrange("p (k t) -> p k t", t=LR),
                   wgk1.rearrange("(k p) t -> p k t", p=128), writes=(wgk1b,))
            kb.dma("gpsimd", "wA", s_wgk2[:, :], wgk2, writes=(wgk2b,))
            kb.dma("gpsimd", "wB", s_wg[:, :].rearrange("p (k t) -> p k t", t=DV),
                   wg.rearrange("(k p) t -> p k t", p=128), writes=(Bg["wgw"],))
            kb.dma("sync", "wA", s_bgk[:, :], bgk, writes=(bgkb,))
            kb.op("vector", lambda v: v.memset(s_S[:, :], 0.0),
                  writes=tuple(Sb))
            kb.op("vector", lambda v: v.memset(s_carry[:, :], 0.0),
                  writes=(carryb,))
            kb.op("vector", lambda v: v.memset(s_basep[:, :], 0.0),
                  writes=(basepb,))

            def xload(c8):
                xsl = c8 % 2
                xoff = xsl * 8 * TC
                t0 = c8 * TC
                kb.dma("gpsimd", f"xf{xsl}",
                       s_xf[:, xoff:xoff + 8 * TC].rearrange("p (k t) -> p k t", t=TC),
                       xf[:, t0:t0 + TC].rearrange("(k p) t -> p k t", p=128),
                       writes=(xfb[xsl],))

            xload(0)
            for c8 in range(NTC):
                xsl = c8 % 2
                xoff = xsl * 8 * TC
                t0 = c8 * TC
                xload(c8 + 1 if c8 + 1 < NTC else 0)

                kb.mm_group(pb[2], [
                    (lambda t, k=k, xoff=xoff: t.matmul(
                        pt[2][0:16, 0:TC], s_wgk1[:, LR * k:LR * (k + 1)],
                        s_xf[:, xoff + TC * k: xoff + TC * (k + 1)],
                        start=(k == 0), stop=(k == 7)))
                    for k in range(8)], reads=(wgk1b, xfb[xsl]))
                kb.op("vector", lambda v: v.tensor_copy(s_z[:, :], pt[2][0:16, 0:TC]),
                      reads=(pb[2],), writes=(zb,))

                for dt in range(8):
                    bk = 2 + (dt % 2)
                    kb.mm(pb[bk], pt[bk][:, 0:TC], s_wgk2[:, 128 * dt:128 * (dt + 1)],
                          s_z[:, :], reads=(wgk2b, zb))
                    kb.op("scalar", lambda s, bk=bk, dt=dt: s.activation(
                        s_sg8[:, TC * dt:TC * (dt + 1)], pt[bk][:, 0:TC], AF.Sigmoid,
                        bias=s_bgk[:, dt:dt + 1]),
                        reads=(pb[bk], bgkb), writes=(sg8b,))
                kb.op("vector", lambda v: v.tensor_tensor(
                    s_resc[:, :], s_carry[:, :], s_basep[:, :], ALU.subtract),
                    reads=(carryb, basepb), writes=(rescb,))
                kb.op("scalar", lambda s: s.activation(
                    s_resc[:, :], s_resc[:, :], AF.Exp, scale=1.0 / 16.0),
                    reads=(rescb,), writes=(rescb,))
                kb.op("vector", lambda v: v.tensor_copy(s_basep[:, :], s_carry[:, :]),
                      reads=(carryb,), writes=(basepb,))
                kb.op("vector", lambda v: v.tensor_scalar(
                    s_nbq[:, :], s_basep[:, :], -1.0 / 16.0, LN16,
                    ALU.mult, ALU.add), reads=(basepb,), writes=(nbqb,))
                kb.op("vector", lambda v: v.tensor_scalar(
                    s_pbc[:, :], s_basep[:, :], 1.0 / 16.0, None,
                    ALU.mult), reads=(basepb,), writes=(pbcb,))
                if c8 > 0:
                    for h in range(HEADS):
                        for i in range(2):
                            kb.op("vector", lambda v, h=h, i=i: v.tensor_scalar(
                                s_S[:, 2 * HDV * h + HDV * i: 2 * HDV * h + HDV * (i + 1)],
                                s_S[:, 2 * HDV * h + HDV * i: 2 * HDV * h + HDV * (i + 1)],
                                s_resc[:, 2 * h + i:2 * h + i + 1], None, ALU.mult),
                                reads=(rescb, Sb[h]), writes=(Sb[h],))
                for dt in range(8):
                    ssl = dt % 2
                    kb.op("scalar", lambda s, dt=dt, ssl=ssl: s.activation(
                        s_sg[:, TC * ssl:TC * (ssl + 1)],
                        s_sg8[:, TC * dt:TC * (dt + 1)], AF.Ln),
                        reads=(sg8b,), writes=(sgb[ssl],))
                    kb.op("vector", lambda v, dt=dt, ssl=ssl: v.tensor_tensor_scan(
                        s_B[:, TC * dt:TC * (dt + 1)],
                        s_sg[:, TC * ssl:TC * (ssl + 1)],
                        s_sg[:, TC * ssl:TC * (ssl + 1)],
                        s_carry[:, dt:dt + 1], ALU.add, ALU.bypass),
                        reads=(sgb[ssl], carryb), writes=(Bb[dt],))
                kb.op("vector", lambda v: v.tensor_copy(
                    s_carry[:, 0:8], s_B[:, TC - 1:8 * TC:TC]),
                    reads=tuple(Bb), writes=(carryb,))

                for dt in range(8):
                    bk = (0, 1, 4, 5)[dt % 4]
                    esl = 0
                    eo = 0
                    kb.op("scalar", lambda s, dt=dt, eo=eo: s.activation(
                        s_Ep[:, eo:eo + TC], s_B[:, TC * dt:TC * (dt + 1)],
                        AF.Exp, bias=s_nbq[:, dt:dt + 1], scale=1.0 / 16.0),
                        reads=(Bb[dt], nbqb), writes=(Epb[esl],))
                    kb.op("scalar", lambda s, dt=dt, eo=eo: s.activation(
                        s_Em[:, eo:eo + TC], s_B[:, TC * dt:TC * (dt + 1)],
                        AF.Exp, bias=s_pbc[:, dt:dt + 1], scale=-1.0 / 16.0),
                        reads=(Bb[dt], pbcb), writes=(Emb[esl],))
                    kb.mm_group(pb[bk], [
                        (lambda t, k=k, bk=bk, dt=dt, xoff=xoff: t.matmul(
                            pt[bk][:, 0:TC],
                            s_wq[:, D * k + 128 * dt: D * k + 128 * (dt + 1)],
                            s_xf[:, xoff + TC * k: xoff + TC * (k + 1)],
                            start=(k == 0), stop=(k == 7)))
                        for k in range(8)], reads=(wqb, xfb[xsl]))
                    kb.op("vector", lambda v, dt=dt, bk=bk, eo=eo:
                          v.scalar_tensor_tensor(
                        s_q[:, TC * dt:TC * (dt + 1)], pt[bk][:, 0:TC], 1.0,
                        s_Ep[:, eo:eo + TC], ALU.bypass, ALU.mult),
                        reads=(pb[bk], Epb[esl]), writes=(qb[dt],))
                    bk2 = (2, 3, 6, 7)[dt % 4]
                    kb.mm_group(pb[bk2], [
                        (lambda t, k=k, bk2=bk2, dt=dt, xoff=xoff: t.matmul(
                            pt[bk2][:, 0:TC],
                            s_wk[:, D * k + 128 * dt: D * k + 128 * (dt + 1)],
                            s_xf[:, xoff + TC * k: xoff + TC * (k + 1)],
                            start=(k == 0), stop=(k == 7)))
                        for k in range(8)], reads=(wkb, xfb[xsl]))
                    kb.op("vector", lambda v, dt=dt, bk2=bk2, eo=eo:
                          v.scalar_tensor_tensor(
                        s_kd[:, TC * dt:TC * (dt + 1)], pt[bk2][:, 0:TC], 1.0,
                        s_Em[:, eo:eo + TC], ALU.bypass, ALU.mult),
                        reads=(pb[bk2], Emb[esl]), writes=(kdb[dt],))

                def v_proj(g):
                    vsl = g % 2
                    for nuc in range(4):
                        bk = 2 + nuc % 2
                        kb.mm_group(pb[bk], [
                            (lambda t, k=k, bk=bk, g=g, nuc=nuc, xoff=xoff: t.matmul(
                                pt[bk][:, :],
                                s_xf[:, xoff + TC * k + CH * g: xoff + TC * k + CH * (g + 1)],
                                s_wv[:, DV * k + 512 * nuc: DV * k + 512 * (nuc + 1)],
                                start=(k == 0), stop=(k == 7)))
                            for k in range(8)], reads=(xfb[xsl], wvb))
                        kb.op("scalar", lambda s, vsl=vsl, nuc=nuc, bk=bk: s.copy(
                            s_v[:, DV * vsl + 512 * nuc: DV * vsl + 512 * (nuc + 1)],
                            pt[bk][:, :]), reads=(pb[bk],), writes=(vb[vsl],))

                def gla_stage1(it):
                    g, h = divmod(it, HEADS)
                    par = it % 2
                    bA = 4 if par == 0 else 6
                    d0, d1 = 2 * h, 2 * h + 1
                    kb.mm_group(pb[bA], [
                        (lambda t, i=i, dt=dt, g=g, bA=bA: t.matmul(
                            pt[bA][:, 0:128],
                            s_kd[:, TC * dt + CH * g: TC * dt + CH * (g + 1)],
                            s_q[:, TC * dt + CH * g: TC * dt + CH * (g + 1)],
                            start=(i == 0), stop=(i == 1)))
                        for i, dt in enumerate((d0, d1))],
                        reads=(kdb[d0], kdb[d1], qb[d0], qb[d1]))
                    for i, dt in enumerate((d0, d1)):
                        kb.mm_group(pb[bA], [
                            (lambda t, i=i, dt=dt, g=g, bA=bA: t.matmul(
                                pt[bA][:, 128 * (i + 1):128 * (i + 2)],
                                s_kd[:, TC * dt + CH * g: TC * dt + CH * (g + 1)],
                                s_id[:, :],
                                start=True, stop=True))],
                            reads=(kdb[dt], Bg["id"]))
                    kb.op("vector", lambda v, bA=bA, par=par: v.tensor_tensor(
                        s_At[:, 128 * par:128 * (par + 1)], pt[bA][:, 0:128],
                        s_mask[:, :], ALU.mult),
                        reads=(pb[bA], Bg["mask"]), writes=(Atb[par],))
                    kb.op("vector", lambda v, bA=bA, par=par: v.tensor_copy(
                        s_kg[:, 256 * par:256 * (par + 1)], pt[bA][:, 128:384]),
                        reads=(pb[bA],), writes=(kgb[par],))

                def gla_stage2(it):
                    g, h = divmod(it, HEADS)
                    par = it % 2
                    bO = 5 if par == 0 else 0
                    bP = (7, 3) if par == 0 else (1, 2)
                    osl = g % 2
                    d0, d1 = 2 * h, 2 * h + 1
                    vsl = g % 2
                    emits = [
                        (lambda t, vsl=vsl, h=h, par=par, bO=bO: t.matmul(
                            pt[bO][:, :], s_At[:, 128 * par:128 * (par + 1)],
                            s_v[:, DV * vsl + HDV * h: DV * vsl + HDV * (h + 1)],
                            start=True, stop=False))]
                    for i, dt in enumerate((d0, d1)):
                        emits.append(
                            lambda t, i=i, dt=dt, g=g, h=h, bO=bO: t.matmul(
                                pt[bO][:, :],
                                s_q[:, TC * dt + CH * g: TC * dt + CH * (g + 1)],
                                s_S[:, 2 * HDV * h + HDV * i: 2 * HDV * h + HDV * (i + 1)],
                                start=False, stop=(i == 1)))
                    kb.mm_group(pb[bO], emits,
                                reads=(Atb[par], vb[vsl], qb[d0], qb[d1], Sb[h]))
                    for i, dt in enumerate((d0, d1)):
                        kb.mm_group(pb[bP[i]], [
                            (lambda t, i=i, vsl=vsl, h=h, par=par, bk=bP[i]: t.matmul(
                                pt[bk][:, :], s_kg[:, 256 * par + 128 * i:256 * par + 128 * (i + 1)],
                                s_v[:, DV * vsl + HDV * h: DV * vsl + HDV * (h + 1)],
                                start=True, stop=True))],
                            reads=(kgb[par], vb[vsl]))
                    kb.op("scalar", lambda s, g=g, h=h, bO=bO: s.activation(
                        s_dumpA[:, 0:512], pt[bO][:, :], AF.Square,
                        accum_out=s_ss[:, HEADS * g + h:HEADS * g + h + 1]),
                        reads=(pb[bO],), writes=(ssb, dumpbuf()))
                    kb.op("scalar", lambda s, h=h, bO=bO: s.copy(
                        s_o[:, HDV * h: HDV * (h + 1)],
                        pt[bO][:, :]), reads=(pb[bO],), writes=(ob[0],))
                    for i in range(2):
                        kb.op("vector", lambda v, h=h, i=i, bk=bP[i]:
                              v.tensor_tensor(
                                  s_S[:, 2 * HDV * h + HDV * i: 2 * HDV * h + HDV * (i + 1)],
                                  s_S[:, 2 * HDV * h + HDV * i: 2 * HDV * h + HDV * (i + 1)],
                                  pt[bk][:, :], ALU.add),
                              reads=(pb[bP[i]], Sb[h]), writes=(Sb[h],))

                def o_spill(g):
                    osl = g % 2
                    kb.dma("sync", f"sp_o{osl}",
                           d_o.ap()[t0 + CH * g: t0 + CH * (g + 1), :],
                           s_o[:, :], reads=(ob[0],),
                           writes=(Bg[f"do{osl}"],))

                NIT = NSEG * HEADS
                for it in range(NIT):
                    if it % HEADS == 0:
                        v_proj(it // HEADS)
                    gla_stage1(it)
                    if wunits:
                        wunits.pop(0)()
                    if it > 0:
                        gla_stage2(it - 1)
                        if (it - 1) % HEADS == HEADS - 1:
                            o_spill((it - 1) // HEADS)
                gla_stage2(NIT - 1)
                o_spill(NSEG - 1)
                kb.dma("sync", "sp_ss",
                       d_ss.ap()[NSEG * c8:NSEG * (c8 + 1), :, :].rearrange(
                           "g p h -> p g h"),
                       s_ss[:, :], reads=(ssb,), writes=(Bg["dss"],))

            while wunits:
                wunits.pop(0)()
            kb.fence()

        with contextlib.ExitStack() as esB:
            SBp = lambda n, s, dt=F32: esB.enter_context(nc.sbuf_tensor(n, s, dt))
            s_rw = SBp("s_rw", [128, DV], BF16)
            s_oi = SBp("s_oi", [128, 2 * NSEG * DV], BF16)
            s_ssi = SBp("s_ssi", [128, 2 * NSEG * HEADS])
            s_wo = SBp("s_wo", [128, 16 * D], BF16)
            s_rst = SBp("s_rst", [128, NSEG * HEADS])
            s_sil = SBp("s_sil", [128, 4 * 512], BF16)
            s_gg = SBp("s_gg", [128, NSEG * DV], BF16)
            s_oop2 = SBp("s_oop2", [128, 2 * DV], BF16)
            s_oopT = SBp("s_oopT", [128, 2 * 16 * 128], BF16)
            s_xt = SBp("s_xt", [128, NSEG * D])
            s_y1 = SBp("s_y1", [128, 2 * NSEG * D])
            s_dumpB = SBp("s_dumpB", [128, 1024], BF16)
            s_lnc = SBp("s_lnc", [128, 2 * NSEG * 5])

            wob, rwb = mkbuf(), mkbuf()
            oib = [mkbuf(), mkbuf()]
            ssib = [mkbuf(), mkbuf()]
            rstb = mkbuf()
            silb = [mkbuf(), mkbuf(), mkbuf(), mkbuf()]
            ggb = [mkbuf(f"gg{g}") for g in range(NSEG)]
            oopb = [mkbuf(), mkbuf()]
            oopTb = [mkbuf(), mkbuf()]
            xtb = mkbuf("xt")
            y1b = [[mkbuf(f"y1{s}{g}") for g in range(NSEG)] for s in range(2)]
            lncb = mkbuf()

            kb.dma("gpsimd", "wB3",
                   s_wo[:, :].rearrange("p (k t) -> p k t", t=D),
                   wo.rearrange("(k p) t -> p k t", p=128), writes=(wob,))
            kb.dma("gpsimd", "wB2", s_rw[:, :], rmsw, writes=(rwb,))



            def bload(c8):
                t0 = c8 * TC
                xsl = c8 % 2
                kb.dma("sync", f"oB{xsl}",
                       s_oi[:, xsl * NSEG * DV:(xsl + 1) * NSEG * DV].rearrange(
                           "p (g v) -> p g v", v=DV),
                       d_o.ap()[t0:t0 + TC, :].rearrange("(g p) v -> p g v", p=128),
                       reads=(Bg["do0"], Bg["do1"]), writes=(oib[xsl],))
                nsh = NSEG * HEADS
                kb.dma("sync", f"oB{xsl}", s_ssi[:, xsl * nsh:(xsl + 1) * nsh],
                       d_ss.ap()[NSEG * c8:NSEG * (c8 + 1), :, :].rearrange(
                           "g p h -> p g h"),
                       reads=(Bg["dss"],), writes=(ssib[xsl],))

            bload(0)
            for c8 in range(NTC):
                t0 = c8 * TC
                xsl = c8 % 2
                xoffB = xsl * 8 * TC
                ooffB = xsl * NSEG * DV
                if c8 + 1 < NTC:
                    bload(c8 + 1)
                    xload(c8 + 1)
                kb.dma("sync", f"xtB{xsl}",
                       s_xt[:, :].rearrange("p (g v) -> p g v", v=D),
                       xt[t0:t0 + TC, :].rearrange("(g p) v -> p g v", p=128),
                       writes=(xtb,))
                nsh = NSEG * HEADS
                kb.op("vector", lambda v, xsl=xsl, nsh=nsh: v.tensor_scalar(
                    s_rst[:, :], s_ssi[:, xsl * nsh:(xsl + 1) * nsh],
                    1.0 / HDV, EPS, ALU.mult, ALU.add),
                    reads=(ssib[xsl],), writes=(rstb,))
                kb.op("scalar", lambda s: s.activation(
                    s_rst[:, :], s_rst[:, :], AF.Sqrt), reads=(rstb,), writes=(rstb,))
                kb.op("vector", lambda v: v.reciprocal(s_rst[:, :], s_rst[:, :]),
                      reads=(rstb,), writes=(rstb,))
                for g in range(NSEG):
                    for nuc in range(4):
                        bk = nuc % 2
                        ssl = nuc
                        kb.mm_group(pb[bk], [
                            (lambda t, k=k, bk=bk, g=g, nuc=nuc, xoffB=xoffB: t.matmul(
                                pt[bk][:, :],
                                s_xf[:, xoffB + TC * k + CH * g: xoffB + TC * k + CH * (g + 1)],
                                s_wg[:, DV * k + 512 * nuc: DV * k + 512 * (nuc + 1)],
                                start=(k == 0), stop=(k == 7)))
                            for k in range(8)], reads=(xfb[xsl], Bg["wgw"]))
                        kb.op("scalar", lambda s, bk=bk, ssl=ssl: s.activation(
                            s_sil[:, 512 * ssl:512 * (ssl + 1)], pt[bk][:, :], AF.Silu),
                            reads=(pb[bk],), writes=(silb[ssl],))
                        kb.op("vector", lambda v, g=g, nuc=nuc, ssl=ssl: v.tensor_tensor(
                            s_gg[:, DV * g + 512 * nuc:DV * g + 512 * (nuc + 1)],
                            s_sil[:, 512 * ssl:512 * (ssl + 1)],
                            s_rw[:, 512 * nuc:512 * (nuc + 1)], ALU.mult),
                            reads=(silb[ssl], rwb), writes=(ggb[g],))
                def b_oop(g):
                    for h in range(HEADS):
                        col = HEADS * g + h
                        kb.op("vector", lambda v, g=g, h=h, col=col,
                              ooffB=ooffB, xsl=xsl:
                              v.scalar_tensor_tensor(
                                  s_oop2[:, DV * (g % 2) + HDV * h: DV * (g % 2) + HDV * (h + 1)],
                                  s_oi[:, ooffB + DV * g + HDV * h: ooffB + DV * g + HDV * (h + 1)],
                                  s_rst[:, col:col + 1],
                                  s_gg[:, DV * g + HDV * h:DV * g + HDV * (h + 1)],
                                  ALU.mult, ALU.mult),
                              reads=(oib[xsl], rstb, ggb[g]), writes=(oopb[g % 2],))

                def b_tr(g):
                    osl = g % 2
                    for jb in range(4):
                        bk = 2 + jb % 2
                        for j4 in range(4):
                            j = 4 * jb + j4
                            kb.mm(pb[bk], pt[bk][:, 128 * j4:128 * (j4 + 1)],
                                  s_oop2[:, DV * osl + 128 * j: DV * osl + 128 * (j + 1)],
                                  s_id[:, :],
                                  reads=(oopb[osl], Bg["id"]), start=True, stop=True)
                        kb.op("vector", lambda v, jb=jb, bk=bk, osl=osl: v.tensor_copy(
                            s_oopT[:, 2048 * osl + 512 * jb:2048 * osl + 512 * (jb + 1)],
                            pt[bk][:, :]),
                            reads=(pb[bk],), writes=(oopTb[osl],))

                def b_out(g):
                    osl = g % 2
                    for dc in range(2):
                        bk = 4 + dc
                        kb.mm_group(pb[bk], [
                            (lambda t, k=k, bk=bk, dc=dc, osl=osl: t.matmul(
                                pt[bk][:, :],
                                s_oopT[:, 2048 * osl + 128 * k:2048 * osl + 128 * (k + 1)],
                                s_wo[:, D * k + 512 * dc: D * k + 512 * (dc + 1)],
                                start=(k == 0), stop=(k == 15)))
                            for k in range(16)], reads=(oopTb[osl], wob))
                        kb.op("vector", lambda v, g=g, dc=dc, bk=bk, xsl=xsl:
                              v.tensor_tensor(
                            s_y1[:, xsl * NSEG * D + D * g + 512 * dc:
                                 xsl * NSEG * D + D * g + 512 * (dc + 1)],
                            pt[bk][:, :],
                            s_xt[:, D * g + 512 * dc: D * g + 512 * (dc + 1)],
                            ALU.add), reads=(pb[bk], xtb), writes=(y1b[xsl][g],))
                    ysl = s_y1[:, xsl * NSEG * D + D * g:xsl * NSEG * D + D * (g + 1)]
                    c0 = 20 * xsl + 4 * g
                    kb.op("scalar", lambda s, ysl=ysl, c0=c0: s.activation(
                        s_dumpB[:, :], ysl, AF.Identity,
                        accum_out=s_lnc[:, c0:c0 + 1]),
                        reads=(y1b[xsl][g],), writes=(dumpbuf(), lncb))
                    kb.op("scalar", lambda s, ysl=ysl, c0=c0: s.activation(
                        s_dumpB[:, :], ysl, AF.Square,
                        accum_out=s_lnc[:, c0 + 1:c0 + 2]),
                        reads=(y1b[xsl][g],), writes=(dumpbuf(), lncb))

                def b_ln_a(c8):
                    t0 = c8 * TC
                    xsl = c8 % 2
                    yo = xsl * NSEG * D
                    lb = 20 * xsl
                    mu = s_lnc[:, lb + 0:lb + 16:4]
                    ss2 = s_lnc[:, lb + 1:lb + 16:4]
                    var = s_lnc[:, lb + 2:lb + 16:4]
                    rst = s_lnc[:, lb + 3:lb + 16:4]
                    kb.op("vector", lambda v: v.tensor_scalar(
                        mu, mu, 1.0 / D, None, ALU.mult), reads=(lncb,), writes=(lncb,))
                    kb.op("vector", lambda v: v.tensor_tensor(
                        var, mu, mu, ALU.mult), reads=(lncb,), writes=(lncb,))
                    kb.op("vector", lambda v: v.tensor_scalar(
                        ss2, ss2, 1.0 / D, EPS, ALU.mult, ALU.add),
                        reads=(lncb,), writes=(lncb,))
                    kb.op("vector", lambda v: v.tensor_tensor(
                        var, ss2, var, ALU.subtract), reads=(lncb,), writes=(lncb,))
                    kb.op("scalar", lambda s: s.activation(
                        rst, var, AF.Sqrt), reads=(lncb,), writes=(lncb,))
                    kb.op("vector", lambda v: v.reciprocal(rst, rst),
                          reads=(lncb,), writes=(lncb,))
                    kb.op("vector", lambda v, mu=mu, rst=rst, lb=lb:
                          v.scalar_tensor_tensor(
                        s_lnc[:, lb + 16:lb + 20], mu, -1.0, rst, ALU.mult, ALU.mult),
                        reads=(lncb,), writes=(lncb,))
                    for g in range(NSEG):
                        ysl = s_y1[:, yo + D * g:yo + D * (g + 1)]
                        c0 = lb + 4 * g
                        kb.op("scalar", lambda s, ysl=ysl, c0=c0, g=g, lb=lb, xsl=xsl:
                              s.activation(
                            ysl, ysl, AF.Identity, scale=s_lnc[:, c0 + 3:c0 + 4],
                            bias=s_lnc[:, lb + 16 + g:lb + 17 + g]),
                            reads=(y1b[xsl][g], lncb), writes=(y1b[xsl][g],))
                        kb.dma("sync", f"spy{xsl}",
                               d_y1.ap()[t0 + CH * g:t0 + CH * (g + 1), :],
                               ysl, reads=(y1b[xsl][g],), writes=(Bg[f"dy1{xsl}"],))

                def b_ln_b(c8):
                    xsl = c8 % 2
                    yo = xsl * NSEG * D
                    for g in range(NSEG):
                        ysl = s_y1[:, yo + D * g:yo + D * (g + 1)]
                        kb.op("vector", lambda v, ysl=ysl, col=NSEG * c8 + g, g=g,
                              xsl=xsl: v.reduce_max(
                            s_max1[:, col:col + 1], ysl, AX.X, apply_absolute_value=True),
                            reads=(y1b[xsl][g],), writes=(Bg["max1"],))

                b_oop(0)
                b_tr(0)
                b_oop(1)
                b_tr(1)
                b_out(0)
                if c8 > 0:
                    b_ln_a(c8 - 1)
                b_oop(2)
                b_tr(2)
                b_out(1)
                b_oop(3)
                if c8 > 0:
                    b_ln_b(c8 - 1)
                b_tr(3)
                b_out(2)
                b_out(3)
                if c8 == NTC - 1:
                    b_ln_a(c8)
                    b_ln_b(c8)
            kb.fence()

        esAB.close()
        def max_collective(s_maxt, ncols, cci, cco, s_scn, scn_buf, maxt_buf,
                           cc_in_b, cc_out_b, sig_col, tny, dscv, dscv_b):
            kb.op("vector", lambda v: v.reduce_max(
                s_maxt[:, ncols:ncols + 1], s_maxt[:, 0:ncols], AX.X),
                reads=(maxt_buf,), writes=(maxt_buf,))
            kb.dma("gpsimd", tny + "g", d_mx.ap(), s_maxt[:, ncols:ncols + 1],
                   reads=(maxt_buf,), writes=(Bg["dmx"],))
            kb.dma("gpsimd", tny + "g", s_mrow[0:1, :], d_mx.ap()[None, :],
                   reads=(Bg["dmx"],), writes=(Bg["mrow"],))
            kb.op("vector", lambda v: v.reduce_max(
                s_cc8[0:1, 0:1], s_mrow[0:1, :], AX.X),
                reads=(Bg["mrow"],), writes=(Bg["cc8"],))
            kb.dma("gpsimd", tny + "g", cci.ap(), s_cc8[0:1, :],
                   reads=(Bg["cc8"],), writes=(cc_in_b,))
            kb._pre("gpsimd", (cc_in_b,), (cc_out_b,))
            kb._post("gpsimd", lambda g: g.collective_compute(
                "AllReduce", ALU.max, replica_groups=[list(range(ncore))],
                ins=[cci.ap()], outs=[cco.ap()]),
                "cc_done", 1, (cc_in_b,), (cc_out_b,))
            kb.dma("gpsimd", tny + "g", s_scn[0:1, 0:1], cco.ap()[None, 0:1],
                   reads=(cc_out_b,), writes=(scn_buf,))
            kb.op("vector", lambda v: v.tensor_tensor(
                s_scn[0:1, 1:2], s_scn[0:1, 0:1], s_sig[0:1, sig_col:sig_col + 1],
                ALU.mult), reads=(scn_buf, Bg["sig"]), writes=(scn_buf,))
            kb.op("vector", lambda v: v.tensor_scalar(
                s_scn[0:1, 1:2], s_scn[0:1, 1:2], 1.0 / 127.0, None, ALU.mult),
                reads=(scn_buf,), writes=(scn_buf,))
            kb.op("vector", lambda v: v.reciprocal(
                s_scn[0:1, 2:3], s_scn[0:1, 1:2]), reads=(scn_buf,), writes=(scn_buf,))
            kb.op("vector", lambda v: v.tensor_tensor(
                s_scn[0:1, 3:4], s_scn[0:1, 1:2], s_scn[0:1, 2:3], ALU.mult),
                reads=(scn_buf,), writes=(scn_buf,))
            kb.op("vector", lambda v: v.tensor_scalar(
                s_scn[0:1, 3:4], s_scn[0:1, 3:4], 2.0, -1.0,
                ALU.subtract, ALU.mult), reads=(scn_buf,), writes=(scn_buf,))
            kb.op("vector", lambda v: v.tensor_tensor(
                s_scn[0:1, 2:3], s_scn[0:1, 2:3], s_scn[0:1, 3:4], ALU.mult),
                reads=(scn_buf,), writes=(scn_buf,))
            kb.dma("gpsimd", tny + "g", dscv.ap()[0:4], s_scn[0:1, 0:4],
                   reads=(scn_buf,), writes=(dscv_b,))

        sc1b, sc2b = Bg["sc1"], Bg["sc2"]
        max_collective(s_max1, NSEG * NTC, cc1i, cc1o, s_sc1, sc1b, Bg["max1"],
                       Bg["cc1i"], Bg["cc1o"], 0, "tnyA", d_sc1v, Bg["dsc1v"])
        colb = Bg["col"]
        kb.dma("gpsimd", "tnyAg", s_col[:, 0:1],
               bass.AP(d_sc1v, 1, [[0, 128], [1, 1]]),
               reads=(Bg["dsc1v"],), writes=(colb,))
        kb.dma("gpsimd", "tnyAg", s_col[:, 1:2],
               bass.AP(d_sc1v, 2, [[0, 128], [1, 1]]),
               reads=(Bg["dsc1v"],), writes=(colb,))
        kb.op("vector", lambda v: v.tensor_scalar(
            s_gcs[:, 0:22], s_gc[:, 0:22], s_col[:, 0:1], None, ALU.mult),
            reads=(Bg["gc"], colb), writes=(Bg["gcs"],))
        kb.op("vector", lambda v: v.tensor_scalar(
            s_ucs[:, 0:22], s_uc[:, 0:22], s_col[:, 0:1], None, ALU.mult),
            reads=(Bg["uc"], colb), writes=(Bg["ucs"],))

        dy1Tb = [mkbuf("dy1T0"), mkbuf("dy1T1")]
        with contextlib.ExitStack() as esT:
            ST = lambda n, s, dt=F32: esT.enter_context(nc.sbuf_tensor(n, s, dt))
            s_ty = ST("s_ty", [128, 2 * NSEG * D])
            s_tyT = ST("s_tyT", [128, 2 * NSEG * D])
            tyb = [mkbuf(), mkbuf()]
            tyTb = [mkbuf(), mkbuf()]

            def tld(c8):
                xsl = c8 % 2
                t0 = c8 * TC
                kb.dma("sync", f"ty{xsl}",
                       s_ty[:, xsl * NSEG * D:(xsl + 1) * NSEG * D].rearrange(
                           "p (g v) -> p g v", v=D),
                       d_y1.ap()[t0:t0 + TC, :].rearrange("(g p) v -> p g v", p=128),
                       reads=(Bg["dy10"], Bg["dy11"]), writes=(tyb[xsl],))

            tld(0)
            tld(1)
            for c8 in range(NTC):
                xsl = c8 % 2
                yo = xsl * NSEG * D
                t0 = c8 * TC
                for db in range(8):
                    bk = db % 4
                    for g in range(NSEG):
                        kb.op("tensor", lambda t, bk=bk, g=g, db=db, yo=yo:
                              t.transpose(
                                  pt[bk][:, 128 * g:128 * (g + 1)],
                                  s_ty[:, yo + D * g + 128 * db:
                                       yo + D * g + 128 * (db + 1)],
                                  s_idf[:, :]),
                              reads=(tyb[xsl], idfb), writes=(pb[bk],), accum=True)
                    kb.op("scalar", lambda s, bk=bk, db=db, yo=yo: s.copy(
                        s_tyT[:, yo + 512 * db:yo + 512 * (db + 1)],
                        pt[bk][:, 0:512]), reads=(pb[bk],), writes=(tyTb[xsl],))
                kb.dma("sync", f"tyo{xsl}",
                       d_y1T.ap()[:, t0:t0 + TC].rearrange(
                           "(db p) t -> p db t", p=128),
                       s_tyT[:, yo:yo + NSEG * D].rearrange(
                           "p (db t) -> p db t", t=TC),
                       reads=(tyTb[xsl],), writes=(dy1Tb[xsl],))
                if c8 + 2 < NTC:
                    tld(c8 + 2)
        kb.final_wait("sync", ["tyo0", "tyo1"])
        kb.wait("sync", "e_tensor", kb.cnt["e_tensor"])
        kb.wait("sync", "e_scalar", kb.cnt["e_scalar"])

        with contextlib.ExitStack() as esC:
            SC = lambda n, s, dt=F32: esC.enter_context(nc.sbuf_tensor(n, s, dt))
            s_wgq = SC("s_wgq", [128, 8 * II], BF16)
            s_wuq = SC("s_wuq", [128, 8 * II], BF16)
            s_y1C = SC("s_y1C", [128, 2 * 8 * TC])
            s_t1 = SC("s_t1", [128, 2 * 2 * TC])
            s_xqT = SC("s_xqT", [128, 2 * 8 * TC], BF16)
            s_silC = SC("s_silC", [128, 2 * TC])
            s_hC = SC("s_hC", [128, 2 * TC])

            wgqb, wuqb = mkbuf(), mkbuf()
            y1Cb = [mkbuf(), mkbuf()]
            xqTb = [mkbuf(), mkbuf()]
            t1b = [mkbuf(), mkbuf()]
            silCb = [mkbuf(), mkbuf()]
            hCb = [mkbuf(), mkbuf()]

            for k in range(8):
                kb.dma("sync", "wCg", s_wgq[:, II * k:II * (k + 1)].rearrange(
                    "p (it c) -> p it c", c=128),
                    g_wgq.ap()[k].rearrange("it p c -> p it c"),
                    reads=(Bg["gwg"],), writes=(wgqb,))
                kb.dma("sync", "wCu", s_wuq[:, II * k:II * (k + 1)].rearrange(
                    "p (it c) -> p it c", c=128),
                    g_wuq.ap()[k].rearrange("it p c -> p it c"),
                    reads=(Bg["gwu"],), writes=(wuqb,))

            def c_quant_units(c8):
                t0 = c8 * TC
                xsl = c8 % 2
                qoff = xsl * 8 * TC
                units = []

                def u_load():
                    kb.dma("sync", f"yC{xsl}",
                           s_y1C[:, qoff:qoff + 8 * TC].rearrange(
                               "p (k t) -> p k t", t=TC),
                           d_y1T.ap()[:, t0:t0 + TC].rearrange(
                               "(k p) t -> p k t", p=128),
                           reads=tuple(dy1Tb), writes=(y1Cb[xsl],))
                units.append(u_load)
                for u in range(4):
                    def u_q(u=u, xsl=xsl, qoff=qoff):
                        usl = u % 2
                        uo = 2 * TC * usl
                        ysl = s_y1C[:, qoff + 2 * TC * u:qoff + 2 * TC * (u + 1)]
                        kb.op("vector", lambda v, ysl=ysl, uo=uo: v.tensor_scalar(
                            s_t1[:, uo:uo + 2 * TC], ysl,
                            s_col[:, 1:2], 127.0, ALU.mult, ALU.min),
                            reads=(y1Cb[xsl], colb), writes=(t1b[usl],))
                        kb.op("vector", lambda v, uo=uo: v.tensor_scalar(
                            s_t1[:, uo:uo + 2 * TC], s_t1[:, uo:uo + 2 * TC],
                            -128.0, MAGIC,
                            ALU.max, ALU.add), reads=(t1b[usl],), writes=(t1b[usl],))
                        kb.op("scalar", lambda s, u=u, uo=uo, qoff=qoff: s.activation(
                            s_xqT[:, qoff + 2 * TC * u:qoff + 2 * TC * (u + 1)],
                            s_t1[:, uo:uo + 2 * TC],
                            AF.Identity, bias=s_col[:, 3:4]),
                            reads=(t1b[usl], colb), writes=(xqTb[xsl],))
                    units.append(u_q)
                return units

            for u in c_quant_units(0):
                u()
            for c8 in range(NTC):
                t0 = c8 * TC
                xsl = c8 % 2
                qoff = xsl * 8 * TC
                nxt = c_quant_units(c8 + 1) if c8 + 1 < NTC else []
                for j in range(22):
                    bkg = j % 2
                    ssl = j % 2
                    kb.mm_group(pb[bkg], [
                        (lambda t, k=k, bkg=bkg, j=j, qoff=qoff: t.matmul(
                            pt[bkg][:, :],
                            s_wgq[:, II * k + 128 * j: II * k + 128 * (j + 1)],
                            s_xqT[:, qoff + TC * k:qoff + TC * (k + 1)],
                            start=(k == 0), stop=(k == 7)))
                        for k in range(8)], reads=(wgqb, xqTb[xsl]))
                    kb.op("scalar", lambda s, j=j, bkg=bkg, ssl=ssl: s.activation(
                        s_silC[:, TC * ssl:TC * (ssl + 1)], pt[bkg][:, :], AF.Silu,
                        scale=s_gcs[:, j:j + 1]),
                        reads=(pb[bkg], Bg["gcs"]), writes=(silCb[ssl],))
                    bku = 4 + j % 2
                    kb.mm_group(pb[bku], [
                        (lambda t, k=k, bku=bku, j=j, qoff=qoff: t.matmul(
                            pt[bku][:, :],
                            s_wuq[:, II * k + 128 * j: II * k + 128 * (j + 1)],
                            s_xqT[:, qoff + TC * k:qoff + TC * (k + 1)],
                            start=(k == 0), stop=(k == 7)))
                        for k in range(8)], reads=(wuqb, xqTb[xsl]))
                    kb.op("vector", lambda v, j=j, bku=bku, ssl=ssl:
                          v.scalar_tensor_tensor(
                              s_hC[:, TC * ssl:TC * (ssl + 1)], pt[bku][:, :],
                              s_ucs[:, j:j + 1], s_silC[:, TC * ssl:TC * (ssl + 1)],
                              ALU.mult, ALU.mult),
                          reads=(pb[bku], Bg["ucs"], silCb[ssl]), writes=(hCb[ssl],))
                    kb.op("vector", lambda v, ssl=ssl, col=22 * c8 + j: v.reduce_max(
                        s_max2[:, col:col + 1], s_hC[:, TC * ssl:TC * (ssl + 1)],
                        AX.X, apply_absolute_value=True),
                        reads=(hCb[ssl],), writes=(Bg["max2"],))
                    kb.dma("sync", f"sph{ssl}",
                           d_h.ap()[128 * j:128 * (j + 1), t0:t0 + TC],
                           s_hC[:, TC * ssl:TC * (ssl + 1)],
                           reads=(hCb[ssl],), writes=(Bg[f"dh{ssl}"],))
                    if nxt and j % 2 == 1:
                        nxt.pop(0)()
                while nxt:
                    nxt.pop(0)()
            kb.fence()

        max_collective(s_max2, 22 * NTC, cc2i, cc2o, s_sc2, sc2b, Bg["max2"],
                       Bg["cc2i"], Bg["cc2o"], 1, "tnyB", d_sc2v, Bg["dsc2v"])
        with contextlib.ExitStack() as esD:
            SD = lambda n, s, dt=F32: esD.enter_context(nc.sbuf_tensor(n, s, dt))
            s_wdq = SD("s_wdq", [128, 22 * D], BF16)
            s_sdxr = SD("s_sdxr", [1, D])
            s_sdx = SD("s_sdx", [128, D])

            kb.dma("sync", "tnyB", s_col[:, 2:3],
                   bass.AP(d_sc2v, 2, [[0, 128], [1, 1]]),
                   reads=(Bg["dsc2v"],), writes=(colb,))
            kb.dma("sync", "tnyC", s_sdxr[0:1, :], d_sd.ap()[None, :],
                   reads=(Bg["dsd"],), writes=(Bg["sdxr"],))
            kb.op("vector", lambda v: v.tensor_scalar(
                s_sdxr[0:1, :], s_sdxr[0:1, :], s_sc2[0:1, 1:2], None, ALU.mult),
                reads=(Bg["sdxr"], sc2b), writes=(Bg["sdxr"],))
            kb.dma("sync", "tnyC", d_sdxv.ap(), s_sdxr[0:1, :],
                   reads=(Bg["sdxr"],), writes=(Bg["dsdxv"],))
            kb.dma("sync", "tnyC", s_sdx[:, :],
                   bass.AP(d_sdxv, 0, [[0, 128], [1, D]]),
                   reads=(Bg["dsdxv"],), writes=(Bg["sdx"],))


            s_hin = SD("s_hin", [128, 2 * TC])
            s_t1D = SD("s_t1D", [128, 2 * TC])
            s_hq = SD("s_hq", [128, 2 * 22 * TC], BF16)
            s_y1D = SD("s_y1D", [128, 2 * NSEG * D])
            s_f = SD("s_f", [128, 512])
            s_y2 = SD("s_y2", [128, NSEG * D])
            s_lncD = SD("s_lncD", [128, NSEG * 5])
            s_dumpD = SD("s_dumpD", [128, 1024], BF16)

            wdqb = mkbuf()
            hinb = [mkbuf(), mkbuf()]
            t1Db = [mkbuf(), mkbuf()]
            hqb = [[mkbuf(f"hq{s}_{j}") for j in range(22)] for s in range(2)]
            y1Db = [mkbuf(), mkbuf()]
            fb, lncDb = mkbuf(), mkbuf()
            y2b = [mkbuf(f"y2{g}") for g in range(NSEG)]

            for k in range(22):
                kb.dma("sync", "wD", s_wdq[:, D * k:D * (k + 1)].rearrange(
                    "p (j c) -> p j c", c=128),
                    g_wdq.ap()[:, k, :, :].rearrange("j p c -> p j c"),
                    reads=(Bg["gwd"],), writes=(wdqb,))

            def d_quant_units(c8):
                t0 = c8 * TC
                xsl = c8 % 2
                yoff = xsl * NSEG * D
                hoff = xsl * 22 * TC
                units = []
                def uy(xsl=xsl, yoff=yoff, t0=t0):
                    kb.dma("sync", f"yD{xsl}",
                           s_y1D[:, yoff:yoff + NSEG * D].rearrange("p (g v) -> p g v", v=D),
                           d_y1.ap()[t0:t0 + TC, :].rearrange("(g p) v -> p g v", p=128),
                           reads=(Bg["dy10"], Bg["dy11"]), writes=(y1Db[xsl],))
                units.append(uy)
                for j in range(22):
                    def uq(j=j, xsl=xsl, hoff=hoff, t0=t0):
                        sl = j % 2
                        kb.dma("sync", f"hD{sl}", s_hin[:, TC * sl:TC * (sl + 1)],
                               d_h.ap()[128 * j:128 * (j + 1), t0:t0 + TC],
                               reads=(Bg["dh0"], Bg["dh1"]), writes=(hinb[sl],))
                        kb.op("vector", lambda v, sl=sl: v.tensor_scalar(
                            s_t1D[:, TC * sl:TC * (sl + 1)], s_hin[:, TC * sl:TC * (sl + 1)],
                            s_col[:, 2:3], 127.0, ALU.mult, ALU.min),
                            reads=(hinb[sl], colb), writes=(t1Db[sl],))
                        kb.op("vector", lambda v, sl=sl: v.tensor_scalar(
                            s_t1D[:, TC * sl:TC * (sl + 1)], s_t1D[:, TC * sl:TC * (sl + 1)],
                            -128.0, MAGIC, ALU.max, ALU.add),
                            reads=(t1Db[sl],), writes=(t1Db[sl],))
                        kb.op("scalar", lambda s, j=j, sl=sl, hoff=hoff: s.activation(
                            s_hq[:, hoff + TC * j:hoff + TC * (j + 1)],
                            s_t1D[:, TC * sl:TC * (sl + 1)],
                            AF.Identity, bias=s_col[:, 3:4]),
                            reads=(t1Db[sl], colb), writes=(hqb[xsl][j],))
                    units.append(uq)
                return units

            def d_mm_epi(c8, g):
                t0 = c8 * TC
                xsl = c8 % 2
                yoff = xsl * NSEG * D
                hoff = xsl * 22 * TC
                for dc in range(2):
                    bk = 4 + dc
                    kb.mm_group(pb[bk], [
                        (lambda t, j=j, bk=bk, g=g, dc=dc, hoff=hoff: t.matmul(
                            pt[bk][:, :],
                            s_hq[:, hoff + TC * j + CH * g: hoff + TC * j + CH * (g + 1)],
                            s_wdq[:, D * j + 512 * dc: D * j + 512 * (dc + 1)],
                            start=(j == 0), stop=(j == 21)))
                        for j in range(22)], reads=tuple(hqb[xsl]) + (wdqb,))
                    kb.op("vector", lambda v, dc=dc, bk=bk: v.tensor_tensor(
                        s_f[:, :], pt[bk][:, :], s_sdx[:, 512 * dc:512 * (dc + 1)],
                        ALU.mult), reads=(pb[bk], Bg["sdx"]), writes=(fb,))
                    kb.op("vector", lambda v, g=g, dc=dc, yoff=yoff: v.tensor_tensor(
                        s_y2[:, D * g + 512 * dc: D * g + 512 * (dc + 1)],
                        s_f[:, :],
                        s_y1D[:, yoff + D * g + 512 * dc: yoff + D * g + 512 * (dc + 1)],
                        ALU.add), reads=(fb, y1Db[xsl]), writes=(y2b[g],))
                ysl = s_y2[:, D * g:D * (g + 1)]
                c0 = 4 * g
                kb.op("scalar", lambda s, ysl=ysl, c0=c0: s.activation(
                    s_dumpD[:, :], ysl, AF.Identity,
                    accum_out=s_lncD[:, c0:c0 + 1]),
                    reads=(y2b[g],), writes=(dumpbuf(), lncDb))
                kb.op("scalar", lambda s, ysl=ysl, c0=c0: s.activation(
                    s_dumpD[:, :], ysl, AF.Square,
                    accum_out=s_lncD[:, c0 + 1:c0 + 2]),
                    reads=(y2b[g],), writes=(dumpbuf(), lncDb))

            def d_ln_finish(c8):
                t0 = c8 * TC
                mu = s_lncD[:, 0:16:4]
                ss2 = s_lncD[:, 1:16:4]
                var = s_lncD[:, 2:16:4]
                rst = s_lncD[:, 3:16:4]
                kb.op("vector", lambda v: v.tensor_scalar(
                    mu, mu, 1.0 / D, None, ALU.mult), reads=(lncDb,), writes=(lncDb,))
                kb.op("vector", lambda v: v.tensor_tensor(
                    var, mu, mu, ALU.mult), reads=(lncDb,), writes=(lncDb,))
                kb.op("vector", lambda v: v.tensor_scalar(
                    ss2, ss2, 1.0 / D, EPS, ALU.mult, ALU.add),
                    reads=(lncDb,), writes=(lncDb,))
                kb.op("vector", lambda v: v.tensor_tensor(
                    var, ss2, var, ALU.subtract), reads=(lncDb,), writes=(lncDb,))
                kb.op("scalar", lambda s: s.activation(
                    rst, var, AF.Sqrt), reads=(lncDb,), writes=(lncDb,))
                kb.op("vector", lambda v: v.reciprocal(rst, rst),
                      reads=(lncDb,), writes=(lncDb,))
                kb.op("vector", lambda v, mu=mu, rst=rst: v.scalar_tensor_tensor(
                    s_lncD[:, 16:20], mu, -1.0, rst, ALU.mult, ALU.mult),
                    reads=(lncDb,), writes=(lncDb,))
                for g in range(NSEG):
                    ysl = s_y2[:, D * g:D * (g + 1)]
                    c0 = 4 * g
                    kb.op("scalar", lambda s, ysl=ysl, c0=c0, g=g: s.activation(
                        ysl, ysl, AF.Identity, scale=s_lncD[:, c0 + 3:c0 + 4],
                        bias=s_lncD[:, 16 + g:17 + g]),
                        reads=(y2b[g], lncDb), writes=(y2b[g],))
                    kb.dma("sync", "st_out",
                           out[t0 + CH * g:t0 + CH * (g + 1), :], ysl,
                           reads=(y2b[g],), writes=(Bg["dout"],))

            pend = d_quant_units(0)
            for u in pend:
                u()
            for c8 in range(NTC):
                nxt = d_quant_units(c8 + 1) if c8 + 1 < NTC else []
                for _ in range(11):
                    if nxt:
                        nxt.pop(0)()
                for g in range(NSEG):
                    d_mm_epi(c8, g)
                    for _ in range(3):
                        if nxt:
                            nxt.pop(0)()
                d_ln_finish(c8)
                while nxt:
                    nxt.pop(0)()
            kb.final_wait("sync", ["st_out"])
            kb.fence(("sync",))
        kb.finish()
    return nc




def _in_maps(inputs, T=4096, ncore=NCORE):
    import ml_dtypes
    f32 = np.float32
    x = np.asarray(inputs["L0_attn_x"], f32)[:, :T, :]
    g = lambda n: np.ascontiguousarray(np.asarray(inputs[n], f32))
    wq, wk = g("L0_attn_Wq"), g("L0_attn_Wk")
    wv, wgm = g("L0_attn_Wv"), g("L0_attn_Wg")
    wgk1, wgk2 = g("L0_attn_Wgk1"), g("L0_attn_Wgk2")
    bgk2, wo = g("L0_attn_bgk2"), g("L0_attn_Wo")
    rms_w = g("L0_attn_rms_w")
    w_gate, w_up, w_down = g("L0_ffn_w_gate"), g("L0_ffn_w_up"), g("L0_ffn_w_down")
    clip = np.ascontiguousarray(
        np.stack([np.asarray(inputs["L0_ffn_clip_x"], f32).ravel()[0],
                  np.asarray(inputs["L0_ffn_clip_down"], f32).ravel()[0]])[None, :])
    shared = {
        "wq": wq, "wk": wk, "wv": wv, "wg": wgm,
        "wgk1": wgk1, "wgk2": wgk2,
        "bgk": np.ascontiguousarray(bgk2.reshape(8, 128).T),
        "wo": wo,
        "rmsw": np.ascontiguousarray(np.tile(rms_w[None, :], (128, 4))),
        "wgo": w_gate, "wuo": w_up, "wdo": w_down,

        "clip": clip,
        "ident": np.eye(128, dtype=ml_dtypes.bfloat16),
        "mask": np.ascontiguousarray(np.triu(np.ones((128, 128), f32))),
        "ones1": np.ones((1, 128), f32),
    }
    maps = []
    for c in range(min(ncore, x.shape[0])):
        m = dict(shared)
        m["xf"] = np.ascontiguousarray(x[c].T)
        m["xt"] = np.ascontiguousarray(x[c])
        m["wg_sl"] = np.ascontiguousarray(w_gate[:, 128 * c:128 * (c + 1)])
        m["wu_sl"] = np.ascontiguousarray(w_up[:, 128 * c:128 * (c + 1)])
        m["wd_sl"] = np.ascontiguousarray(w_down[128 * c:128 * (c + 1), :])
        maps.append(m)
    return maps


def kernel(**inputs):
    T = 4096
    maps = _in_maps(inputs, T)
    nc = build(T)
    res = run_bass_kernel_spmd(nc, maps, core_ids=list(range(NCORE)))
    out = np.stack([np.asarray(res.results[c]["out"]) for c in range(NCORE)], 0)
    return np.ascontiguousarray(out.astype(np.float32))


# version 23
# speedup vs baseline: 1.0267x; 1.0267x over previous
import contextlib
import math
import numpy as np
import concourse.bass as bass
import concourse.mybir as mybir
from concourse.bass_utils import run_bass_kernel_spmd

F32 = mybir.dt.float32
BF16 = mybir.dt.bfloat16
I32 = mybir.dt.int32
AF = mybir.ActivationFunctionType
ALU = mybir.AluOpType
AX = mybir.AxisListType

ENGS = ("tensor", "vector", "scalar", "gpsimd", "sync")



class Buf:
    __slots__ = ("name", "last_w", "readers")

    def __init__(self, name=""):
        self.name = name
        self.last_w = None
        self.readers = {}


class KB:
    def __init__(self, nc, es):
        self.nc = nc
        self.es = es
        self.sems = {}
        self.cnt = {}
        self.known = {e: {} for e in ENGS}
        self.q = {e: [] for e in ENGS}
        for e in ENGS:
            self.mk_sem("e_" + e)

    def mk_sem(self, name):
        self.sems[name] = self.es.enter_context(self.nc.semaphore(name))
        self.cnt[name] = 0
        return name

    def wait(self, ename, sem_name, val):
        k = self.known[ename]
        if k.get(sem_name, 0) < val:
            sem = self.sems[sem_name]
            self.q[ename].append(lambda eng, s=sem, v=val: eng.wait_ge(s, v))
            k[sem_name] = val

    def _pre(self, ename, reads, writes, exempt_self_waw=False, dma_sem=None):
        need = {}

        def add(item, waw=False):
            if item is None:
                return
            sn, v = item
            if exempt_self_waw and sn == "e_" + ename:
                return
            if waw and dma_sem is not None and sn == dma_sem:
                return
            if "@" in sn:
                v = max(v, self.cnt[sn])
            if need.get(sn, 0) < v:
                need[sn] = v

        for b in reads:
            add(b.last_w)
        for b in writes:
            add(b.last_w, waw=True)
            for sn, v in b.readers.items():
                add((sn, v))
        for sn in sorted(need):
            self.wait(ename, sn, need[sn])

    def _post(self, ename, f, sem_name, inc, reads, writes):
        self.cnt[sem_name] += inc
        v = self.cnt[sem_name]
        sem = self.sems[sem_name]
        self.q[ename].append(lambda eng, s=sem, i=inc: f(eng).then_inc(s, i))
        for b in writes:
            b.last_w = (sem_name, v)
            b.readers = {}
        for b in reads:
            if b.readers.get(sem_name, 0) < v:
                b.readers[sem_name] = v

    def op(self, ename, f, reads=(), writes=(), accum=False):
        self._pre(ename, reads, writes, exempt_self_waw=accum)
        self._post(ename, f, "e_" + ename, 1, reads, writes)

    def mm(self, psum_buf, out_ap, lhsT_ap, rhs_ap, reads=(), start=True, stop=True):
        self.op("tensor",
                lambda t: t.matmul(out_ap, lhsT_ap, rhs_ap, start=start, stop=stop),
                reads=reads, writes=(psum_buf,), accum=True)

    def mm_group(self, psum_buf, emits, reads=()):
        self._pre("tensor", reads, (psum_buf,), exempt_self_waw=True)
        for f in emits[:-1]:
            self.q["tensor"].append(f)
        self._post("tensor", emits[-1], "e_tensor", 1, reads, (psum_buf,))

    def dma(self, ename, stream, out, in_, reads=(), writes=()):
        key = f"{stream}@{ename}"
        if key not in self.sems:
            self.mk_sem(key)
        self._pre(ename, reads, writes, dma_sem=key)
        self._post(ename, lambda eng, o=out, i=in_: eng.dma_start(out=o, in_=i),
                   key, 16, reads, writes)

    def fence(self, engines=ENGS):
        for e in engines:
            for sn in sorted(self.cnt):
                if self.cnt[sn]:
                    self.wait(e, sn, self.cnt[sn])

    def final_wait(self, ename, streams):
        for sn in sorted(self.cnt):
            if sn.split("@")[0] in streams and self.cnt[sn]:
                self.wait(ename, sn, self.cnt[sn])

    def finish(self):
        with self.nc.Block() as block:
            for ename in ENGS:
                queue = self.q[ename]
                if queue:
                    getattr(block, ename)(lambda eng, qq=queue: [c(eng) for c in qq])



D = 1024
HEADS = 4
HDK = 256
HDV = 512
DV = 2048
LR = 16
II = 2816
CH = 128
TC = 512
LN16 = math.log(1.0 / 16.0)
MAGIC = 12582912.0
EPS = 1e-5
NCORE = 8


def build(T=4096, ncore=NCORE):
    NTC = T // TC
    NSEG = TC // CH
    nc = bass.Bass("TRN2", target_bir_lowering=False)

    dr = lambda n, s, dt=F32: nc.dram_tensor(n, s, dt, kind="ExternalInput").ap()
    xf = dr("xf", [D, T])
    xt = dr("xt", [T, D])
    wq = dr("wq", [D, D])
    wk = dr("wk", [D, D])
    wv = dr("wv", [D, DV])
    wg = dr("wg", [D, DV])
    wgk1 = dr("wgk1", [D, LR])
    wgk2 = dr("wgk2", [LR, D])
    bgk = dr("bgk", [128, 8])
    wo = dr("wo", [DV, D])
    rmsw = dr("rmsw", [128, DV])
    wgo = dr("wgo", [II, D])
    wuo = dr("wuo", [II, D])
    wdo = dr("wdo", [D, II])
    wg_sl = dr("wg_sl", [II, 128])
    wu_sl = dr("wu_sl", [II, 128])
    wd_sl = dr("wd_sl", [128, II])
    clip = dr("clip", [1, 2])
    ident = dr("ident", [128, 128], BF16)
    mask = dr("mask", [128, 128])
    ones1 = dr("ones1", [1, 128])

    out = nc.dram_tensor("out", [T, D], F32, kind="ExternalOutput").ap()

    ds = lambda n, s, dt=F32: nc.dram_tensor(n, s, dt)
    d_o = ds("d_o", [T, DV], BF16)
    d_ss = ds("d_ss", [T // 128, 128, HEADS])
    d_y1 = ds("d_y1", [T, D])
    d_y1T = ds("d_y1T", [D, T])
    d_h = ds("d_h", [II, T])
    d_g_sh = ds("d_g_sh", [22, 128, 128], BF16)
    d_u_sh = ds("d_u_sh", [22, 128, 128], BF16)
    d_d_sh = ds("d_d_sh", [22, 128, 128], BF16)
    assert ncore == NCORE, "sharded weight-quant requires 8 cores"
    g_wgq = ds("g_wgq", [8, 22, 128, 128], BF16)
    g_wuq = ds("g_wuq", [8, 22, 128, 128], BF16)
    g_wdq = ds("g_wdq", [8, 22, 128, 128], BF16)
    d_sd = ds("d_sd", [D])
    d_mx = ds("d_mx", [128])
    d_sc1v = ds("d_sc1v", [8])
    d_sc2v = ds("d_sc2v", [8])
    d_sdxv = ds("d_sdxv", [D])
    cc1i = ds("cc1i", [8])
    cc1o = ds("cc1o", [8])
    cc2i = ds("cc2i", [8])
    cc2o = ds("cc2o", [8])

    es = contextlib.ExitStack()
    with es:
        kb = KB(nc, es)
        SB = lambda n, s, dt=F32: es.enter_context(nc.sbuf_tensor(n, s, dt))
        nb = [0]

        def mkbuf(name=None):
            nb[0] += 1
            return Buf(name or f"b{nb[0]}")

        pt = [es.enter_context(nc.psum_tensor(f"p{i}", [128, 512], F32))
              for i in range(8)]
        pb = [mkbuf(f"p{i}") for i in range(8)]

        s_id = SB("s_id", [128, 128], BF16)
        s_idf = SB("s_idf", [128, 128])
        s_mask = SB("s_mask", [128, 128])
        s_ones = SB("s_ones", [1, 128])
        s_clip = SB("s_clip", [1, 2])
        s_sig = SB("s_sig", [1, 2])
        s_sc1 = SB("s_sc1", [1, 8])
        s_sc2 = SB("s_sc2", [1, 8])
        s_cc8 = SB("s_cc8", [1, 8])
        s_mrow = SB("s_mrow", [1, 128])
        s_col = SB("s_col", [128, 8])
        s_gc = SB("s_gc", [128, 24])
        s_uc = SB("s_uc", [128, 24])
        s_dc = SB("s_dc", [128, 8])
        s_gcs = SB("s_gcs", [128, 24])
        s_ucs = SB("s_ucs", [128, 24])
        s_max1 = SB("s_max1", [128, 34])
        s_max2 = SB("s_max2", [128, 178])

        Bg = {n: mkbuf(n) for n in
              ("id mask ones clip sig sc1 sc2 cc8 mrow col gc uc dc gcs ucs "
               "sdxr sdx dump wgw max1 max2 dmx cc1i cc1o cc2i cc2o "
               "dsd dsc1v dsc2v dsdxv gsh ush dsh gwg gwu gwd "
               "do0 do1 dss dy10 dy11 dh0 dh1 dout").split()}
        dump_bufs = [mkbuf("dump0"), mkbuf("dump1")]
        dump_i = [0]

        def dumpbuf():
            dump_i[0] ^= 1
            return dump_bufs[dump_i[0]]

        cc_sem = kb.mk_sem("cc_done")

        kb.dma("sync", "cst", s_id[:, :], ident, writes=(Bg["id"],))
        kb.dma("sync", "cst", s_mask[:, :], mask, writes=(Bg["mask"],))
        kb.dma("sync", "cst", s_ones[0:1, :], ones1, writes=(Bg["ones"],))
        kb.dma("sync", "cst", s_clip[0:1, :], clip, writes=(Bg["clip"],))
        kb.op("scalar", lambda s: s.activation(s_sig[0:1, :], s_clip[0:1, :], AF.Sigmoid),
              reads=(Bg["clip"],), writes=(Bg["sig"],))
        kb.op("vector", lambda v: v.memset(s_col[:, 3:4], -MAGIC), writes=(Bg["col"],))
        idfb = mkbuf("idf")
        kb.op("scalar", lambda s: s.copy(s_idf[:, :], s_id[:, :]),
              reads=(Bg["id"],), writes=(idfb,))
        kb.op("vector", lambda v: v.memset(s_max1[:, :], 0.0), writes=(Bg["max1"],))
        kb.op("vector", lambda v: v.memset(s_max2[:, :], 0.0), writes=(Bg["max2"],))
        kb.op("vector", lambda v: v.memset(s_cc8[0:1, :], 0.0), writes=(Bg["cc8"],))


        esAB = contextlib.ExitStack()
        s_xf = esAB.enter_context(nc.sbuf_tensor(
            "s_xf", [128, 2 * 8 * TC], BF16))
        s_wg = esAB.enter_context(nc.sbuf_tensor(
            "s_wg", [128, 8 * DV], BF16))
        with contextlib.ExitStack() as esA:
            SA = lambda n, s, dt=F32: esA.enter_context(nc.sbuf_tensor(n, s, dt))
            s_wq = SA("s_wq", [128, 8 * D], BF16)
            s_wk = SA("s_wk", [128, 8 * D], BF16)
            s_wv = SA("s_wv", [128, 8 * DV], BF16)
            s_wgk1 = SA("s_wgk1", [128, 8 * LR], BF16)
            s_wgk2 = SA("s_wgk2", [16, D], BF16)
            s_bgk = SA("s_bgk", [128, 8])
            s_z = SA("s_z", [16, TC], BF16)
            s_sg = SA("s_sg", [128, 2 * TC])
            s_sg8 = SA("s_sg8", [128, 8 * TC], BF16)
            s_B = SA("s_B", [128, 8 * TC])
            s_carry = SA("s_carry", [128, 8])
            s_nbq = SA("s_nbq", [128, 8])
            s_pbc = SA("s_pbc", [128, 8])
            s_basep = SA("s_basep", [128, 8])
            s_resc = SA("s_resc", [128, 8])
            s_Ep = SA("s_Ep", [128, TC], BF16)
            s_Em = SA("s_Em", [128, TC], BF16)
            s_q = SA("s_q", [128, 8 * TC], BF16)
            s_kd = SA("s_kd", [128, 8 * TC], BF16)
            s_v = SA("s_v", [128, 2 * DV], BF16)
            s_At = SA("s_At", [128, 2 * 128], BF16)
            s_kg = SA("s_kg", [128, 4 * 128], BF16)
            s_S = SA("s_S", [128, HEADS * 2 * HDV], BF16)
            s_o = SA("s_o", [128, DV], BF16)
            s_ss = SA("s_ss", [128, NSEG * HEADS])
            s_dumpA = SA("s_dumpA", [128, 512], BF16)

            wunits = []
            HWI = 1408
            s_wsl = SA("s_wsl", [128, II])
            s_wq8 = SA("s_wq8", [128, II], BF16)
            s_evt = SA("s_evt", [128, 512], BF16)
            s_rcol = SA("s_rcol", [128, 48])
            wst_b = [mkbuf("wst0"), mkbuf("wst1")]
            wq8_b = mkbuf("wq8")
            evt_b = mkbuf("evt")
            rcol_b = mkbuf("rcol")

            def w_scales(worig, n_tiles, hwid, sc_sbuf, sc_buf, rbase,
                         srow_d=None, srow_buf=None):
                for j in range(n_tiles):
                    def u(j=j):
                        if hwid <= HWI:
                            sl = j % 2
                            off = sl * HWI
                            kb.dma("sync", f"ws{sl}", s_wsl[:, off:off + hwid],
                                   worig[128 * j:128 * (j + 1), :], writes=(wst_b[sl],))
                            kb.op("vector", lambda v, j=j, off=off, sl=sl: v.reduce_max(
                                sc_sbuf[:, j:j + 1], s_wsl[:, off:off + hwid],
                                AX.X, apply_absolute_value=True),
                                reads=(wst_b[sl],), writes=(sc_buf,))
                        else:
                            nh = (hwid + HWI - 1) // HWI
                            for hh in range(nh):
                                w0 = HWI * hh
                                wl = min(hwid, HWI * (hh + 1)) - w0
                                sl = hh % 2
                                off = sl * HWI
                                kb.dma("sync", f"ws{sl}", s_wsl[:, off:off + wl],
                                       worig[128 * j:128 * (j + 1), w0:w0 + wl],
                                       writes=(wst_b[sl],))
                                kb.op("vector", lambda v, wl=wl, hh=hh, off=off,
                                      sl=sl: v.reduce_max(
                                    s_rcol[:, 45 + hh:46 + hh], s_wsl[:, off:off + wl],
                                    AX.X, apply_absolute_value=True),
                                    reads=(wst_b[sl],), writes=(rcol_b,))
                            for hh in range(1, nh):
                                kb.op("vector", lambda v, hh=hh: v.tensor_tensor(
                                    s_rcol[:, 45:46], s_rcol[:, 45:46],
                                    s_rcol[:, 45 + hh:46 + hh], ALU.max),
                                    reads=(rcol_b,), writes=(rcol_b,))
                            kb.op("vector", lambda v, j=j: v.tensor_copy(
                                sc_sbuf[:, j:j + 1], s_rcol[:, 45:46]),
                                reads=(rcol_b,), writes=(sc_buf,))
                    wunits.append(u)

                def u_fin():
                    kb.op("vector", lambda v: v.tensor_scalar(
                        sc_sbuf[:, 0:n_tiles], sc_sbuf[:, 0:n_tiles],
                        1.0 / 127.0, None, ALU.mult),
                        reads=(sc_buf,), writes=(sc_buf,))
                    if rbase is not None:
                        rc = s_rcol[:, rbase:rbase + n_tiles]
                        kb.op("vector", lambda v: v.reciprocal(
                            rc, sc_sbuf[:, 0:n_tiles]), reads=(sc_buf,), writes=(rcol_b,))
                        kb.op("vector", lambda v: v.tensor_tensor(
                            s_wsl[:, 0:n_tiles], sc_sbuf[:, 0:n_tiles], rc, ALU.mult),
                            reads=(sc_buf, rcol_b), writes=(wst_b[0],))
                        kb.op("vector", lambda v: v.tensor_scalar(
                            s_wsl[:, 0:n_tiles], s_wsl[:, 0:n_tiles], 2.0, -1.0,
                            ALU.subtract, ALU.mult), reads=(wst_b[0],), writes=(wst_b[0],))
                        kb.op("vector", lambda v: v.tensor_tensor(
                            rc, rc, s_wsl[:, 0:n_tiles], ALU.mult),
                            reads=(wst_b[0], rcol_b), writes=(rcol_b,))
                    if srow_d is not None:
                        for j in range(n_tiles):
                            kb.dma("sync", "trow", srow_d.ap()[128 * j:128 * (j + 1)],
                                   sc_sbuf[:, j:j + 1], reads=(sc_buf,), writes=(srow_buf,))
                wunits.append(u_fin)

            def w_shard_quant(wsl_dram, mode, rbase, sh_dram, sh_buf, g_dram, g_buf,
                              own_scale=False):
                def u_load():
                    if mode == "gu":
                        kb.dma("sync", "wsl", s_wsl[:, :].rearrange(
                            "p (it h) -> p it h", h=128),
                            wsl_dram.rearrange("(it p) h -> p it h", p=128),
                            writes=(wst_b[0], wst_b[1]))
                    else:
                        kb.dma("sync", "wsl", s_wsl[:, :], wsl_dram,
                               writes=(wst_b[0], wst_b[1]))
                wunits.append(u_load)
                if own_scale:
                    def u_osc():
                        kb.op("vector", lambda v: v.reduce_max(
                            s_rcol[:, 44:45], s_wsl[:, :], AX.X,
                            apply_absolute_value=True),
                            reads=(wst_b[0], wst_b[1]), writes=(rcol_b,))
                        kb.op("vector", lambda v: v.tensor_scalar(
                            s_rcol[:, 44:45], s_rcol[:, 44:45], 1.0 / 127.0, None,
                            ALU.mult), reads=(rcol_b,), writes=(rcol_b,))
                        kb.op("vector", lambda v: v.reciprocal(
                            s_rcol[:, 46:47], s_rcol[:, 44:45]),
                            reads=(rcol_b,), writes=(rcol_b,))
                        kb.op("vector", lambda v: v.tensor_tensor(
                            s_rcol[:, 47:48], s_rcol[:, 44:45], s_rcol[:, 46:47],
                            ALU.mult), reads=(rcol_b,), writes=(rcol_b,))
                        kb.op("vector", lambda v: v.tensor_scalar(
                            s_rcol[:, 47:48], s_rcol[:, 47:48], 2.0, -1.0,
                            ALU.subtract, ALU.mult), reads=(rcol_b,), writes=(rcol_b,))
                        kb.op("vector", lambda v: v.tensor_tensor(
                            s_rcol[:, 46:47], s_rcol[:, 46:47], s_rcol[:, 47:48],
                            ALU.mult), reads=(rcol_b,), writes=(rcol_b,))
                    wunits.append(u_osc)

                def u_quant():
                    if mode == "gu":
                        rv = bass.AP(s_rcol, rbase, [[48, 128], [1, 22], [0, 128]])
                        kb.op("vector", lambda v, rv=rv: v.tensor_tensor(
                            s_wsl[:, :].rearrange("p (it h) -> p it h", h=128),
                            s_wsl[:, :].rearrange("p (it h) -> p it h", h=128),
                            rv, ALU.mult),
                            reads=(wst_b[0], wst_b[1], rcol_b),
                            writes=(wst_b[0], wst_b[1]))
                    else:
                        kb.op("vector", lambda v: v.tensor_scalar(
                            s_wsl[:, :], s_wsl[:, :], s_rcol[:, 46:47], None,
                            ALU.mult), reads=(wst_b[0], wst_b[1], rcol_b),
                            writes=(wst_b[0], wst_b[1]))
                    kb.op("vector", lambda v: v.tensor_scalar(
                        s_wsl[:, :], s_wsl[:, :], 127.0, -128.0,
                        ALU.min, ALU.max),
                        reads=(wst_b[0], wst_b[1]), writes=(wst_b[0], wst_b[1]))
                    kb.op("vector", lambda v: v.tensor_scalar(
                        s_wsl[:, :], s_wsl[:, :], MAGIC, MAGIC,
                        ALU.add, ALU.subtract),
                        reads=(wst_b[0], wst_b[1]), writes=(wst_b[0], wst_b[1]))
                    kb.op("scalar", lambda s: s.copy(s_wq8[:, :], s_wsl[:, :]),
                          reads=(wst_b[0], wst_b[1]), writes=(wq8_b,))
                wunits.append(u_quant)

                for it0 in range(0, 22, 4):
                    def u_tr(it0=it0):
                        nb4 = min(4, 22 - it0)
                        bk = 2 + (it0 // 4) % 2
                        for z in range(nb4):
                            it = it0 + z
                            kb.mm(pb[bk], pt[bk][:, 128 * z:128 * (z + 1)],
                                  s_wq8[:, 128 * it:128 * (it + 1)], s_id[:, :],
                                  reads=(wq8_b, Bg["id"]), start=True, stop=True)
                        kb.op("vector", lambda v, bk=bk, nb4=nb4: v.tensor_copy(
                            s_evt[:, 0:128 * nb4], pt[bk][:, 0:128 * nb4]),
                            reads=(pb[bk],), writes=(evt_b,))
                        kb.dma("sync", "wsp",
                               sh_dram.ap()[it0:it0 + nb4, :, :].rearrange(
                                   "it p c -> p it c"),
                               s_evt[:, 0:128 * nb4].rearrange(
                                   "p (it c) -> p it c", c=128),
                               reads=(evt_b,), writes=(sh_buf,))
                    wunits.append(u_tr)

                def u_ag():
                    kb._pre("gpsimd", (sh_buf,), (g_buf,))
                    kb._post("gpsimd", lambda g2: g2.collective_compute(
                        "AllGather", ALU.bypass,
                        replica_groups=[list(range(ncore))],
                        ins=[sh_dram.ap()], outs=[g_dram.ap()]),
                        "cc_done", 1, (sh_buf,), (g_buf,))
                wunits.append(u_ag)

            w_scales(wgo, 22, D, s_gc, Bg["gc"], 0)
            w_shard_quant(wg_sl, "gu", 0, d_g_sh, Bg["gsh"], g_wgq, Bg["gwg"])
            w_scales(wuo, 22, D, s_uc, Bg["uc"], 22)
            w_shard_quant(wu_sl, "gu", 22, d_u_sh, Bg["ush"], g_wuq, Bg["gwu"])
            w_scales(wdo, 8, II, s_dc, Bg["dc"], None, d_sd, Bg["dsd"])
            w_shard_quant(wd_sl, "dn", 46, d_d_sh, Bg["dsh"], g_wdq, Bg["gwd"],
                          own_scale=True)
            wu_per_iter = (len(wunits) + NTC - 1) // NTC




            wqb, wkb, wvb = mkbuf("wq"), mkbuf("wk"), mkbuf("wv")
            wgk1b, wgk2b, bgkb = mkbuf(), mkbuf(), mkbuf()
            xfb = [mkbuf("xf0"), mkbuf("xf1")]
            zb = mkbuf()
            sgb = [mkbuf(), mkbuf()]
            sg8b = mkbuf("sg8")
            Bb = [mkbuf(f"B{d}") for d in range(8)]
            carryb = mkbuf("carry")
            nbqb, pbcb = mkbuf(), mkbuf()
            basepb, rescb = mkbuf("basep"), mkbuf("resc")
            Epb = [mkbuf(), mkbuf()]
            Emb = [mkbuf(), mkbuf()]
            qb = [mkbuf(f"q{d}") for d in range(8)]
            kdb = [mkbuf(f"kd{d}") for d in range(8)]
            vb = [mkbuf("v0"), mkbuf("v1")]
            Atb = [mkbuf(), mkbuf()]
            kgb = [mkbuf(), mkbuf()]
            Sb = [mkbuf(f"S{h}") for h in range(HEADS)]
            ob = [mkbuf("o0"), mkbuf("o1")]
            ssb = mkbuf("ss")

            kb.dma("gpsimd", "wA", s_wq[:, :].rearrange("p (k t) -> p k t", t=D),
                   wq.rearrange("(k p) t -> p k t", p=128), writes=(wqb,))
            kb.dma("gpsimd", "wA", s_wk[:, :].rearrange("p (k t) -> p k t", t=D),
                   wk.rearrange("(k p) t -> p k t", p=128), writes=(wkb,))
            kb.dma("gpsimd", "wA", s_wv[:, :].rearrange("p (k t) -> p k t", t=DV),
                   wv.rearrange("(k p) t -> p k t", p=128), writes=(wvb,))
            kb.dma("gpsimd", "wA", s_wgk1[:, :].rearrange("p (k t) -> p k t", t=LR),
                   wgk1.rearrange("(k p) t -> p k t", p=128), writes=(wgk1b,))
            kb.dma("gpsimd", "wA", s_wgk2[:, :], wgk2, writes=(wgk2b,))
            kb.dma("gpsimd", "wB", s_wg[:, :].rearrange("p (k t) -> p k t", t=DV),
                   wg.rearrange("(k p) t -> p k t", p=128), writes=(Bg["wgw"],))
            kb.dma("sync", "wA", s_bgk[:, :], bgk, writes=(bgkb,))
            kb.op("vector", lambda v: v.memset(s_S[:, :], 0.0),
                  writes=tuple(Sb))
            kb.op("vector", lambda v: v.memset(s_carry[:, :], 0.0),
                  writes=(carryb,))
            kb.op("vector", lambda v: v.memset(s_basep[:, :], 0.0),
                  writes=(basepb,))

            def xload(c8):
                xsl = c8 % 2
                xoff = xsl * 8 * TC
                t0 = c8 * TC
                kb.dma("gpsimd", f"xf{xsl}",
                       s_xf[:, xoff:xoff + 8 * TC].rearrange("p (k t) -> p k t", t=TC),
                       xf[:, t0:t0 + TC].rearrange("(k p) t -> p k t", p=128),
                       writes=(xfb[xsl],))

            xload(0)
            for c8 in range(NTC):
                xsl = c8 % 2
                xoff = xsl * 8 * TC
                t0 = c8 * TC
                xload(c8 + 1 if c8 + 1 < NTC else 0)

                kb.mm_group(pb[2], [
                    (lambda t, k=k, xoff=xoff: t.matmul(
                        pt[2][0:16, 0:TC], s_wgk1[:, LR * k:LR * (k + 1)],
                        s_xf[:, xoff + TC * k: xoff + TC * (k + 1)],
                        start=(k == 0), stop=(k == 7)))
                    for k in range(8)], reads=(wgk1b, xfb[xsl]))
                kb.op("vector", lambda v: v.tensor_copy(s_z[:, :], pt[2][0:16, 0:TC]),
                      reads=(pb[2],), writes=(zb,))

                for dt in range(8):
                    bk = 2 + (dt % 2)
                    kb.mm(pb[bk], pt[bk][:, 0:TC], s_wgk2[:, 128 * dt:128 * (dt + 1)],
                          s_z[:, :], reads=(wgk2b, zb))
                    kb.op("scalar", lambda s, bk=bk, dt=dt: s.activation(
                        s_sg8[:, TC * dt:TC * (dt + 1)], pt[bk][:, 0:TC], AF.Sigmoid,
                        bias=s_bgk[:, dt:dt + 1]),
                        reads=(pb[bk], bgkb), writes=(sg8b,))
                kb.op("vector", lambda v: v.tensor_tensor(
                    s_resc[:, :], s_carry[:, :], s_basep[:, :], ALU.subtract),
                    reads=(carryb, basepb), writes=(rescb,))
                kb.op("scalar", lambda s: s.activation(
                    s_resc[:, :], s_resc[:, :], AF.Exp, scale=1.0 / 16.0),
                    reads=(rescb,), writes=(rescb,))
                kb.op("vector", lambda v: v.tensor_copy(s_basep[:, :], s_carry[:, :]),
                      reads=(carryb,), writes=(basepb,))
                kb.op("vector", lambda v: v.tensor_scalar(
                    s_nbq[:, :], s_basep[:, :], -1.0 / 16.0, LN16,
                    ALU.mult, ALU.add), reads=(basepb,), writes=(nbqb,))
                kb.op("vector", lambda v: v.tensor_scalar(
                    s_pbc[:, :], s_basep[:, :], 1.0 / 16.0, None,
                    ALU.mult), reads=(basepb,), writes=(pbcb,))
                if c8 > 0:
                    for h in range(HEADS):
                        for i in range(2):
                            kb.op("vector", lambda v, h=h, i=i: v.tensor_scalar(
                                s_S[:, 2 * HDV * h + HDV * i: 2 * HDV * h + HDV * (i + 1)],
                                s_S[:, 2 * HDV * h + HDV * i: 2 * HDV * h + HDV * (i + 1)],
                                s_resc[:, 2 * h + i:2 * h + i + 1], None, ALU.mult),
                                reads=(rescb, Sb[h]), writes=(Sb[h],))
                for dt in range(8):
                    ssl = dt % 2
                    kb.op("scalar", lambda s, dt=dt, ssl=ssl: s.activation(
                        s_sg[:, TC * ssl:TC * (ssl + 1)],
                        s_sg8[:, TC * dt:TC * (dt + 1)], AF.Ln),
                        reads=(sg8b,), writes=(sgb[ssl],))
                    kb.op("vector", lambda v, dt=dt, ssl=ssl: v.tensor_tensor_scan(
                        s_B[:, TC * dt:TC * (dt + 1)],
                        s_sg[:, TC * ssl:TC * (ssl + 1)],
                        s_sg[:, TC * ssl:TC * (ssl + 1)],
                        s_carry[:, dt:dt + 1], ALU.add, ALU.bypass),
                        reads=(sgb[ssl], carryb), writes=(Bb[dt],))
                kb.op("vector", lambda v: v.tensor_copy(
                    s_carry[:, 0:8], s_B[:, TC - 1:8 * TC:TC]),
                    reads=tuple(Bb), writes=(carryb,))

                for dt in range(8):
                    bk = (0, 1, 4, 5)[dt % 4]
                    esl = 0
                    eo = 0
                    kb.op("scalar", lambda s, dt=dt, eo=eo: s.activation(
                        s_Ep[:, eo:eo + TC], s_B[:, TC * dt:TC * (dt + 1)],
                        AF.Exp, bias=s_nbq[:, dt:dt + 1], scale=1.0 / 16.0),
                        reads=(Bb[dt], nbqb), writes=(Epb[esl],))
                    kb.op("scalar", lambda s, dt=dt, eo=eo: s.activation(
                        s_Em[:, eo:eo + TC], s_B[:, TC * dt:TC * (dt + 1)],
                        AF.Exp, bias=s_pbc[:, dt:dt + 1], scale=-1.0 / 16.0),
                        reads=(Bb[dt], pbcb), writes=(Emb[esl],))
                    kb.mm_group(pb[bk], [
                        (lambda t, k=k, bk=bk, dt=dt, xoff=xoff: t.matmul(
                            pt[bk][:, 0:TC],
                            s_wq[:, D * k + 128 * dt: D * k + 128 * (dt + 1)],
                            s_xf[:, xoff + TC * k: xoff + TC * (k + 1)],
                            start=(k == 0), stop=(k == 7)))
                        for k in range(8)], reads=(wqb, xfb[xsl]))
                    kb.op("vector", lambda v, dt=dt, bk=bk, eo=eo:
                          v.scalar_tensor_tensor(
                        s_q[:, TC * dt:TC * (dt + 1)], pt[bk][:, 0:TC], 1.0,
                        s_Ep[:, eo:eo + TC], ALU.bypass, ALU.mult),
                        reads=(pb[bk], Epb[esl]), writes=(qb[dt],))
                    bk2 = (2, 3, 6, 7)[dt % 4]
                    kb.mm_group(pb[bk2], [
                        (lambda t, k=k, bk2=bk2, dt=dt, xoff=xoff: t.matmul(
                            pt[bk2][:, 0:TC],
                            s_wk[:, D * k + 128 * dt: D * k + 128 * (dt + 1)],
                            s_xf[:, xoff + TC * k: xoff + TC * (k + 1)],
                            start=(k == 0), stop=(k == 7)))
                        for k in range(8)], reads=(wkb, xfb[xsl]))
                    kb.op("vector", lambda v, dt=dt, bk2=bk2, eo=eo:
                          v.scalar_tensor_tensor(
                        s_kd[:, TC * dt:TC * (dt + 1)], pt[bk2][:, 0:TC], 1.0,
                        s_Em[:, eo:eo + TC], ALU.bypass, ALU.mult),
                        reads=(pb[bk2], Emb[esl]), writes=(kdb[dt],))

                def v_proj(g):
                    vsl = g % 2
                    for nuc in range(4):
                        bk = 2 + nuc % 2
                        kb.mm_group(pb[bk], [
                            (lambda t, k=k, bk=bk, g=g, nuc=nuc, xoff=xoff: t.matmul(
                                pt[bk][:, :],
                                s_xf[:, xoff + TC * k + CH * g: xoff + TC * k + CH * (g + 1)],
                                s_wv[:, DV * k + 512 * nuc: DV * k + 512 * (nuc + 1)],
                                start=(k == 0), stop=(k == 7)))
                            for k in range(8)], reads=(xfb[xsl], wvb))
                        kb.op("scalar", lambda s, vsl=vsl, nuc=nuc, bk=bk: s.copy(
                            s_v[:, DV * vsl + 512 * nuc: DV * vsl + 512 * (nuc + 1)],
                            pt[bk][:, :]), reads=(pb[bk],), writes=(vb[vsl],))

                def gla_stage1(it):
                    g, h = divmod(it, HEADS)
                    par = it % 2
                    bA = 4 if par == 0 else 6
                    d0, d1 = 2 * h, 2 * h + 1
                    kb.mm_group(pb[bA], [
                        (lambda t, i=i, dt=dt, g=g, bA=bA: t.matmul(
                            pt[bA][:, 0:128],
                            s_kd[:, TC * dt + CH * g: TC * dt + CH * (g + 1)],
                            s_q[:, TC * dt + CH * g: TC * dt + CH * (g + 1)],
                            start=(i == 0), stop=(i == 1)))
                        for i, dt in enumerate((d0, d1))],
                        reads=(kdb[d0], kdb[d1], qb[d0], qb[d1]))
                    for i, dt in enumerate((d0, d1)):
                        kb.mm_group(pb[bA], [
                            (lambda t, i=i, dt=dt, g=g, bA=bA: t.matmul(
                                pt[bA][:, 128 * (i + 1):128 * (i + 2)],
                                s_kd[:, TC * dt + CH * g: TC * dt + CH * (g + 1)],
                                s_id[:, :],
                                start=True, stop=True))],
                            reads=(kdb[dt], Bg["id"]))
                    kb.op("vector", lambda v, bA=bA, par=par: v.tensor_tensor(
                        s_At[:, 128 * par:128 * (par + 1)], pt[bA][:, 0:128],
                        s_mask[:, :], ALU.mult),
                        reads=(pb[bA], Bg["mask"]), writes=(Atb[par],))
                    kb.op("vector", lambda v, bA=bA, par=par: v.tensor_copy(
                        s_kg[:, 256 * par:256 * (par + 1)], pt[bA][:, 128:384]),
                        reads=(pb[bA],), writes=(kgb[par],))

                def gla_stage2(it):
                    g, h = divmod(it, HEADS)
                    par = it % 2
                    bO = 5 if par == 0 else 0
                    bP = (7, 3) if par == 0 else (1, 2)
                    osl = g % 2
                    d0, d1 = 2 * h, 2 * h + 1
                    vsl = g % 2
                    emits = [
                        (lambda t, vsl=vsl, h=h, par=par, bO=bO: t.matmul(
                            pt[bO][:, :], s_At[:, 128 * par:128 * (par + 1)],
                            s_v[:, DV * vsl + HDV * h: DV * vsl + HDV * (h + 1)],
                            start=True, stop=False))]
                    for i, dt in enumerate((d0, d1)):
                        emits.append(
                            lambda t, i=i, dt=dt, g=g, h=h, bO=bO: t.matmul(
                                pt[bO][:, :],
                                s_q[:, TC * dt + CH * g: TC * dt + CH * (g + 1)],
                                s_S[:, 2 * HDV * h + HDV * i: 2 * HDV * h + HDV * (i + 1)],
                                start=False, stop=(i == 1)))
                    kb.mm_group(pb[bO], emits,
                                reads=(Atb[par], vb[vsl], qb[d0], qb[d1], Sb[h]))
                    for i, dt in enumerate((d0, d1)):
                        kb.mm_group(pb[bP[i]], [
                            (lambda t, i=i, vsl=vsl, h=h, par=par, bk=bP[i]: t.matmul(
                                pt[bk][:, :], s_kg[:, 256 * par + 128 * i:256 * par + 128 * (i + 1)],
                                s_v[:, DV * vsl + HDV * h: DV * vsl + HDV * (h + 1)],
                                start=True, stop=True))],
                            reads=(kgb[par], vb[vsl]))
                    kb.op("scalar", lambda s, g=g, h=h, bO=bO: s.activation(
                        s_dumpA[:, 0:512], pt[bO][:, :], AF.Square,
                        accum_out=s_ss[:, HEADS * g + h:HEADS * g + h + 1]),
                        reads=(pb[bO],), writes=(ssb, dumpbuf()))
                    kb.op("scalar", lambda s, h=h, bO=bO: s.copy(
                        s_o[:, HDV * h: HDV * (h + 1)],
                        pt[bO][:, :]), reads=(pb[bO],), writes=(ob[0],))
                    for i in range(2):
                        kb.op("vector", lambda v, h=h, i=i, bk=bP[i]:
                              v.tensor_tensor(
                                  s_S[:, 2 * HDV * h + HDV * i: 2 * HDV * h + HDV * (i + 1)],
                                  s_S[:, 2 * HDV * h + HDV * i: 2 * HDV * h + HDV * (i + 1)],
                                  pt[bk][:, :], ALU.add),
                              reads=(pb[bP[i]], Sb[h]), writes=(Sb[h],))

                def o_spill(g):
                    osl = g % 2
                    kb.dma("sync", f"sp_o{osl}",
                           d_o.ap()[t0 + CH * g: t0 + CH * (g + 1), :],
                           s_o[:, :], reads=(ob[0],),
                           writes=(Bg[f"do{osl}"],))

                NIT = NSEG * HEADS
                for it in range(NIT):
                    if it % HEADS == 0:
                        v_proj(it // HEADS)
                    gla_stage1(it)
                    if wunits:
                        wunits.pop(0)()
                    if it > 0:
                        gla_stage2(it - 1)
                        if (it - 1) % HEADS == HEADS - 1:
                            o_spill((it - 1) // HEADS)
                gla_stage2(NIT - 1)
                o_spill(NSEG - 1)
                kb.dma("sync", "sp_ss",
                       d_ss.ap()[NSEG * c8:NSEG * (c8 + 1), :, :].rearrange(
                           "g p h -> p g h"),
                       s_ss[:, :], reads=(ssb,), writes=(Bg["dss"],))

            while wunits:
                wunits.pop(0)()
            kb.fence()

        with contextlib.ExitStack() as esB:
            SBp = lambda n, s, dt=F32: esB.enter_context(nc.sbuf_tensor(n, s, dt))
            s_rw = SBp("s_rw", [128, DV], BF16)
            s_oi = SBp("s_oi", [128, 2 * NSEG * DV], BF16)
            s_ssi = SBp("s_ssi", [128, 2 * NSEG * HEADS])
            s_wo = SBp("s_wo", [128, 16 * D], BF16)
            s_rst = SBp("s_rst", [128, NSEG * HEADS])
            s_sil = SBp("s_sil", [128, 4 * 512], BF16)
            s_gg = SBp("s_gg", [128, NSEG * DV], BF16)
            s_oop2 = SBp("s_oop2", [128, 2 * DV], BF16)
            s_oopT = SBp("s_oopT", [128, 2 * 16 * 128], BF16)
            s_xt = SBp("s_xt", [128, NSEG * D])
            s_y1 = SBp("s_y1", [128, 2 * NSEG * D])
            s_dumpB = SBp("s_dumpB", [128, 1024], BF16)
            s_lnc = SBp("s_lnc", [128, 2 * NSEG * 5])

            wob, rwb = mkbuf(), mkbuf()
            oib = [mkbuf(), mkbuf()]
            ssib = [mkbuf(), mkbuf()]
            rstb = mkbuf()
            silb = [mkbuf(), mkbuf(), mkbuf(), mkbuf()]
            ggb = [mkbuf(f"gg{g}") for g in range(NSEG)]
            oopb = [mkbuf(), mkbuf()]
            oopTb = [mkbuf(), mkbuf()]
            xtb = mkbuf("xt")
            y1b = [[mkbuf(f"y1{s}{g}") for g in range(NSEG)] for s in range(2)]
            lncb = mkbuf()

            kb.dma("gpsimd", "wB3",
                   s_wo[:, :].rearrange("p (k t) -> p k t", t=D),
                   wo.rearrange("(k p) t -> p k t", p=128), writes=(wob,))
            kb.dma("gpsimd", "wB2", s_rw[:, :], rmsw, writes=(rwb,))



            def bload(c8):
                t0 = c8 * TC
                xsl = c8 % 2
                kb.dma("sync", f"oB{xsl}",
                       s_oi[:, xsl * NSEG * DV:(xsl + 1) * NSEG * DV].rearrange(
                           "p (g v) -> p g v", v=DV),
                       d_o.ap()[t0:t0 + TC, :].rearrange("(g p) v -> p g v", p=128),
                       reads=(Bg["do0"], Bg["do1"]), writes=(oib[xsl],))
                nsh = NSEG * HEADS
                kb.dma("sync", f"oB{xsl}", s_ssi[:, xsl * nsh:(xsl + 1) * nsh],
                       d_ss.ap()[NSEG * c8:NSEG * (c8 + 1), :, :].rearrange(
                           "g p h -> p g h"),
                       reads=(Bg["dss"],), writes=(ssib[xsl],))

            bload(0)
            for c8 in range(NTC):
                t0 = c8 * TC
                xsl = c8 % 2
                xoffB = xsl * 8 * TC
                ooffB = xsl * NSEG * DV
                if c8 + 1 < NTC:
                    bload(c8 + 1)
                    xload(c8 + 1)
                kb.dma("sync", f"xtB{xsl}",
                       s_xt[:, :].rearrange("p (g v) -> p g v", v=D),
                       xt[t0:t0 + TC, :].rearrange("(g p) v -> p g v", p=128),
                       writes=(xtb,))
                nsh = NSEG * HEADS
                kb.op("vector", lambda v, xsl=xsl, nsh=nsh: v.tensor_scalar(
                    s_rst[:, :], s_ssi[:, xsl * nsh:(xsl + 1) * nsh],
                    1.0 / HDV, EPS, ALU.mult, ALU.add),
                    reads=(ssib[xsl],), writes=(rstb,))
                kb.op("scalar", lambda s: s.activation(
                    s_rst[:, :], s_rst[:, :], AF.Sqrt), reads=(rstb,), writes=(rstb,))
                kb.op("vector", lambda v: v.reciprocal(s_rst[:, :], s_rst[:, :]),
                      reads=(rstb,), writes=(rstb,))
                for g in range(NSEG):
                    for nuc in range(4):
                        bk = nuc % 2
                        ssl = nuc
                        kb.mm_group(pb[bk], [
                            (lambda t, k=k, bk=bk, g=g, nuc=nuc, xoffB=xoffB: t.matmul(
                                pt[bk][:, :],
                                s_xf[:, xoffB + TC * k + CH * g: xoffB + TC * k + CH * (g + 1)],
                                s_wg[:, DV * k + 512 * nuc: DV * k + 512 * (nuc + 1)],
                                start=(k == 0), stop=(k == 7)))
                            for k in range(8)], reads=(xfb[xsl], Bg["wgw"]))
                        kb.op("scalar", lambda s, bk=bk, ssl=ssl: s.activation(
                            s_sil[:, 512 * ssl:512 * (ssl + 1)], pt[bk][:, :], AF.Silu),
                            reads=(pb[bk],), writes=(silb[ssl],))
                        kb.op("vector", lambda v, g=g, nuc=nuc, ssl=ssl: v.tensor_tensor(
                            s_gg[:, DV * g + 512 * nuc:DV * g + 512 * (nuc + 1)],
                            s_sil[:, 512 * ssl:512 * (ssl + 1)],
                            s_rw[:, 512 * nuc:512 * (nuc + 1)], ALU.mult),
                            reads=(silb[ssl], rwb), writes=(ggb[g],))
                def b_oop(g):
                    for h in range(HEADS):
                        col = HEADS * g + h
                        kb.op("vector", lambda v, g=g, h=h, col=col,
                              ooffB=ooffB, xsl=xsl:
                              v.scalar_tensor_tensor(
                                  s_oop2[:, DV * (g % 2) + HDV * h: DV * (g % 2) + HDV * (h + 1)],
                                  s_oi[:, ooffB + DV * g + HDV * h: ooffB + DV * g + HDV * (h + 1)],
                                  s_rst[:, col:col + 1],
                                  s_gg[:, DV * g + HDV * h:DV * g + HDV * (h + 1)],
                                  ALU.mult, ALU.mult),
                              reads=(oib[xsl], rstb, ggb[g]), writes=(oopb[g % 2],))

                def b_tr(g):
                    osl = g % 2
                    for jb in range(4):
                        bk = 2 + jb % 2
                        for j4 in range(4):
                            j = 4 * jb + j4
                            kb.mm(pb[bk], pt[bk][:, 128 * j4:128 * (j4 + 1)],
                                  s_oop2[:, DV * osl + 128 * j: DV * osl + 128 * (j + 1)],
                                  s_id[:, :],
                                  reads=(oopb[osl], Bg["id"]), start=True, stop=True)
                        kb.op("vector", lambda v, jb=jb, bk=bk, osl=osl: v.tensor_copy(
                            s_oopT[:, 2048 * osl + 512 * jb:2048 * osl + 512 * (jb + 1)],
                            pt[bk][:, :]),
                            reads=(pb[bk],), writes=(oopTb[osl],))

                def b_out(g):
                    osl = g % 2
                    for dc in range(2):
                        bk = 4 + dc
                        kb.mm_group(pb[bk], [
                            (lambda t, k=k, bk=bk, dc=dc, osl=osl: t.matmul(
                                pt[bk][:, :],
                                s_oopT[:, 2048 * osl + 128 * k:2048 * osl + 128 * (k + 1)],
                                s_wo[:, D * k + 512 * dc: D * k + 512 * (dc + 1)],
                                start=(k == 0), stop=(k == 15)))
                            for k in range(16)], reads=(oopTb[osl], wob))
                        kb.op("vector", lambda v, g=g, dc=dc, bk=bk, xsl=xsl:
                              v.tensor_tensor(
                            s_y1[:, xsl * NSEG * D + D * g + 512 * dc:
                                 xsl * NSEG * D + D * g + 512 * (dc + 1)],
                            pt[bk][:, :],
                            s_xt[:, D * g + 512 * dc: D * g + 512 * (dc + 1)],
                            ALU.add), reads=(pb[bk], xtb), writes=(y1b[xsl][g],))
                    ysl = s_y1[:, xsl * NSEG * D + D * g:xsl * NSEG * D + D * (g + 1)]
                    c0 = 20 * xsl + 4 * g
                    kb.op("scalar", lambda s, ysl=ysl, c0=c0: s.activation(
                        s_dumpB[:, :], ysl, AF.Identity,
                        accum_out=s_lnc[:, c0:c0 + 1]),
                        reads=(y1b[xsl][g],), writes=(dumpbuf(), lncb))
                    kb.op("scalar", lambda s, ysl=ysl, c0=c0: s.activation(
                        s_dumpB[:, :], ysl, AF.Square,
                        accum_out=s_lnc[:, c0 + 1:c0 + 2]),
                        reads=(y1b[xsl][g],), writes=(dumpbuf(), lncb))

                def b_ln_a(c8):
                    t0 = c8 * TC
                    xsl = c8 % 2
                    yo = xsl * NSEG * D
                    lb = 20 * xsl
                    mu = s_lnc[:, lb + 0:lb + 16:4]
                    ss2 = s_lnc[:, lb + 1:lb + 16:4]
                    var = s_lnc[:, lb + 2:lb + 16:4]
                    rst = s_lnc[:, lb + 3:lb + 16:4]
                    kb.op("vector", lambda v: v.tensor_scalar(
                        mu, mu, 1.0 / D, None, ALU.mult), reads=(lncb,), writes=(lncb,))
                    kb.op("vector", lambda v: v.tensor_tensor(
                        var, mu, mu, ALU.mult), reads=(lncb,), writes=(lncb,))
                    kb.op("vector", lambda v: v.tensor_scalar(
                        ss2, ss2, 1.0 / D, EPS, ALU.mult, ALU.add),
                        reads=(lncb,), writes=(lncb,))
                    kb.op("vector", lambda v: v.tensor_tensor(
                        var, ss2, var, ALU.subtract), reads=(lncb,), writes=(lncb,))
                    kb.op("scalar", lambda s: s.activation(
                        rst, var, AF.Sqrt), reads=(lncb,), writes=(lncb,))
                    kb.op("vector", lambda v: v.reciprocal(rst, rst),
                          reads=(lncb,), writes=(lncb,))
                    kb.op("vector", lambda v, mu=mu, rst=rst, lb=lb:
                          v.scalar_tensor_tensor(
                        s_lnc[:, lb + 16:lb + 20], mu, -1.0, rst, ALU.mult, ALU.mult),
                        reads=(lncb,), writes=(lncb,))
                    for g in range(NSEG):
                        ysl = s_y1[:, yo + D * g:yo + D * (g + 1)]
                        c0 = lb + 4 * g
                        kb.op("scalar", lambda s, ysl=ysl, c0=c0, g=g, lb=lb, xsl=xsl:
                              s.activation(
                            ysl, ysl, AF.Identity, scale=s_lnc[:, c0 + 3:c0 + 4],
                            bias=s_lnc[:, lb + 16 + g:lb + 17 + g]),
                            reads=(y1b[xsl][g], lncb), writes=(y1b[xsl][g],))
                        kb.dma("sync", f"spy{xsl}",
                               d_y1.ap()[t0 + CH * g:t0 + CH * (g + 1), :],
                               ysl, reads=(y1b[xsl][g],), writes=(Bg[f"dy1{xsl}"],))

                def b_ln_b(c8):
                    xsl = c8 % 2
                    yo = xsl * NSEG * D
                    for g in range(NSEG):
                        ysl = s_y1[:, yo + D * g:yo + D * (g + 1)]
                        kb.op("vector", lambda v, ysl=ysl, col=NSEG * c8 + g, g=g,
                              xsl=xsl: v.reduce_max(
                            s_max1[:, col:col + 1], ysl, AX.X, apply_absolute_value=True),
                            reads=(y1b[xsl][g],), writes=(Bg["max1"],))

                b_oop(0)
                b_tr(0)
                b_oop(1)
                b_tr(1)
                b_out(0)
                if c8 > 0:
                    b_ln_a(c8 - 1)
                b_oop(2)
                b_tr(2)
                b_out(1)
                b_oop(3)
                if c8 > 0:
                    b_ln_b(c8 - 1)
                b_tr(3)
                b_out(2)
                b_out(3)
                if c8 == NTC - 1:
                    b_ln_a(c8)
                    b_ln_b(c8)
            kb.fence()

        esAB.close()
        def max_collective(s_maxt, ncols, cci, cco, s_scn, scn_buf, maxt_buf,
                           cc_in_b, cc_out_b, sig_col, tny, dscv, dscv_b):
            kb.op("vector", lambda v: v.reduce_max(
                s_maxt[:, ncols:ncols + 1], s_maxt[:, 0:ncols], AX.X),
                reads=(maxt_buf,), writes=(maxt_buf,))
            kb.dma("gpsimd", tny + "g", d_mx.ap(), s_maxt[:, ncols:ncols + 1],
                   reads=(maxt_buf,), writes=(Bg["dmx"],))
            kb.dma("gpsimd", tny + "g", s_mrow[0:1, :], d_mx.ap()[None, :],
                   reads=(Bg["dmx"],), writes=(Bg["mrow"],))
            kb.op("vector", lambda v: v.reduce_max(
                s_cc8[0:1, 0:1], s_mrow[0:1, :], AX.X),
                reads=(Bg["mrow"],), writes=(Bg["cc8"],))
            kb.dma("gpsimd", tny + "g", cci.ap(), s_cc8[0:1, :],
                   reads=(Bg["cc8"],), writes=(cc_in_b,))
            kb._pre("gpsimd", (cc_in_b,), (cc_out_b,))
            kb._post("gpsimd", lambda g: g.collective_compute(
                "AllReduce", ALU.max, replica_groups=[list(range(ncore))],
                ins=[cci.ap()], outs=[cco.ap()]),
                "cc_done", 1, (cc_in_b,), (cc_out_b,))
            kb.dma("gpsimd", tny + "g", s_scn[0:1, 0:1], cco.ap()[None, 0:1],
                   reads=(cc_out_b,), writes=(scn_buf,))
            kb.op("vector", lambda v: v.tensor_tensor(
                s_scn[0:1, 1:2], s_scn[0:1, 0:1], s_sig[0:1, sig_col:sig_col + 1],
                ALU.mult), reads=(scn_buf, Bg["sig"]), writes=(scn_buf,))
            kb.op("vector", lambda v: v.tensor_scalar(
                s_scn[0:1, 1:2], s_scn[0:1, 1:2], 1.0 / 127.0, None, ALU.mult),
                reads=(scn_buf,), writes=(scn_buf,))
            kb.op("vector", lambda v: v.reciprocal(
                s_scn[0:1, 2:3], s_scn[0:1, 1:2]), reads=(scn_buf,), writes=(scn_buf,))
            kb.op("vector", lambda v: v.tensor_tensor(
                s_scn[0:1, 3:4], s_scn[0:1, 1:2], s_scn[0:1, 2:3], ALU.mult),
                reads=(scn_buf,), writes=(scn_buf,))
            kb.op("vector", lambda v: v.tensor_scalar(
                s_scn[0:1, 3:4], s_scn[0:1, 3:4], 2.0, -1.0,
                ALU.subtract, ALU.mult), reads=(scn_buf,), writes=(scn_buf,))
            kb.op("vector", lambda v: v.tensor_tensor(
                s_scn[0:1, 2:3], s_scn[0:1, 2:3], s_scn[0:1, 3:4], ALU.mult),
                reads=(scn_buf,), writes=(scn_buf,))
            kb.dma("gpsimd", tny + "g", dscv.ap()[0:4], s_scn[0:1, 0:4],
                   reads=(scn_buf,), writes=(dscv_b,))

        sc1b, sc2b = Bg["sc1"], Bg["sc2"]
        max_collective(s_max1, NSEG * NTC, cc1i, cc1o, s_sc1, sc1b, Bg["max1"],
                       Bg["cc1i"], Bg["cc1o"], 0, "tnyA", d_sc1v, Bg["dsc1v"])
        colb = Bg["col"]
        kb.dma("gpsimd", "tnyAg", s_col[:, 0:1],
               bass.AP(d_sc1v, 1, [[0, 128], [1, 1]]),
               reads=(Bg["dsc1v"],), writes=(colb,))
        kb.dma("gpsimd", "tnyAg", s_col[:, 1:2],
               bass.AP(d_sc1v, 2, [[0, 128], [1, 1]]),
               reads=(Bg["dsc1v"],), writes=(colb,))
        kb.op("vector", lambda v: v.tensor_scalar(
            s_gcs[:, 0:22], s_gc[:, 0:22], s_col[:, 0:1], None, ALU.mult),
            reads=(Bg["gc"], colb), writes=(Bg["gcs"],))
        kb.op("vector", lambda v: v.tensor_scalar(
            s_ucs[:, 0:22], s_uc[:, 0:22], s_col[:, 0:1], None, ALU.mult),
            reads=(Bg["uc"], colb), writes=(Bg["ucs"],))

        dy1Tb = [mkbuf("dy1T0"), mkbuf("dy1T1")]
        with contextlib.ExitStack() as esT:
            ST = lambda n, s, dt=F32: esT.enter_context(nc.sbuf_tensor(n, s, dt))
            s_ty = ST("s_ty", [128, 2 * NSEG * D])
            s_tyT = ST("s_tyT", [128, 2 * NSEG * D])
            tyb = [mkbuf(), mkbuf()]
            tyTb = [mkbuf(), mkbuf()]

            def tld(c8):
                xsl = c8 % 2
                t0 = c8 * TC
                kb.dma("sync", f"ty{xsl}",
                       s_ty[:, xsl * NSEG * D:(xsl + 1) * NSEG * D].rearrange(
                           "p (g v) -> p g v", v=D),
                       d_y1.ap()[t0:t0 + TC, :].rearrange("(g p) v -> p g v", p=128),
                       reads=(Bg["dy10"], Bg["dy11"]), writes=(tyb[xsl],))

            tld(0)
            tld(1)
            for c8 in range(NTC):
                xsl = c8 % 2
                yo = xsl * NSEG * D
                t0 = c8 * TC
                for db in range(8):
                    bk = db % 4
                    for g in range(NSEG):
                        kb.op("tensor", lambda t, bk=bk, g=g, db=db, yo=yo:
                              t.transpose(
                                  pt[bk][:, 128 * g:128 * (g + 1)],
                                  s_ty[:, yo + D * g + 128 * db:
                                       yo + D * g + 128 * (db + 1)],
                                  s_idf[:, :]),
                              reads=(tyb[xsl], idfb), writes=(pb[bk],), accum=True)
                    kb.op("scalar", lambda s, bk=bk, db=db, yo=yo: s.copy(
                        s_tyT[:, yo + 512 * db:yo + 512 * (db + 1)],
                        pt[bk][:, 0:512]), reads=(pb[bk],), writes=(tyTb[xsl],))
                kb.dma("sync", f"tyo{xsl}",
                       d_y1T.ap()[:, t0:t0 + TC].rearrange(
                           "(db p) t -> p db t", p=128),
                       s_tyT[:, yo:yo + NSEG * D].rearrange(
                           "p (db t) -> p db t", t=TC),
                       reads=(tyTb[xsl],), writes=(dy1Tb[xsl],))
                if c8 + 2 < NTC:
                    tld(c8 + 2)
        kb.final_wait("sync", ["tyo0", "tyo1"])
        kb.wait("sync", "e_tensor", kb.cnt["e_tensor"])
        kb.wait("sync", "e_scalar", kb.cnt["e_scalar"])

        with contextlib.ExitStack() as esC:
            SC = lambda n, s, dt=F32: esC.enter_context(nc.sbuf_tensor(n, s, dt))
            s_wgq = SC("s_wgq", [128, 8 * II], BF16)
            s_wuq = SC("s_wuq", [128, 8 * II], BF16)
            s_y1C = SC("s_y1C", [128, 2 * 8 * TC])
            s_t1 = SC("s_t1", [128, 2 * 2 * TC])
            s_xqT = SC("s_xqT", [128, 2 * 8 * TC], BF16)
            s_silC = SC("s_silC", [128, 2 * TC])
            s_hC = SC("s_hC", [128, 2 * TC])

            wgqb, wuqb = mkbuf(), mkbuf()
            y1Cb = [mkbuf(), mkbuf()]
            xqTb = [mkbuf(), mkbuf()]
            t1b = [mkbuf(), mkbuf()]
            silCb = [mkbuf(), mkbuf()]
            hCb = [mkbuf(), mkbuf()]

            for k in range(8):
                kb.dma("sync", "wCg", s_wgq[:, II * k:II * (k + 1)].rearrange(
                    "p (it c) -> p it c", c=128),
                    g_wgq.ap()[k].rearrange("it p c -> p it c"),
                    reads=(Bg["gwg"],), writes=(wgqb,))
                kb.dma("sync", "wCu", s_wuq[:, II * k:II * (k + 1)].rearrange(
                    "p (it c) -> p it c", c=128),
                    g_wuq.ap()[k].rearrange("it p c -> p it c"),
                    reads=(Bg["gwu"],), writes=(wuqb,))

            def c_quant_units(c8):
                t0 = c8 * TC
                xsl = c8 % 2
                qoff = xsl * 8 * TC
                units = []

                def u_load():
                    kb.dma("sync", f"yC{xsl}",
                           s_y1C[:, qoff:qoff + 8 * TC].rearrange(
                               "p (k t) -> p k t", t=TC),
                           d_y1T.ap()[:, t0:t0 + TC].rearrange(
                               "(k p) t -> p k t", p=128),
                           reads=tuple(dy1Tb), writes=(y1Cb[xsl],))
                units.append(u_load)
                for u in range(4):
                    def u_q(u=u, xsl=xsl, qoff=qoff):
                        usl = u % 2
                        uo = 2 * TC * usl
                        ysl = s_y1C[:, qoff + 2 * TC * u:qoff + 2 * TC * (u + 1)]
                        kb.op("vector", lambda v, ysl=ysl, uo=uo: v.tensor_scalar(
                            s_t1[:, uo:uo + 2 * TC], ysl,
                            s_col[:, 1:2], 127.0, ALU.mult, ALU.min),
                            reads=(y1Cb[xsl], colb), writes=(t1b[usl],))
                        kb.op("vector", lambda v, uo=uo: v.tensor_scalar(
                            s_t1[:, uo:uo + 2 * TC], s_t1[:, uo:uo + 2 * TC],
                            -128.0, MAGIC,
                            ALU.max, ALU.add), reads=(t1b[usl],), writes=(t1b[usl],))
                        kb.op("scalar", lambda s, u=u, uo=uo, qoff=qoff: s.activation(
                            s_xqT[:, qoff + 2 * TC * u:qoff + 2 * TC * (u + 1)],
                            s_t1[:, uo:uo + 2 * TC],
                            AF.Identity, bias=s_col[:, 3:4]),
                            reads=(t1b[usl], colb), writes=(xqTb[xsl],))
                    units.append(u_q)
                return units

            for u in c_quant_units(0):
                u()
            for c8 in range(NTC):
                t0 = c8 * TC
                xsl = c8 % 2
                qoff = xsl * 8 * TC
                nxt = c_quant_units(c8 + 1) if c8 + 1 < NTC else []
                for j in range(22):
                    bkg = j % 2
                    ssl = j % 2
                    kb.mm_group(pb[bkg], [
                        (lambda t, k=k, bkg=bkg, j=j, qoff=qoff: t.matmul(
                            pt[bkg][:, :],
                            s_wgq[:, II * k + 128 * j: II * k + 128 * (j + 1)],
                            s_xqT[:, qoff + TC * k:qoff + TC * (k + 1)],
                            start=(k == 0), stop=(k == 7)))
                        for k in range(8)], reads=(wgqb, xqTb[xsl]))
                    kb.op("scalar", lambda s, j=j, bkg=bkg, ssl=ssl: s.activation(
                        s_silC[:, TC * ssl:TC * (ssl + 1)], pt[bkg][:, :], AF.Silu,
                        scale=s_gcs[:, j:j + 1]),
                        reads=(pb[bkg], Bg["gcs"]), writes=(silCb[ssl],))
                    bku = 4 + j % 2
                    kb.mm_group(pb[bku], [
                        (lambda t, k=k, bku=bku, j=j, qoff=qoff: t.matmul(
                            pt[bku][:, :],
                            s_wuq[:, II * k + 128 * j: II * k + 128 * (j + 1)],
                            s_xqT[:, qoff + TC * k:qoff + TC * (k + 1)],
                            start=(k == 0), stop=(k == 7)))
                        for k in range(8)], reads=(wuqb, xqTb[xsl]))
                    kb.op("vector", lambda v, j=j, bku=bku, ssl=ssl:
                          v.scalar_tensor_tensor(
                              s_hC[:, TC * ssl:TC * (ssl + 1)], pt[bku][:, :],
                              s_ucs[:, j:j + 1], s_silC[:, TC * ssl:TC * (ssl + 1)],
                              ALU.mult, ALU.mult),
                          reads=(pb[bku], Bg["ucs"], silCb[ssl]), writes=(hCb[ssl],))
                    kb.op("vector", lambda v, ssl=ssl, col=22 * c8 + j: v.reduce_max(
                        s_max2[:, col:col + 1], s_hC[:, TC * ssl:TC * (ssl + 1)],
                        AX.X, apply_absolute_value=True),
                        reads=(hCb[ssl],), writes=(Bg["max2"],))
                    kb.dma("sync", f"sph{ssl}",
                           d_h.ap()[128 * j:128 * (j + 1), t0:t0 + TC],
                           s_hC[:, TC * ssl:TC * (ssl + 1)],
                           reads=(hCb[ssl],), writes=(Bg[f"dh{ssl}"],))
                    if nxt and j % 2 == 1:
                        nxt.pop(0)()
                while nxt:
                    nxt.pop(0)()
            kb.fence()

        max_collective(s_max2, 22 * NTC, cc2i, cc2o, s_sc2, sc2b, Bg["max2"],
                       Bg["cc2i"], Bg["cc2o"], 1, "tnyB", d_sc2v, Bg["dsc2v"])
        with contextlib.ExitStack() as esD:
            SD = lambda n, s, dt=F32: esD.enter_context(nc.sbuf_tensor(n, s, dt))
            s_wdq = SD("s_wdq", [128, 22 * D], BF16)
            s_sdxr = SD("s_sdxr", [1, D])
            s_sdx = SD("s_sdx", [128, D])

            kb.dma("sync", "tnyB", s_col[:, 2:3],
                   bass.AP(d_sc2v, 2, [[0, 128], [1, 1]]),
                   reads=(Bg["dsc2v"],), writes=(colb,))
            kb.dma("sync", "tnyC", s_sdxr[0:1, :], d_sd.ap()[None, :],
                   reads=(Bg["dsd"],), writes=(Bg["sdxr"],))
            kb.op("vector", lambda v: v.tensor_scalar(
                s_sdxr[0:1, :], s_sdxr[0:1, :], s_sc2[0:1, 1:2], None, ALU.mult),
                reads=(Bg["sdxr"], sc2b), writes=(Bg["sdxr"],))
            kb.dma("sync", "tnyC", d_sdxv.ap(), s_sdxr[0:1, :],
                   reads=(Bg["sdxr"],), writes=(Bg["dsdxv"],))
            kb.dma("sync", "tnyC", s_sdx[:, :],
                   bass.AP(d_sdxv, 0, [[0, 128], [1, D]]),
                   reads=(Bg["dsdxv"],), writes=(Bg["sdx"],))


            s_hin = SD("s_hin", [128, 4 * TC])
            s_t1D = SD("s_t1D", [128, 4 * TC])
            s_hq = SD("s_hq", [128, 2 * 22 * TC], BF16)
            s_y1D = SD("s_y1D", [128, 2 * NSEG * D])
            s_f = SD("s_f", [128, 512])
            s_y2 = SD("s_y2", [128, NSEG * D])
            s_lncD = SD("s_lncD", [128, NSEG * 5])
            s_dumpD = SD("s_dumpD", [128, 1024], BF16)

            wdqb = mkbuf()
            hinb = [mkbuf(), mkbuf(), mkbuf(), mkbuf()]
            t1Db = [mkbuf(), mkbuf(), mkbuf(), mkbuf()]
            hqb = [[mkbuf(f"hq{s}_{j}") for j in range(22)] for s in range(2)]
            y1Db = [mkbuf(), mkbuf()]
            fb, lncDb = mkbuf(), mkbuf()
            y2b = [mkbuf(f"y2{g}") for g in range(NSEG)]

            for k in range(22):
                kb.dma("sync", "wD", s_wdq[:, D * k:D * (k + 1)].rearrange(
                    "p (j c) -> p j c", c=128),
                    g_wdq.ap()[:, k, :, :].rearrange("j p c -> p j c"),
                    reads=(Bg["gwd"],), writes=(wdqb,))

            def d_quant_units(c8):
                t0 = c8 * TC
                xsl = c8 % 2
                yoff = xsl * NSEG * D
                hoff = xsl * 22 * TC
                units = []
                def uy(xsl=xsl, yoff=yoff, t0=t0):
                    kb.dma("sync", f"yD{xsl}",
                           s_y1D[:, yoff:yoff + NSEG * D].rearrange("p (g v) -> p g v", v=D),
                           d_y1.ap()[t0:t0 + TC, :].rearrange("(g p) v -> p g v", p=128),
                           reads=(Bg["dy10"], Bg["dy11"]), writes=(y1Db[xsl],))
                units.append(uy)
                for j in range(22):
                    def uq(j=j, xsl=xsl, hoff=hoff, t0=t0):
                        sl = j % 4
                        kb.dma("sync", f"hD{sl}", s_hin[:, TC * sl:TC * (sl + 1)],
                               d_h.ap()[128 * j:128 * (j + 1), t0:t0 + TC],
                               reads=(Bg["dh0"], Bg["dh1"]), writes=(hinb[sl],))
                        kb.op("vector", lambda v, sl=sl: v.tensor_scalar(
                            s_t1D[:, TC * sl:TC * (sl + 1)], s_hin[:, TC * sl:TC * (sl + 1)],
                            s_col[:, 2:3], 127.0, ALU.mult, ALU.min),
                            reads=(hinb[sl], colb), writes=(t1Db[sl],))
                        kb.op("vector", lambda v, sl=sl: v.tensor_scalar(
                            s_t1D[:, TC * sl:TC * (sl + 1)], s_t1D[:, TC * sl:TC * (sl + 1)],
                            -128.0, MAGIC, ALU.max, ALU.add),
                            reads=(t1Db[sl],), writes=(t1Db[sl],))
                        kb.op("scalar", lambda s, j=j, sl=sl, hoff=hoff: s.activation(
                            s_hq[:, hoff + TC * j:hoff + TC * (j + 1)],
                            s_t1D[:, TC * sl:TC * (sl + 1)],
                            AF.Identity, bias=s_col[:, 3:4]),
                            reads=(t1Db[sl], colb), writes=(hqb[xsl][j],))
                    units.append(uq)
                return units

            def d_mm_epi(c8, g):
                t0 = c8 * TC
                xsl = c8 % 2
                yoff = xsl * NSEG * D
                hoff = xsl * 22 * TC
                for dc in range(2):
                    bk = 4 + dc
                    kb.mm_group(pb[bk], [
                        (lambda t, j=j, bk=bk, g=g, dc=dc, hoff=hoff: t.matmul(
                            pt[bk][:, :],
                            s_hq[:, hoff + TC * j + CH * g: hoff + TC * j + CH * (g + 1)],
                            s_wdq[:, D * j + 512 * dc: D * j + 512 * (dc + 1)],
                            start=(j == 0), stop=(j == 21)))
                        for j in range(22)], reads=tuple(hqb[xsl]) + (wdqb,))
                    kb.op("vector", lambda v, dc=dc, bk=bk: v.tensor_tensor(
                        s_f[:, :], pt[bk][:, :], s_sdx[:, 512 * dc:512 * (dc + 1)],
                        ALU.mult), reads=(pb[bk], Bg["sdx"]), writes=(fb,))
                    kb.op("vector", lambda v, g=g, dc=dc, yoff=yoff: v.tensor_tensor(
                        s_y2[:, D * g + 512 * dc: D * g + 512 * (dc + 1)],
                        s_f[:, :],
                        s_y1D[:, yoff + D * g + 512 * dc: yoff + D * g + 512 * (dc + 1)],
                        ALU.add), reads=(fb, y1Db[xsl]), writes=(y2b[g],))
                ysl = s_y2[:, D * g:D * (g + 1)]
                c0 = 4 * g
                kb.op("scalar", lambda s, ysl=ysl, c0=c0: s.activation(
                    s_dumpD[:, :], ysl, AF.Identity,
                    accum_out=s_lncD[:, c0:c0 + 1]),
                    reads=(y2b[g],), writes=(dumpbuf(), lncDb))
                kb.op("scalar", lambda s, ysl=ysl, c0=c0: s.activation(
                    s_dumpD[:, :], ysl, AF.Square,
                    accum_out=s_lncD[:, c0 + 1:c0 + 2]),
                    reads=(y2b[g],), writes=(dumpbuf(), lncDb))

            def d_ln_finish(c8):
                t0 = c8 * TC
                mu = s_lncD[:, 0:16:4]
                ss2 = s_lncD[:, 1:16:4]
                var = s_lncD[:, 2:16:4]
                rst = s_lncD[:, 3:16:4]
                kb.op("vector", lambda v: v.tensor_scalar(
                    mu, mu, 1.0 / D, None, ALU.mult), reads=(lncDb,), writes=(lncDb,))
                kb.op("vector", lambda v: v.tensor_tensor(
                    var, mu, mu, ALU.mult), reads=(lncDb,), writes=(lncDb,))
                kb.op("vector", lambda v: v.tensor_scalar(
                    ss2, ss2, 1.0 / D, EPS, ALU.mult, ALU.add),
                    reads=(lncDb,), writes=(lncDb,))
                kb.op("vector", lambda v: v.tensor_tensor(
                    var, ss2, var, ALU.subtract), reads=(lncDb,), writes=(lncDb,))
                kb.op("scalar", lambda s: s.activation(
                    rst, var, AF.Sqrt), reads=(lncDb,), writes=(lncDb,))
                kb.op("vector", lambda v: v.reciprocal(rst, rst),
                      reads=(lncDb,), writes=(lncDb,))
                kb.op("vector", lambda v, mu=mu, rst=rst: v.scalar_tensor_tensor(
                    s_lncD[:, 16:20], mu, -1.0, rst, ALU.mult, ALU.mult),
                    reads=(lncDb,), writes=(lncDb,))
                for g in range(NSEG):
                    ysl = s_y2[:, D * g:D * (g + 1)]
                    c0 = 4 * g
                    kb.op("scalar", lambda s, ysl=ysl, c0=c0, g=g: s.activation(
                        ysl, ysl, AF.Identity, scale=s_lncD[:, c0 + 3:c0 + 4],
                        bias=s_lncD[:, 16 + g:17 + g]),
                        reads=(y2b[g], lncDb), writes=(y2b[g],))
                    kb.dma("sync", "st_out",
                           out[t0 + CH * g:t0 + CH * (g + 1), :], ysl,
                           reads=(y2b[g],), writes=(Bg["dout"],))

            pend = d_quant_units(0)
            for u in pend:
                u()
            for c8 in range(NTC):
                nxt = d_quant_units(c8 + 1) if c8 + 1 < NTC else []
                for _ in range(11):
                    if nxt:
                        nxt.pop(0)()
                for g in range(NSEG):
                    d_mm_epi(c8, g)
                    for _ in range(3):
                        if nxt:
                            nxt.pop(0)()
                d_ln_finish(c8)
                while nxt:
                    nxt.pop(0)()
            kb.final_wait("sync", ["st_out"])
            kb.fence(("sync",))
        kb.finish()
    return nc




def _in_maps(inputs, T=4096, ncore=NCORE):
    import ml_dtypes
    f32 = np.float32
    x = np.asarray(inputs["L0_attn_x"], f32)[:, :T, :]
    g = lambda n: np.ascontiguousarray(np.asarray(inputs[n], f32))
    wq, wk = g("L0_attn_Wq"), g("L0_attn_Wk")
    wv, wgm = g("L0_attn_Wv"), g("L0_attn_Wg")
    wgk1, wgk2 = g("L0_attn_Wgk1"), g("L0_attn_Wgk2")
    bgk2, wo = g("L0_attn_bgk2"), g("L0_attn_Wo")
    rms_w = g("L0_attn_rms_w")
    w_gate, w_up, w_down = g("L0_ffn_w_gate"), g("L0_ffn_w_up"), g("L0_ffn_w_down")
    clip = np.ascontiguousarray(
        np.stack([np.asarray(inputs["L0_ffn_clip_x"], f32).ravel()[0],
                  np.asarray(inputs["L0_ffn_clip_down"], f32).ravel()[0]])[None, :])
    shared = {
        "wq": wq, "wk": wk, "wv": wv, "wg": wgm,
        "wgk1": wgk1, "wgk2": wgk2,
        "bgk": np.ascontiguousarray(bgk2.reshape(8, 128).T),
        "wo": wo,
        "rmsw": np.ascontiguousarray(np.tile(rms_w[None, :], (128, 4))),
        "wgo": w_gate, "wuo": w_up, "wdo": w_down,

        "clip": clip,
        "ident": np.eye(128, dtype=ml_dtypes.bfloat16),
        "mask": np.ascontiguousarray(np.triu(np.ones((128, 128), f32))),
        "ones1": np.ones((1, 128), f32),
    }
    maps = []
    for c in range(min(ncore, x.shape[0])):
        m = dict(shared)
        m["xf"] = np.ascontiguousarray(x[c].T)
        m["xt"] = np.ascontiguousarray(x[c])
        m["wg_sl"] = np.ascontiguousarray(w_gate[:, 128 * c:128 * (c + 1)])
        m["wu_sl"] = np.ascontiguousarray(w_up[:, 128 * c:128 * (c + 1)])
        m["wd_sl"] = np.ascontiguousarray(w_down[128 * c:128 * (c + 1), :])
        maps.append(m)
    return maps


def kernel(**inputs):
    T = 4096
    maps = _in_maps(inputs, T)
    nc = build(T)
    res = run_bass_kernel_spmd(nc, maps, core_ids=list(range(NCORE)))
    out = np.stack([np.asarray(res.results[c]["out"]) for c in range(NCORE)], 0)
    return np.ascontiguousarray(out.astype(np.float32))


# version 24
# speedup vs baseline: 1.0891x; 1.0608x over previous
import contextlib
import math
import numpy as np
import concourse.bass as bass
import concourse.mybir as mybir
from concourse.bass_utils import run_bass_kernel_spmd

F32 = mybir.dt.float32
BF16 = mybir.dt.bfloat16
I32 = mybir.dt.int32
AF = mybir.ActivationFunctionType
ALU = mybir.AluOpType
AX = mybir.AxisListType

ENGS = ("tensor", "vector", "scalar", "gpsimd", "sync")



class Buf:
    __slots__ = ("name", "last_w", "readers")

    def __init__(self, name=""):
        self.name = name
        self.last_w = None
        self.readers = {}


class KB:
    def __init__(self, nc, es):
        self.nc = nc
        self.es = es
        self.sems = {}
        self.cnt = {}
        self.known = {e: {} for e in ENGS}
        self.q = {e: [] for e in ENGS}
        for e in ENGS:
            self.mk_sem("e_" + e)

    def mk_sem(self, name):
        self.sems[name] = self.es.enter_context(self.nc.semaphore(name))
        self.cnt[name] = 0
        return name

    def wait(self, ename, sem_name, val):
        k = self.known[ename]
        if k.get(sem_name, 0) < val:
            sem = self.sems[sem_name]
            self.q[ename].append(lambda eng, s=sem, v=val: eng.wait_ge(s, v))
            k[sem_name] = val

    def _pre(self, ename, reads, writes, exempt_self_waw=False, dma_sem=None):
        need = {}

        def add(item, waw=False):
            if item is None:
                return
            sn, v = item
            if exempt_self_waw and sn == "e_" + ename:
                return
            if waw and dma_sem is not None and sn == dma_sem:
                return
            if "@" in sn:
                v = max(v, self.cnt[sn])
            if need.get(sn, 0) < v:
                need[sn] = v

        for b in reads:
            add(b.last_w)
        for b in writes:
            add(b.last_w, waw=True)
            for sn, v in b.readers.items():
                add((sn, v))
        for sn in sorted(need):
            self.wait(ename, sn, need[sn])

    def _post(self, ename, f, sem_name, inc, reads, writes):
        self.cnt[sem_name] += inc
        v = self.cnt[sem_name]
        sem = self.sems[sem_name]
        self.q[ename].append(lambda eng, s=sem, i=inc: f(eng).then_inc(s, i))
        for b in writes:
            b.last_w = (sem_name, v)
            b.readers = {}
        for b in reads:
            if b.readers.get(sem_name, 0) < v:
                b.readers[sem_name] = v

    def op(self, ename, f, reads=(), writes=(), accum=False):
        self._pre(ename, reads, writes, exempt_self_waw=accum)
        self._post(ename, f, "e_" + ename, 1, reads, writes)

    def mm(self, psum_buf, out_ap, lhsT_ap, rhs_ap, reads=(), start=True, stop=True):
        self.op("tensor",
                lambda t: t.matmul(out_ap, lhsT_ap, rhs_ap, start=start, stop=stop),
                reads=reads, writes=(psum_buf,), accum=True)

    def mm_group(self, psum_buf, emits, reads=()):
        self._pre("tensor", reads, (psum_buf,), exempt_self_waw=True)
        for f in emits[:-1]:
            self.q["tensor"].append(f)
        self._post("tensor", emits[-1], "e_tensor", 1, reads, (psum_buf,))

    def dma(self, ename, stream, out, in_, reads=(), writes=()):
        key = f"{stream}@{ename}"
        if key not in self.sems:
            self.mk_sem(key)
        self._pre(ename, reads, writes, dma_sem=key)
        self._post(ename, lambda eng, o=out, i=in_: eng.dma_start(out=o, in_=i),
                   key, 16, reads, writes)

    def fence(self, engines=ENGS):
        for e in engines:
            for sn in sorted(self.cnt):
                if self.cnt[sn]:
                    self.wait(e, sn, self.cnt[sn])

    def final_wait(self, ename, streams):
        for sn in sorted(self.cnt):
            if sn.split("@")[0] in streams and self.cnt[sn]:
                self.wait(ename, sn, self.cnt[sn])

    def finish(self):
        with self.nc.Block() as block:
            for ename in ENGS:
                queue = self.q[ename]
                if queue:
                    getattr(block, ename)(lambda eng, qq=queue: [c(eng) for c in qq])



D = 1024
HEADS = 4
HDK = 256
HDV = 512
DV = 2048
LR = 16
II = 2816
CH = 128
TC = 512
LN16 = math.log(1.0 / 16.0)
MAGIC = 12582912.0
EPS = 1e-5
NCORE = 8


def build(T=4096, ncore=NCORE):
    NTC = T // TC
    NSEG = TC // CH
    nc = bass.Bass("TRN2", target_bir_lowering=False)

    dr = lambda n, s, dt=F32: nc.dram_tensor(n, s, dt, kind="ExternalInput").ap()
    xf = dr("xf", [D, T])
    xt = dr("xt", [T, D])
    wq = dr("wq", [D, D])
    wk = dr("wk", [D, D])
    wv = dr("wv", [D, DV])
    wg = dr("wg", [D, DV])
    wgk1 = dr("wgk1", [D, LR])
    wgk2 = dr("wgk2", [LR, D])
    bgk = dr("bgk", [128, 8])
    wo = dr("wo", [DV, D])
    rmsw = dr("rmsw", [128, DV])
    wgo = dr("wgo", [II, D])
    wuo = dr("wuo", [II, D])
    wdo = dr("wdo", [D, II])
    wg_sl = dr("wg_sl", [II, 128])
    wu_sl = dr("wu_sl", [II, 128])
    wd_sl = dr("wd_sl", [128, II])
    clip = dr("clip", [1, 2])
    ident = dr("ident", [128, 128], BF16)
    mask = dr("mask", [128, 128])
    ones1 = dr("ones1", [1, 128])

    out = nc.dram_tensor("out", [T, D], F32, kind="ExternalOutput").ap()

    ds = lambda n, s, dt=F32: nc.dram_tensor(n, s, dt)
    d_o = ds("d_o", [T, DV], BF16)
    d_ss = ds("d_ss", [T // 128, 128, HEADS])
    d_y1 = ds("d_y1", [T, D])
    d_y1T = ds("d_y1T", [D, T])
    d_h = ds("d_h", [II, T])
    d_g_sh = ds("d_g_sh", [22, 128, 128], BF16)
    d_u_sh = ds("d_u_sh", [22, 128, 128], BF16)
    d_d_sh = ds("d_d_sh", [22, 128, 128], BF16)
    assert ncore == NCORE, "sharded weight-quant requires 8 cores"
    g_wgq = ds("g_wgq", [8, 22, 128, 128], BF16)
    g_wuq = ds("g_wuq", [8, 22, 128, 128], BF16)
    g_wdq = ds("g_wdq", [8, 22, 128, 128], BF16)
    d_sd = ds("d_sd", [D])
    d_mx = ds("d_mx", [128])
    d_sc1v = ds("d_sc1v", [8])
    d_sc2v = ds("d_sc2v", [8])
    d_sdxv = ds("d_sdxv", [D])
    cc1i = ds("cc1i", [8])
    cc1o = ds("cc1o", [8])
    cc2i = ds("cc2i", [8])
    cc2o = ds("cc2o", [8])

    es = contextlib.ExitStack()
    with es:
        kb = KB(nc, es)
        SB = lambda n, s, dt=F32: es.enter_context(nc.sbuf_tensor(n, s, dt))
        nb = [0]

        def mkbuf(name=None):
            nb[0] += 1
            return Buf(name or f"b{nb[0]}")

        pt = [es.enter_context(nc.psum_tensor(f"p{i}", [128, 512], F32))
              for i in range(8)]
        pb = [mkbuf(f"p{i}") for i in range(8)]

        s_id = SB("s_id", [128, 128], BF16)
        s_idf = SB("s_idf", [128, 128])
        s_mask = SB("s_mask", [128, 128])
        s_ones = SB("s_ones", [1, 128])
        s_clip = SB("s_clip", [1, 2])
        s_sig = SB("s_sig", [1, 2])
        s_sc1 = SB("s_sc1", [1, 8])
        s_sc2 = SB("s_sc2", [1, 8])
        s_cc8 = SB("s_cc8", [1, 8])
        s_mrow = SB("s_mrow", [1, 128])
        s_col = SB("s_col", [128, 8])
        s_gc = SB("s_gc", [128, 24])
        s_uc = SB("s_uc", [128, 24])
        s_dc = SB("s_dc", [128, 8])
        s_gcs = SB("s_gcs", [128, 24])
        s_ucs = SB("s_ucs", [128, 24])
        s_max1 = SB("s_max1", [128, 34])
        s_max2 = SB("s_max2", [128, 178])

        Bg = {n: mkbuf(n) for n in
              ("id mask ones clip sig sc1 sc2 cc8 mrow col gc uc dc gcs ucs "
               "sdxr sdx dump wgw max1 max2 dmx cc1i cc1o cc2i cc2o "
               "dsd dsc1v dsc2v dsdxv gsh ush dsh gwg gwu gwd "
               "do0 do1 dss dy10 dy11 dh0 dh1 dout").split()}
        dump_bufs = [mkbuf("dump0"), mkbuf("dump1")]
        dump_i = [0]

        def dumpbuf():
            dump_i[0] ^= 1
            return dump_bufs[dump_i[0]]

        cc_sem = kb.mk_sem("cc_done")

        kb.dma("sync", "cst", s_id[:, :], ident, writes=(Bg["id"],))
        kb.dma("sync", "cst", s_mask[:, :], mask, writes=(Bg["mask"],))
        kb.dma("sync", "cst", s_ones[0:1, :], ones1, writes=(Bg["ones"],))
        kb.dma("sync", "cst", s_clip[0:1, :], clip, writes=(Bg["clip"],))
        kb.op("scalar", lambda s: s.activation(s_sig[0:1, :], s_clip[0:1, :], AF.Sigmoid),
              reads=(Bg["clip"],), writes=(Bg["sig"],))
        kb.op("vector", lambda v: v.memset(s_col[:, 3:4], -MAGIC), writes=(Bg["col"],))
        idfb = mkbuf("idf")
        kb.op("scalar", lambda s: s.copy(s_idf[:, :], s_id[:, :]),
              reads=(Bg["id"],), writes=(idfb,))
        kb.op("vector", lambda v: v.memset(s_max1[:, :], 0.0), writes=(Bg["max1"],))
        kb.op("vector", lambda v: v.memset(s_max2[:, :], 0.0), writes=(Bg["max2"],))
        kb.op("vector", lambda v: v.memset(s_cc8[0:1, :], 0.0), writes=(Bg["cc8"],))


        esAB = contextlib.ExitStack()
        s_xf = esAB.enter_context(nc.sbuf_tensor(
            "s_xf", [128, 2 * 8 * TC], BF16))
        s_wg = esAB.enter_context(nc.sbuf_tensor(
            "s_wg", [128, 8 * DV], BF16))
        with contextlib.ExitStack() as esA:
            SA = lambda n, s, dt=F32: esA.enter_context(nc.sbuf_tensor(n, s, dt))
            s_wq = SA("s_wq", [128, 8 * D], BF16)
            s_wk = SA("s_wk", [128, 8 * D], BF16)
            s_wv = SA("s_wv", [128, 8 * DV], BF16)
            s_wgk1 = SA("s_wgk1", [128, 8 * LR], BF16)
            s_wgk2 = SA("s_wgk2", [16, D], BF16)
            s_bgk = SA("s_bgk", [128, 8])
            s_z = SA("s_z", [16, TC], BF16)
            s_sg = SA("s_sg", [128, 2 * TC])
            s_sg8 = SA("s_sg8", [128, 8 * TC], BF16)
            s_B = SA("s_B", [128, 8 * TC])
            s_carry = SA("s_carry", [128, 8])
            s_nbq = SA("s_nbq", [128, 8])
            s_pbc = SA("s_pbc", [128, 8])
            s_basep = SA("s_basep", [128, 8])
            s_resc = SA("s_resc", [128, 8])
            s_Ep = SA("s_Ep", [128, TC], BF16)
            s_Em = SA("s_Em", [128, TC], BF16)
            s_q = SA("s_q", [128, 8 * TC], BF16)
            s_kd = SA("s_kd", [128, 8 * TC], BF16)
            s_v = SA("s_v", [128, 2 * DV], BF16)
            s_At = SA("s_At", [128, 2 * 128], BF16)
            s_kg = SA("s_kg", [128, 4 * 128], BF16)
            s_S = SA("s_S", [128, HEADS * 2 * HDV], BF16)
            s_o = SA("s_o", [128, DV], BF16)
            s_ss = SA("s_ss", [128, NSEG * HEADS])
            s_dumpA = SA("s_dumpA", [128, 512], BF16)

            wunits = []
            HWI = 1408
            s_wsl = SA("s_wsl", [128, II])
            s_wq8 = SA("s_wq8", [128, II], BF16)
            s_evt = SA("s_evt", [128, 512], BF16)
            s_rcol = SA("s_rcol", [128, 48])
            wst_b = [mkbuf("wst0"), mkbuf("wst1")]
            wq8_b = mkbuf("wq8")
            evt_b = mkbuf("evt")
            rcol_b = mkbuf("rcol")

            def w_scales(worig, n_tiles, hwid, sc_sbuf, sc_buf, rbase,
                         srow_d=None, srow_buf=None):
                for j in range(n_tiles):
                    def u(j=j):
                        if hwid <= HWI:
                            sl = j % 2
                            off = sl * HWI
                            kb.dma("sync", f"ws{sl}", s_wsl[:, off:off + hwid],
                                   worig[128 * j:128 * (j + 1), :], writes=(wst_b[sl],))
                            kb.op("vector", lambda v, j=j, off=off, sl=sl: v.reduce_max(
                                sc_sbuf[:, j:j + 1], s_wsl[:, off:off + hwid],
                                AX.X, apply_absolute_value=True),
                                reads=(wst_b[sl],), writes=(sc_buf,))
                        else:
                            nh = (hwid + HWI - 1) // HWI
                            for hh in range(nh):
                                w0 = HWI * hh
                                wl = min(hwid, HWI * (hh + 1)) - w0
                                sl = hh % 2
                                off = sl * HWI
                                kb.dma("sync", f"ws{sl}", s_wsl[:, off:off + wl],
                                       worig[128 * j:128 * (j + 1), w0:w0 + wl],
                                       writes=(wst_b[sl],))
                                kb.op("vector", lambda v, wl=wl, hh=hh, off=off,
                                      sl=sl: v.reduce_max(
                                    s_rcol[:, 45 + hh:46 + hh], s_wsl[:, off:off + wl],
                                    AX.X, apply_absolute_value=True),
                                    reads=(wst_b[sl],), writes=(rcol_b,))
                            for hh in range(1, nh):
                                kb.op("vector", lambda v, hh=hh: v.tensor_tensor(
                                    s_rcol[:, 45:46], s_rcol[:, 45:46],
                                    s_rcol[:, 45 + hh:46 + hh], ALU.max),
                                    reads=(rcol_b,), writes=(rcol_b,))
                            kb.op("vector", lambda v, j=j: v.tensor_copy(
                                sc_sbuf[:, j:j + 1], s_rcol[:, 45:46]),
                                reads=(rcol_b,), writes=(sc_buf,))
                    wunits.append(u)

                def u_fin():
                    kb.op("vector", lambda v: v.tensor_scalar(
                        sc_sbuf[:, 0:n_tiles], sc_sbuf[:, 0:n_tiles],
                        1.0 / 127.0, None, ALU.mult),
                        reads=(sc_buf,), writes=(sc_buf,))
                    if rbase is not None:
                        rc = s_rcol[:, rbase:rbase + n_tiles]
                        kb.op("vector", lambda v: v.reciprocal(
                            rc, sc_sbuf[:, 0:n_tiles]), reads=(sc_buf,), writes=(rcol_b,))
                        kb.op("vector", lambda v: v.tensor_tensor(
                            s_wsl[:, 0:n_tiles], sc_sbuf[:, 0:n_tiles], rc, ALU.mult),
                            reads=(sc_buf, rcol_b), writes=(wst_b[0],))
                        kb.op("vector", lambda v: v.tensor_scalar(
                            s_wsl[:, 0:n_tiles], s_wsl[:, 0:n_tiles], 2.0, -1.0,
                            ALU.subtract, ALU.mult), reads=(wst_b[0],), writes=(wst_b[0],))
                        kb.op("vector", lambda v: v.tensor_tensor(
                            rc, rc, s_wsl[:, 0:n_tiles], ALU.mult),
                            reads=(wst_b[0], rcol_b), writes=(rcol_b,))
                    if srow_d is not None:
                        for j in range(n_tiles):
                            kb.dma("sync", "trow", srow_d.ap()[128 * j:128 * (j + 1)],
                                   sc_sbuf[:, j:j + 1], reads=(sc_buf,), writes=(srow_buf,))
                wunits.append(u_fin)

            def w_shard_quant(wsl_dram, mode, rbase, sh_dram, sh_buf, g_dram, g_buf,
                              own_scale=False):
                def u_load():
                    if mode == "gu":
                        kb.dma("sync", "wsl", s_wsl[:, :].rearrange(
                            "p (it h) -> p it h", h=128),
                            wsl_dram.rearrange("(it p) h -> p it h", p=128),
                            writes=(wst_b[0], wst_b[1]))
                    else:
                        kb.dma("sync", "wsl", s_wsl[:, :], wsl_dram,
                               writes=(wst_b[0], wst_b[1]))
                wunits.append(u_load)
                if own_scale:
                    def u_osc():
                        kb.op("vector", lambda v: v.reduce_max(
                            s_rcol[:, 44:45], s_wsl[:, :], AX.X,
                            apply_absolute_value=True),
                            reads=(wst_b[0], wst_b[1]), writes=(rcol_b,))
                        kb.op("vector", lambda v: v.tensor_scalar(
                            s_rcol[:, 44:45], s_rcol[:, 44:45], 1.0 / 127.0, None,
                            ALU.mult), reads=(rcol_b,), writes=(rcol_b,))
                        kb.op("vector", lambda v: v.reciprocal(
                            s_rcol[:, 46:47], s_rcol[:, 44:45]),
                            reads=(rcol_b,), writes=(rcol_b,))
                        kb.op("vector", lambda v: v.tensor_tensor(
                            s_rcol[:, 47:48], s_rcol[:, 44:45], s_rcol[:, 46:47],
                            ALU.mult), reads=(rcol_b,), writes=(rcol_b,))
                        kb.op("vector", lambda v: v.tensor_scalar(
                            s_rcol[:, 47:48], s_rcol[:, 47:48], 2.0, -1.0,
                            ALU.subtract, ALU.mult), reads=(rcol_b,), writes=(rcol_b,))
                        kb.op("vector", lambda v: v.tensor_tensor(
                            s_rcol[:, 46:47], s_rcol[:, 46:47], s_rcol[:, 47:48],
                            ALU.mult), reads=(rcol_b,), writes=(rcol_b,))
                    wunits.append(u_osc)

                def u_quant():
                    if mode == "gu":
                        rv = bass.AP(s_rcol, rbase, [[48, 128], [1, 22], [0, 128]])
                        kb.op("vector", lambda v, rv=rv: v.tensor_tensor(
                            s_wsl[:, :].rearrange("p (it h) -> p it h", h=128),
                            s_wsl[:, :].rearrange("p (it h) -> p it h", h=128),
                            rv, ALU.mult),
                            reads=(wst_b[0], wst_b[1], rcol_b),
                            writes=(wst_b[0], wst_b[1]))
                    else:
                        kb.op("vector", lambda v: v.tensor_scalar(
                            s_wsl[:, :], s_wsl[:, :], s_rcol[:, 46:47], None,
                            ALU.mult), reads=(wst_b[0], wst_b[1], rcol_b),
                            writes=(wst_b[0], wst_b[1]))
                    kb.op("vector", lambda v: v.tensor_scalar(
                        s_wsl[:, :], s_wsl[:, :], 127.0, -128.0,
                        ALU.min, ALU.max),
                        reads=(wst_b[0], wst_b[1]), writes=(wst_b[0], wst_b[1]))
                    kb.op("vector", lambda v: v.tensor_scalar(
                        s_wsl[:, :], s_wsl[:, :], MAGIC, MAGIC,
                        ALU.add, ALU.subtract),
                        reads=(wst_b[0], wst_b[1]), writes=(wst_b[0], wst_b[1]))
                    kb.op("scalar", lambda s: s.copy(s_wq8[:, :], s_wsl[:, :]),
                          reads=(wst_b[0], wst_b[1]), writes=(wq8_b,))
                wunits.append(u_quant)

                for it0 in range(0, 22, 4):
                    def u_tr(it0=it0):
                        nb4 = min(4, 22 - it0)
                        bk = 2 + (it0 // 4) % 2
                        for z in range(nb4):
                            it = it0 + z
                            kb.mm(pb[bk], pt[bk][:, 128 * z:128 * (z + 1)],
                                  s_wq8[:, 128 * it:128 * (it + 1)], s_id[:, :],
                                  reads=(wq8_b, Bg["id"]), start=True, stop=True)
                        kb.op("vector", lambda v, bk=bk, nb4=nb4: v.tensor_copy(
                            s_evt[:, 0:128 * nb4], pt[bk][:, 0:128 * nb4]),
                            reads=(pb[bk],), writes=(evt_b,))
                        kb.dma("sync", "wsp",
                               sh_dram.ap()[it0:it0 + nb4, :, :].rearrange(
                                   "it p c -> p it c"),
                               s_evt[:, 0:128 * nb4].rearrange(
                                   "p (it c) -> p it c", c=128),
                               reads=(evt_b,), writes=(sh_buf,))
                    wunits.append(u_tr)

                def u_ag():
                    kb._pre("gpsimd", (sh_buf,), (g_buf,))
                    kb._post("gpsimd", lambda g2: g2.collective_compute(
                        "AllGather", ALU.bypass,
                        replica_groups=[list(range(ncore))],
                        ins=[sh_dram.ap()], outs=[g_dram.ap()]),
                        "cc_done", 1, (sh_buf,), (g_buf,))
                wunits.append(u_ag)

            w_scales(wgo, 22, D, s_gc, Bg["gc"], 0)
            w_shard_quant(wg_sl, "gu", 0, d_g_sh, Bg["gsh"], g_wgq, Bg["gwg"])
            w_scales(wuo, 22, D, s_uc, Bg["uc"], 22)
            w_shard_quant(wu_sl, "gu", 22, d_u_sh, Bg["ush"], g_wuq, Bg["gwu"])
            w_scales(wdo, 8, II, s_dc, Bg["dc"], None, d_sd, Bg["dsd"])
            w_shard_quant(wd_sl, "dn", 46, d_d_sh, Bg["dsh"], g_wdq, Bg["gwd"],
                          own_scale=True)
            wu_per_iter = (len(wunits) + NTC - 1) // NTC




            wqb, wkb, wvb = mkbuf("wq"), mkbuf("wk"), mkbuf("wv")
            wgk1b, wgk2b, bgkb = mkbuf(), mkbuf(), mkbuf()
            xfb = [mkbuf("xf0"), mkbuf("xf1")]
            zb = mkbuf()
            sgb = [mkbuf(), mkbuf()]
            sg8b = mkbuf("sg8")
            Bb = [mkbuf(f"B{d}") for d in range(8)]
            carryb = mkbuf("carry")
            nbqb, pbcb = mkbuf(), mkbuf()
            basepb, rescb = mkbuf("basep"), mkbuf("resc")
            Epb = [mkbuf(), mkbuf()]
            Emb = [mkbuf(), mkbuf()]
            qb = [mkbuf(f"q{d}") for d in range(8)]
            kdb = [mkbuf(f"kd{d}") for d in range(8)]
            vb = [mkbuf("v0"), mkbuf("v1")]
            Atb = [mkbuf(), mkbuf()]
            kgb = [mkbuf(), mkbuf()]
            Sb = [mkbuf(f"S{h}") for h in range(HEADS)]
            ob = [mkbuf("o0"), mkbuf("o1")]
            ssb = mkbuf("ss")

            kb.dma("gpsimd", "wA", s_wq[:, :].rearrange("p (k t) -> p k t", t=D),
                   wq.rearrange("(k p) t -> p k t", p=128), writes=(wqb,))
            kb.dma("gpsimd", "wA", s_wk[:, :].rearrange("p (k t) -> p k t", t=D),
                   wk.rearrange("(k p) t -> p k t", p=128), writes=(wkb,))
            kb.dma("gpsimd", "wA", s_wv[:, :].rearrange("p (k t) -> p k t", t=DV),
                   wv.rearrange("(k p) t -> p k t", p=128), writes=(wvb,))
            kb.dma("gpsimd", "wA", s_wgk1[:, :].rearrange("p (k t) -> p k t", t=LR),
                   wgk1.rearrange("(k p) t -> p k t", p=128), writes=(wgk1b,))
            kb.dma("gpsimd", "wA", s_wgk2[:, :], wgk2, writes=(wgk2b,))
            kb.dma("gpsimd", "wB", s_wg[:, :].rearrange("p (k t) -> p k t", t=DV),
                   wg.rearrange("(k p) t -> p k t", p=128), writes=(Bg["wgw"],))
            kb.dma("sync", "wA", s_bgk[:, :], bgk, writes=(bgkb,))
            kb.op("vector", lambda v: v.memset(s_S[:, :], 0.0),
                  writes=tuple(Sb))
            kb.op("vector", lambda v: v.memset(s_carry[:, :], 0.0),
                  writes=(carryb,))
            kb.op("vector", lambda v: v.memset(s_basep[:, :], 0.0),
                  writes=(basepb,))

            def xload(c8):
                xsl = c8 % 2
                xoff = xsl * 8 * TC
                t0 = c8 * TC
                kb.dma("gpsimd", f"xf{xsl}",
                       s_xf[:, xoff:xoff + 8 * TC].rearrange("p (k t) -> p k t", t=TC),
                       xf[:, t0:t0 + TC].rearrange("(k p) t -> p k t", p=128),
                       writes=(xfb[xsl],))

            xload(0)
            for c8 in range(NTC):
                xsl = c8 % 2
                xoff = xsl * 8 * TC
                t0 = c8 * TC
                xload(c8 + 1 if c8 + 1 < NTC else 0)

                kb.mm_group(pb[2], [
                    (lambda t, k=k, xoff=xoff: t.matmul(
                        pt[2][0:16, 0:TC], s_wgk1[:, LR * k:LR * (k + 1)],
                        s_xf[:, xoff + TC * k: xoff + TC * (k + 1)],
                        start=(k == 0), stop=(k == 7)))
                    for k in range(8)], reads=(wgk1b, xfb[xsl]))
                kb.op("vector", lambda v: v.tensor_copy(s_z[:, :], pt[2][0:16, 0:TC]),
                      reads=(pb[2],), writes=(zb,))

                for dt in range(8):
                    bk = 2 + (dt % 2)
                    kb.mm(pb[bk], pt[bk][:, 0:TC], s_wgk2[:, 128 * dt:128 * (dt + 1)],
                          s_z[:, :], reads=(wgk2b, zb))
                    kb.op("scalar", lambda s, bk=bk, dt=dt: s.activation(
                        s_sg8[:, TC * dt:TC * (dt + 1)], pt[bk][:, 0:TC], AF.Sigmoid,
                        bias=s_bgk[:, dt:dt + 1]),
                        reads=(pb[bk], bgkb), writes=(sg8b,))
                kb.op("vector", lambda v: v.tensor_tensor(
                    s_resc[:, :], s_carry[:, :], s_basep[:, :], ALU.subtract),
                    reads=(carryb, basepb), writes=(rescb,))
                kb.op("scalar", lambda s: s.activation(
                    s_resc[:, :], s_resc[:, :], AF.Exp, scale=1.0 / 16.0),
                    reads=(rescb,), writes=(rescb,))
                kb.op("vector", lambda v: v.tensor_copy(s_basep[:, :], s_carry[:, :]),
                      reads=(carryb,), writes=(basepb,))
                kb.op("vector", lambda v: v.tensor_scalar(
                    s_nbq[:, :], s_basep[:, :], -1.0 / 16.0, LN16,
                    ALU.mult, ALU.add), reads=(basepb,), writes=(nbqb,))
                kb.op("vector", lambda v: v.tensor_scalar(
                    s_pbc[:, :], s_basep[:, :], 1.0 / 16.0, None,
                    ALU.mult), reads=(basepb,), writes=(pbcb,))
                if c8 > 0:
                    for h in range(HEADS):
                        for i in range(2):
                            kb.op("vector", lambda v, h=h, i=i: v.tensor_scalar(
                                s_S[:, 2 * HDV * h + HDV * i: 2 * HDV * h + HDV * (i + 1)],
                                s_S[:, 2 * HDV * h + HDV * i: 2 * HDV * h + HDV * (i + 1)],
                                s_resc[:, 2 * h + i:2 * h + i + 1], None, ALU.mult),
                                reads=(rescb, Sb[h]), writes=(Sb[h],))
                for dt in range(8):
                    ssl = dt % 2
                    kb.op("scalar", lambda s, dt=dt, ssl=ssl: s.activation(
                        s_sg[:, TC * ssl:TC * (ssl + 1)],
                        s_sg8[:, TC * dt:TC * (dt + 1)], AF.Ln),
                        reads=(sg8b,), writes=(sgb[ssl],))
                    kb.op("vector", lambda v, dt=dt, ssl=ssl: v.tensor_tensor_scan(
                        s_B[:, TC * dt:TC * (dt + 1)],
                        s_sg[:, TC * ssl:TC * (ssl + 1)],
                        s_sg[:, TC * ssl:TC * (ssl + 1)],
                        s_carry[:, dt:dt + 1], ALU.add, ALU.bypass),
                        reads=(sgb[ssl], carryb), writes=(Bb[dt],))
                kb.op("vector", lambda v: v.tensor_copy(
                    s_carry[:, 0:8], s_B[:, TC - 1:8 * TC:TC]),
                    reads=tuple(Bb), writes=(carryb,))

                for dt in range(8):
                    bk = (0, 1, 4, 5)[dt % 4]
                    esl = 0
                    eo = 0
                    kb.op("scalar", lambda s, dt=dt, eo=eo: s.activation(
                        s_Ep[:, eo:eo + TC], s_B[:, TC * dt:TC * (dt + 1)],
                        AF.Exp, bias=s_nbq[:, dt:dt + 1], scale=1.0 / 16.0),
                        reads=(Bb[dt], nbqb), writes=(Epb[esl],))
                    kb.op("scalar", lambda s, dt=dt, eo=eo: s.activation(
                        s_Em[:, eo:eo + TC], s_B[:, TC * dt:TC * (dt + 1)],
                        AF.Exp, bias=s_pbc[:, dt:dt + 1], scale=-1.0 / 16.0),
                        reads=(Bb[dt], pbcb), writes=(Emb[esl],))
                    kb.mm_group(pb[bk], [
                        (lambda t, k=k, bk=bk, dt=dt, xoff=xoff: t.matmul(
                            pt[bk][:, 0:TC],
                            s_wq[:, D * k + 128 * dt: D * k + 128 * (dt + 1)],
                            s_xf[:, xoff + TC * k: xoff + TC * (k + 1)],
                            start=(k == 0), stop=(k == 7)))
                        for k in range(8)], reads=(wqb, xfb[xsl]))
                    kb.op("vector", lambda v, dt=dt, bk=bk, eo=eo:
                          v.scalar_tensor_tensor(
                        s_q[:, TC * dt:TC * (dt + 1)], pt[bk][:, 0:TC], 1.0,
                        s_Ep[:, eo:eo + TC], ALU.bypass, ALU.mult),
                        reads=(pb[bk], Epb[esl]), writes=(qb[dt],))
                    bk2 = (2, 3, 6, 7)[dt % 4]
                    kb.mm_group(pb[bk2], [
                        (lambda t, k=k, bk2=bk2, dt=dt, xoff=xoff: t.matmul(
                            pt[bk2][:, 0:TC],
                            s_wk[:, D * k + 128 * dt: D * k + 128 * (dt + 1)],
                            s_xf[:, xoff + TC * k: xoff + TC * (k + 1)],
                            start=(k == 0), stop=(k == 7)))
                        for k in range(8)], reads=(wkb, xfb[xsl]))
                    kb.op("vector", lambda v, dt=dt, bk2=bk2, eo=eo:
                          v.scalar_tensor_tensor(
                        s_kd[:, TC * dt:TC * (dt + 1)], pt[bk2][:, 0:TC], 1.0,
                        s_Em[:, eo:eo + TC], ALU.bypass, ALU.mult),
                        reads=(pb[bk2], Emb[esl]), writes=(kdb[dt],))

                def v_proj(g):
                    vsl = g % 2
                    for nuc in range(4):
                        bk = 2 + nuc % 2
                        kb.mm_group(pb[bk], [
                            (lambda t, k=k, bk=bk, g=g, nuc=nuc, xoff=xoff: t.matmul(
                                pt[bk][:, :],
                                s_xf[:, xoff + TC * k + CH * g: xoff + TC * k + CH * (g + 1)],
                                s_wv[:, DV * k + 512 * nuc: DV * k + 512 * (nuc + 1)],
                                start=(k == 0), stop=(k == 7)))
                            for k in range(8)], reads=(xfb[xsl], wvb))
                        kb.op("scalar", lambda s, vsl=vsl, nuc=nuc, bk=bk: s.copy(
                            s_v[:, DV * vsl + 512 * nuc: DV * vsl + 512 * (nuc + 1)],
                            pt[bk][:, :]), reads=(pb[bk],), writes=(vb[vsl],))

                def gla_stage1(it):
                    g, h = divmod(it, HEADS)
                    par = it % 2
                    bA = 4 if par == 0 else 6
                    d0, d1 = 2 * h, 2 * h + 1
                    kb.mm_group(pb[bA], [
                        (lambda t, i=i, dt=dt, g=g, bA=bA: t.matmul(
                            pt[bA][:, 0:128],
                            s_kd[:, TC * dt + CH * g: TC * dt + CH * (g + 1)],
                            s_q[:, TC * dt + CH * g: TC * dt + CH * (g + 1)],
                            start=(i == 0), stop=(i == 1)))
                        for i, dt in enumerate((d0, d1))],
                        reads=(kdb[d0], kdb[d1], qb[d0], qb[d1]))
                    for i, dt in enumerate((d0, d1)):
                        kb.mm_group(pb[bA], [
                            (lambda t, i=i, dt=dt, g=g, bA=bA: t.matmul(
                                pt[bA][:, 128 * (i + 1):128 * (i + 2)],
                                s_kd[:, TC * dt + CH * g: TC * dt + CH * (g + 1)],
                                s_id[:, :],
                                start=True, stop=True))],
                            reads=(kdb[dt], Bg["id"]))
                    kb.op("vector", lambda v, bA=bA, par=par: v.tensor_tensor(
                        s_At[:, 128 * par:128 * (par + 1)], pt[bA][:, 0:128],
                        s_mask[:, :], ALU.mult),
                        reads=(pb[bA], Bg["mask"]), writes=(Atb[par],))
                    kb.op("vector", lambda v, bA=bA, par=par: v.tensor_copy(
                        s_kg[:, 256 * par:256 * (par + 1)], pt[bA][:, 128:384]),
                        reads=(pb[bA],), writes=(kgb[par],))

                def gla_stage2(it):
                    g, h = divmod(it, HEADS)
                    par = it % 2
                    bO = 5 if par == 0 else 0
                    bP = (7, 3) if par == 0 else (1, 2)
                    osl = g % 2
                    d0, d1 = 2 * h, 2 * h + 1
                    vsl = g % 2
                    emits = [
                        (lambda t, vsl=vsl, h=h, par=par, bO=bO: t.matmul(
                            pt[bO][:, :], s_At[:, 128 * par:128 * (par + 1)],
                            s_v[:, DV * vsl + HDV * h: DV * vsl + HDV * (h + 1)],
                            start=True, stop=False))]
                    for i, dt in enumerate((d0, d1)):
                        emits.append(
                            lambda t, i=i, dt=dt, g=g, h=h, bO=bO: t.matmul(
                                pt[bO][:, :],
                                s_q[:, TC * dt + CH * g: TC * dt + CH * (g + 1)],
                                s_S[:, 2 * HDV * h + HDV * i: 2 * HDV * h + HDV * (i + 1)],
                                start=False, stop=(i == 1)))
                    kb.mm_group(pb[bO], emits,
                                reads=(Atb[par], vb[vsl], qb[d0], qb[d1], Sb[h]))
                    for i, dt in enumerate((d0, d1)):
                        kb.mm_group(pb[bP[i]], [
                            (lambda t, i=i, vsl=vsl, h=h, par=par, bk=bP[i]: t.matmul(
                                pt[bk][:, :], s_kg[:, 256 * par + 128 * i:256 * par + 128 * (i + 1)],
                                s_v[:, DV * vsl + HDV * h: DV * vsl + HDV * (h + 1)],
                                start=True, stop=True))],
                            reads=(kgb[par], vb[vsl]))
                    kb.op("scalar", lambda s, g=g, h=h, bO=bO: s.activation(
                        s_dumpA[:, 0:512], pt[bO][:, :], AF.Square,
                        accum_out=s_ss[:, HEADS * g + h:HEADS * g + h + 1]),
                        reads=(pb[bO],), writes=(ssb, dumpbuf()))
                    kb.op("scalar", lambda s, h=h, bO=bO: s.copy(
                        s_o[:, HDV * h: HDV * (h + 1)],
                        pt[bO][:, :]), reads=(pb[bO],), writes=(ob[0],))
                    for i in range(2):
                        kb.op("vector", lambda v, h=h, i=i, bk=bP[i]:
                              v.tensor_tensor(
                                  s_S[:, 2 * HDV * h + HDV * i: 2 * HDV * h + HDV * (i + 1)],
                                  s_S[:, 2 * HDV * h + HDV * i: 2 * HDV * h + HDV * (i + 1)],
                                  pt[bk][:, :], ALU.add),
                              reads=(pb[bP[i]], Sb[h]), writes=(Sb[h],))

                def o_spill(g):
                    osl = g % 2
                    kb.dma("sync", f"sp_o{osl}",
                           d_o.ap()[t0 + CH * g: t0 + CH * (g + 1), :],
                           s_o[:, :], reads=(ob[0],),
                           writes=(Bg[f"do{osl}"],))

                NIT = NSEG * HEADS
                for it in range(NIT):
                    if it % HEADS == 0:
                        v_proj(it // HEADS)
                    gla_stage1(it)
                    if wunits:
                        wunits.pop(0)()
                    if it > 0:
                        gla_stage2(it - 1)
                        if (it - 1) % HEADS == HEADS - 1:
                            o_spill((it - 1) // HEADS)
                gla_stage2(NIT - 1)
                o_spill(NSEG - 1)
                kb.dma("sync", "sp_ss",
                       d_ss.ap()[NSEG * c8:NSEG * (c8 + 1), :, :].rearrange(
                           "g p h -> p g h"),
                       s_ss[:, :], reads=(ssb,), writes=(Bg["dss"],))

            while wunits:
                wunits.pop(0)()
            kb.fence()

        with contextlib.ExitStack() as esB:
            SBp = lambda n, s, dt=F32: esB.enter_context(nc.sbuf_tensor(n, s, dt))
            s_rw = SBp("s_rw", [128, DV], BF16)
            s_oi = SBp("s_oi", [128, 2 * NSEG * DV], BF16)
            s_ssi = SBp("s_ssi", [128, 2 * NSEG * HEADS])
            s_wo = SBp("s_wo", [128, 16 * D], BF16)
            s_rst = SBp("s_rst", [128, NSEG * HEADS])
            s_sil = SBp("s_sil", [128, 4 * 512], BF16)
            s_gg = SBp("s_gg", [128, NSEG * DV], BF16)
            s_oop2 = SBp("s_oop2", [128, 2 * DV], BF16)
            s_oopT = SBp("s_oopT", [128, 2 * 16 * 128], BF16)
            s_xt = SBp("s_xt", [128, NSEG * D])
            s_y1 = SBp("s_y1", [128, 2 * NSEG * D])
            s_dumpB = SBp("s_dumpB", [128, 1024], BF16)
            s_lnc = SBp("s_lnc", [128, 2 * NSEG * 5])

            wob, rwb = mkbuf(), mkbuf()
            oib = [mkbuf(), mkbuf()]
            ssib = [mkbuf(), mkbuf()]
            rstb = mkbuf()
            silb = [mkbuf(), mkbuf(), mkbuf(), mkbuf()]
            ggb = [mkbuf(f"gg{g}") for g in range(NSEG)]
            oopb = [mkbuf(), mkbuf()]
            oopTb = [mkbuf(), mkbuf()]
            xtb = mkbuf("xt")
            y1b = [[mkbuf(f"y1{s}{g}") for g in range(NSEG)] for s in range(2)]
            lncb = mkbuf()

            kb.dma("gpsimd", "wB3",
                   s_wo[:, :].rearrange("p (k t) -> p k t", t=D),
                   wo.rearrange("(k p) t -> p k t", p=128), writes=(wob,))
            kb.dma("gpsimd", "wB2", s_rw[:, :], rmsw, writes=(rwb,))



            def bload(c8):
                t0 = c8 * TC
                xsl = c8 % 2
                kb.dma("sync", f"oB{xsl}",
                       s_oi[:, xsl * NSEG * DV:(xsl + 1) * NSEG * DV].rearrange(
                           "p (g v) -> p g v", v=DV),
                       d_o.ap()[t0:t0 + TC, :].rearrange("(g p) v -> p g v", p=128),
                       reads=(Bg["do0"], Bg["do1"]), writes=(oib[xsl],))
                nsh = NSEG * HEADS
                kb.dma("sync", f"oB{xsl}", s_ssi[:, xsl * nsh:(xsl + 1) * nsh],
                       d_ss.ap()[NSEG * c8:NSEG * (c8 + 1), :, :].rearrange(
                           "g p h -> p g h"),
                       reads=(Bg["dss"],), writes=(ssib[xsl],))

            bload(0)
            for c8 in range(NTC):
                t0 = c8 * TC
                xsl = c8 % 2
                xoffB = xsl * 8 * TC
                ooffB = xsl * NSEG * DV
                if c8 + 1 < NTC:
                    bload(c8 + 1)
                    xload(c8 + 1)
                kb.dma("sync", f"xtB{xsl}",
                       s_xt[:, :].rearrange("p (g v) -> p g v", v=D),
                       xt[t0:t0 + TC, :].rearrange("(g p) v -> p g v", p=128),
                       writes=(xtb,))
                nsh = NSEG * HEADS
                kb.op("vector", lambda v, xsl=xsl, nsh=nsh: v.tensor_scalar(
                    s_rst[:, :], s_ssi[:, xsl * nsh:(xsl + 1) * nsh],
                    1.0 / HDV, EPS, ALU.mult, ALU.add),
                    reads=(ssib[xsl],), writes=(rstb,))
                kb.op("scalar", lambda s: s.activation(
                    s_rst[:, :], s_rst[:, :], AF.Sqrt), reads=(rstb,), writes=(rstb,))
                kb.op("vector", lambda v: v.reciprocal(s_rst[:, :], s_rst[:, :]),
                      reads=(rstb,), writes=(rstb,))
                for g in range(NSEG):
                    for nuc in range(4):
                        bk = nuc % 2
                        ssl = nuc
                        kb.mm_group(pb[bk], [
                            (lambda t, k=k, bk=bk, g=g, nuc=nuc, xoffB=xoffB: t.matmul(
                                pt[bk][:, :],
                                s_xf[:, xoffB + TC * k + CH * g: xoffB + TC * k + CH * (g + 1)],
                                s_wg[:, DV * k + 512 * nuc: DV * k + 512 * (nuc + 1)],
                                start=(k == 0), stop=(k == 7)))
                            for k in range(8)], reads=(xfb[xsl], Bg["wgw"]))
                        kb.op("scalar", lambda s, bk=bk, ssl=ssl: s.activation(
                            s_sil[:, 512 * ssl:512 * (ssl + 1)], pt[bk][:, :], AF.Silu),
                            reads=(pb[bk],), writes=(silb[ssl],))
                        kb.op("vector", lambda v, g=g, nuc=nuc, ssl=ssl: v.tensor_tensor(
                            s_gg[:, DV * g + 512 * nuc:DV * g + 512 * (nuc + 1)],
                            s_sil[:, 512 * ssl:512 * (ssl + 1)],
                            s_rw[:, 512 * nuc:512 * (nuc + 1)], ALU.mult),
                            reads=(silb[ssl], rwb), writes=(ggb[g],))
                def b_oop(g):
                    for h in range(HEADS):
                        col = HEADS * g + h
                        kb.op("vector", lambda v, g=g, h=h, col=col,
                              ooffB=ooffB, xsl=xsl:
                              v.scalar_tensor_tensor(
                                  s_oop2[:, DV * (g % 2) + HDV * h: DV * (g % 2) + HDV * (h + 1)],
                                  s_oi[:, ooffB + DV * g + HDV * h: ooffB + DV * g + HDV * (h + 1)],
                                  s_rst[:, col:col + 1],
                                  s_gg[:, DV * g + HDV * h:DV * g + HDV * (h + 1)],
                                  ALU.mult, ALU.mult),
                              reads=(oib[xsl], rstb, ggb[g]), writes=(oopb[g % 2],))

                def b_tr(g):
                    osl = g % 2
                    for jb in range(4):
                        bk = 2 + jb % 2
                        for j4 in range(4):
                            j = 4 * jb + j4
                            kb.mm(pb[bk], pt[bk][:, 128 * j4:128 * (j4 + 1)],
                                  s_oop2[:, DV * osl + 128 * j: DV * osl + 128 * (j + 1)],
                                  s_id[:, :],
                                  reads=(oopb[osl], Bg["id"]), start=True, stop=True)
                        kb.op("vector", lambda v, jb=jb, bk=bk, osl=osl: v.tensor_copy(
                            s_oopT[:, 2048 * osl + 512 * jb:2048 * osl + 512 * (jb + 1)],
                            pt[bk][:, :]),
                            reads=(pb[bk],), writes=(oopTb[osl],))

                def b_out(g):
                    osl = g % 2
                    for dc in range(2):
                        bk = 4 + dc
                        kb.mm_group(pb[bk], [
                            (lambda t, k=k, bk=bk, dc=dc, osl=osl: t.matmul(
                                pt[bk][:, :],
                                s_oopT[:, 2048 * osl + 128 * k:2048 * osl + 128 * (k + 1)],
                                s_wo[:, D * k + 512 * dc: D * k + 512 * (dc + 1)],
                                start=(k == 0), stop=(k == 15)))
                            for k in range(16)], reads=(oopTb[osl], wob))
                        kb.op("vector", lambda v, g=g, dc=dc, bk=bk, xsl=xsl:
                              v.tensor_tensor(
                            s_y1[:, xsl * NSEG * D + D * g + 512 * dc:
                                 xsl * NSEG * D + D * g + 512 * (dc + 1)],
                            pt[bk][:, :],
                            s_xt[:, D * g + 512 * dc: D * g + 512 * (dc + 1)],
                            ALU.add), reads=(pb[bk], xtb), writes=(y1b[xsl][g],))
                    ysl = s_y1[:, xsl * NSEG * D + D * g:xsl * NSEG * D + D * (g + 1)]
                    c0 = 20 * xsl + 4 * g
                    kb.op("scalar", lambda s, ysl=ysl, c0=c0: s.activation(
                        s_dumpB[:, :], ysl, AF.Identity,
                        accum_out=s_lnc[:, c0:c0 + 1]),
                        reads=(y1b[xsl][g],), writes=(dumpbuf(), lncb))
                    kb.op("scalar", lambda s, ysl=ysl, c0=c0: s.activation(
                        s_dumpB[:, :], ysl, AF.Square,
                        accum_out=s_lnc[:, c0 + 1:c0 + 2]),
                        reads=(y1b[xsl][g],), writes=(dumpbuf(), lncb))

                def b_ln_a(c8):
                    t0 = c8 * TC
                    xsl = c8 % 2
                    yo = xsl * NSEG * D
                    lb = 20 * xsl
                    mu = s_lnc[:, lb + 0:lb + 16:4]
                    ss2 = s_lnc[:, lb + 1:lb + 16:4]
                    var = s_lnc[:, lb + 2:lb + 16:4]
                    rst = s_lnc[:, lb + 3:lb + 16:4]
                    kb.op("vector", lambda v: v.tensor_scalar(
                        mu, mu, 1.0 / D, None, ALU.mult), reads=(lncb,), writes=(lncb,))
                    kb.op("vector", lambda v: v.tensor_tensor(
                        var, mu, mu, ALU.mult), reads=(lncb,), writes=(lncb,))
                    kb.op("vector", lambda v: v.tensor_scalar(
                        ss2, ss2, 1.0 / D, EPS, ALU.mult, ALU.add),
                        reads=(lncb,), writes=(lncb,))
                    kb.op("vector", lambda v: v.tensor_tensor(
                        var, ss2, var, ALU.subtract), reads=(lncb,), writes=(lncb,))
                    kb.op("scalar", lambda s: s.activation(
                        rst, var, AF.Sqrt), reads=(lncb,), writes=(lncb,))
                    kb.op("vector", lambda v: v.reciprocal(rst, rst),
                          reads=(lncb,), writes=(lncb,))
                    kb.op("vector", lambda v, mu=mu, rst=rst, lb=lb:
                          v.scalar_tensor_tensor(
                        s_lnc[:, lb + 16:lb + 20], mu, -1.0, rst, ALU.mult, ALU.mult),
                        reads=(lncb,), writes=(lncb,))
                    for g in range(NSEG):
                        ysl = s_y1[:, yo + D * g:yo + D * (g + 1)]
                        c0 = lb + 4 * g
                        kb.op("scalar", lambda s, ysl=ysl, c0=c0, g=g, lb=lb, xsl=xsl:
                              s.activation(
                            ysl, ysl, AF.Identity, scale=s_lnc[:, c0 + 3:c0 + 4],
                            bias=s_lnc[:, lb + 16 + g:lb + 17 + g]),
                            reads=(y1b[xsl][g], lncb), writes=(y1b[xsl][g],))
                        kb.dma("sync", f"spy{xsl}",
                               d_y1.ap()[t0 + CH * g:t0 + CH * (g + 1), :],
                               ysl, reads=(y1b[xsl][g],), writes=(Bg[f"dy1{xsl}"],))

                def b_ln_b(c8):
                    xsl = c8 % 2
                    yo = xsl * NSEG * D
                    for g in range(NSEG):
                        ysl = s_y1[:, yo + D * g:yo + D * (g + 1)]
                        kb.op("vector", lambda v, ysl=ysl, col=NSEG * c8 + g, g=g,
                              xsl=xsl: v.reduce_max(
                            s_max1[:, col:col + 1], ysl, AX.X, apply_absolute_value=True),
                            reads=(y1b[xsl][g],), writes=(Bg["max1"],))

                b_oop(0)
                b_tr(0)
                b_oop(1)
                b_tr(1)
                b_out(0)
                if c8 > 0:
                    b_ln_a(c8 - 1)
                b_oop(2)
                b_tr(2)
                b_out(1)
                b_oop(3)
                if c8 > 0:
                    b_ln_b(c8 - 1)
                b_tr(3)
                b_out(2)
                b_out(3)
                if c8 == NTC - 1:
                    b_ln_a(c8)
                    b_ln_b(c8)
            kb.fence()

        esAB.close()
        def max_collective(s_maxt, ncols, cci, cco, s_scn, scn_buf, maxt_buf,
                           cc_in_b, cc_out_b, sig_col, tny, dscv, dscv_b):
            kb.op("vector", lambda v: v.reduce_max(
                s_maxt[:, ncols:ncols + 1], s_maxt[:, 0:ncols], AX.X),
                reads=(maxt_buf,), writes=(maxt_buf,))
            kb.dma("gpsimd", tny + "g", d_mx.ap(), s_maxt[:, ncols:ncols + 1],
                   reads=(maxt_buf,), writes=(Bg["dmx"],))
            kb.dma("gpsimd", tny + "g", s_mrow[0:1, :], d_mx.ap()[None, :],
                   reads=(Bg["dmx"],), writes=(Bg["mrow"],))
            kb.op("vector", lambda v: v.reduce_max(
                s_cc8[0:1, 0:1], s_mrow[0:1, :], AX.X),
                reads=(Bg["mrow"],), writes=(Bg["cc8"],))
            kb.dma("gpsimd", tny + "g", cci.ap(), s_cc8[0:1, :],
                   reads=(Bg["cc8"],), writes=(cc_in_b,))
            kb._pre("gpsimd", (cc_in_b,), (cc_out_b,))
            kb._post("gpsimd", lambda g: g.collective_compute(
                "AllReduce", ALU.max, replica_groups=[list(range(ncore))],
                ins=[cci.ap()], outs=[cco.ap()]),
                "cc_done", 1, (cc_in_b,), (cc_out_b,))
            kb.dma("gpsimd", tny + "g", s_scn[0:1, 0:1], cco.ap()[None, 0:1],
                   reads=(cc_out_b,), writes=(scn_buf,))
            kb.op("vector", lambda v: v.tensor_tensor(
                s_scn[0:1, 1:2], s_scn[0:1, 0:1], s_sig[0:1, sig_col:sig_col + 1],
                ALU.mult), reads=(scn_buf, Bg["sig"]), writes=(scn_buf,))
            kb.op("vector", lambda v: v.tensor_scalar(
                s_scn[0:1, 1:2], s_scn[0:1, 1:2], 1.0 / 127.0, None, ALU.mult),
                reads=(scn_buf,), writes=(scn_buf,))
            kb.op("vector", lambda v: v.reciprocal(
                s_scn[0:1, 2:3], s_scn[0:1, 1:2]), reads=(scn_buf,), writes=(scn_buf,))
            kb.op("vector", lambda v: v.tensor_tensor(
                s_scn[0:1, 3:4], s_scn[0:1, 1:2], s_scn[0:1, 2:3], ALU.mult),
                reads=(scn_buf,), writes=(scn_buf,))
            kb.op("vector", lambda v: v.tensor_scalar(
                s_scn[0:1, 3:4], s_scn[0:1, 3:4], 2.0, -1.0,
                ALU.subtract, ALU.mult), reads=(scn_buf,), writes=(scn_buf,))
            kb.op("vector", lambda v: v.tensor_tensor(
                s_scn[0:1, 2:3], s_scn[0:1, 2:3], s_scn[0:1, 3:4], ALU.mult),
                reads=(scn_buf,), writes=(scn_buf,))
            kb.dma("gpsimd", tny + "g", dscv.ap()[0:4], s_scn[0:1, 0:4],
                   reads=(scn_buf,), writes=(dscv_b,))

        sc1b, sc2b = Bg["sc1"], Bg["sc2"]
        max_collective(s_max1, NSEG * NTC, cc1i, cc1o, s_sc1, sc1b, Bg["max1"],
                       Bg["cc1i"], Bg["cc1o"], 0, "tnyA", d_sc1v, Bg["dsc1v"])
        colb = Bg["col"]
        kb.dma("gpsimd", "tnyAg", s_col[:, 0:1],
               bass.AP(d_sc1v, 1, [[0, 128], [1, 1]]),
               reads=(Bg["dsc1v"],), writes=(colb,))
        kb.dma("gpsimd", "tnyAg", s_col[:, 1:2],
               bass.AP(d_sc1v, 2, [[0, 128], [1, 1]]),
               reads=(Bg["dsc1v"],), writes=(colb,))
        kb.op("vector", lambda v: v.tensor_scalar(
            s_gcs[:, 0:22], s_gc[:, 0:22], s_col[:, 0:1], None, ALU.mult),
            reads=(Bg["gc"], colb), writes=(Bg["gcs"],))
        kb.op("vector", lambda v: v.tensor_scalar(
            s_ucs[:, 0:22], s_uc[:, 0:22], s_col[:, 0:1], None, ALU.mult),
            reads=(Bg["uc"], colb), writes=(Bg["ucs"],))

        dy1Tb = [mkbuf("dy1T0"), mkbuf("dy1T1")]
        with contextlib.ExitStack() as esT:
            ST = lambda n, s, dt=F32: esT.enter_context(nc.sbuf_tensor(n, s, dt))
            s_ty = ST("s_ty", [128, 2 * NSEG * D])
            s_tyT = ST("s_tyT", [128, 2 * NSEG * D])
            tyb = [mkbuf(), mkbuf()]
            tyTb = [mkbuf(), mkbuf()]

            def tld(c8):
                xsl = c8 % 2
                t0 = c8 * TC
                kb.dma("sync", f"ty{xsl}",
                       s_ty[:, xsl * NSEG * D:(xsl + 1) * NSEG * D].rearrange(
                           "p (g v) -> p g v", v=D),
                       d_y1.ap()[t0:t0 + TC, :].rearrange("(g p) v -> p g v", p=128),
                       reads=(Bg["dy10"], Bg["dy11"]), writes=(tyb[xsl],))

            tld(0)
            tld(1)
            for c8 in range(NTC):
                xsl = c8 % 2
                yo = xsl * NSEG * D
                t0 = c8 * TC
                for db in range(8):
                    bk = db % 4
                    for g in range(NSEG):
                        kb.op("tensor", lambda t, bk=bk, g=g, db=db, yo=yo:
                              t.transpose(
                                  pt[bk][:, 128 * g:128 * (g + 1)],
                                  s_ty[:, yo + D * g + 128 * db:
                                       yo + D * g + 128 * (db + 1)],
                                  s_idf[:, :]),
                              reads=(tyb[xsl], idfb), writes=(pb[bk],), accum=True)
                    kb.op("scalar", lambda s, bk=bk, db=db, yo=yo: s.copy(
                        s_tyT[:, yo + 512 * db:yo + 512 * (db + 1)],
                        pt[bk][:, 0:512]), reads=(pb[bk],), writes=(tyTb[xsl],))
                kb.dma("sync", f"tyo{xsl}",
                       d_y1T.ap()[:, t0:t0 + TC].rearrange(
                           "(db p) t -> p db t", p=128),
                       s_tyT[:, yo:yo + NSEG * D].rearrange(
                           "p (db t) -> p db t", t=TC),
                       reads=(tyTb[xsl],), writes=(dy1Tb[xsl],))
                if c8 + 2 < NTC:
                    tld(c8 + 2)
        kb.final_wait("sync", ["tyo0", "tyo1"])
        kb.wait("sync", "e_tensor", kb.cnt["e_tensor"])
        kb.wait("sync", "e_scalar", kb.cnt["e_scalar"])

        with contextlib.ExitStack() as esC:
            SC = lambda n, s, dt=F32: esC.enter_context(nc.sbuf_tensor(n, s, dt))
            s_wgq = SC("s_wgq", [128, 8 * II], BF16)
            s_wuq = SC("s_wuq", [128, 8 * II], BF16)
            s_y1C = SC("s_y1C", [128, 2 * 8 * TC])
            s_t1 = SC("s_t1", [128, 2 * 2 * TC])
            s_xqT = SC("s_xqT", [128, 2 * 8 * TC], BF16)
            s_silC = SC("s_silC", [128, 2 * TC])
            s_hC = SC("s_hC", [128, 2 * TC])

            wgqb, wuqb = mkbuf(), mkbuf()
            y1Cb = [mkbuf(), mkbuf()]
            xqTb = [mkbuf(), mkbuf()]
            t1b = [mkbuf(), mkbuf()]
            silCb = [mkbuf(), mkbuf()]
            hCb = [mkbuf(), mkbuf()]

            for k in range(8):
                kb.dma("sync", "wCg", s_wgq[:, II * k:II * (k + 1)].rearrange(
                    "p (it c) -> p it c", c=128),
                    g_wgq.ap()[k].rearrange("it p c -> p it c"),
                    reads=(Bg["gwg"],), writes=(wgqb,))
                kb.dma("sync", "wCu", s_wuq[:, II * k:II * (k + 1)].rearrange(
                    "p (it c) -> p it c", c=128),
                    g_wuq.ap()[k].rearrange("it p c -> p it c"),
                    reads=(Bg["gwu"],), writes=(wuqb,))

            def c_quant_units(c8):
                t0 = c8 * TC
                xsl = c8 % 2
                qoff = xsl * 8 * TC
                units = []

                def u_load():
                    kb.dma("sync", f"yC{xsl}",
                           s_y1C[:, qoff:qoff + 8 * TC].rearrange(
                               "p (k t) -> p k t", t=TC),
                           d_y1T.ap()[:, t0:t0 + TC].rearrange(
                               "(k p) t -> p k t", p=128),
                           reads=tuple(dy1Tb), writes=(y1Cb[xsl],))
                units.append(u_load)
                for u in range(4):
                    def u_q(u=u, xsl=xsl, qoff=qoff):
                        usl = u % 2
                        uo = 2 * TC * usl
                        ysl = s_y1C[:, qoff + 2 * TC * u:qoff + 2 * TC * (u + 1)]
                        kb.op("vector", lambda v, ysl=ysl, uo=uo: v.tensor_scalar(
                            s_t1[:, uo:uo + 2 * TC], ysl,
                            s_col[:, 1:2], 127.0, ALU.mult, ALU.min),
                            reads=(y1Cb[xsl], colb), writes=(t1b[usl],))
                        kb.op("vector", lambda v, uo=uo: v.tensor_scalar(
                            s_t1[:, uo:uo + 2 * TC], s_t1[:, uo:uo + 2 * TC],
                            -128.0, MAGIC,
                            ALU.max, ALU.add), reads=(t1b[usl],), writes=(t1b[usl],))
                        kb.op("scalar", lambda s, u=u, uo=uo, qoff=qoff: s.activation(
                            s_xqT[:, qoff + 2 * TC * u:qoff + 2 * TC * (u + 1)],
                            s_t1[:, uo:uo + 2 * TC],
                            AF.Identity, bias=s_col[:, 3:4]),
                            reads=(t1b[usl], colb), writes=(xqTb[xsl],))
                    units.append(u_q)
                return units

            for u in c_quant_units(0):
                u()
            for c8 in range(NTC):
                t0 = c8 * TC
                xsl = c8 % 2
                qoff = xsl * 8 * TC
                nxt = c_quant_units(c8 + 1) if c8 + 1 < NTC else []
                for j in range(22):
                    bkg = j % 2
                    ssl = j % 2
                    kb.mm_group(pb[bkg], [
                        (lambda t, k=k, bkg=bkg, j=j, qoff=qoff: t.matmul(
                            pt[bkg][:, :],
                            s_wgq[:, II * k + 128 * j: II * k + 128 * (j + 1)],
                            s_xqT[:, qoff + TC * k:qoff + TC * (k + 1)],
                            start=(k == 0), stop=(k == 7)))
                        for k in range(8)], reads=(wgqb, xqTb[xsl]))
                    kb.op("scalar", lambda s, j=j, bkg=bkg, ssl=ssl: s.activation(
                        s_silC[:, TC * ssl:TC * (ssl + 1)], pt[bkg][:, :], AF.Silu,
                        scale=s_gcs[:, j:j + 1]),
                        reads=(pb[bkg], Bg["gcs"]), writes=(silCb[ssl],))
                    bku = 4 + j % 2
                    kb.mm_group(pb[bku], [
                        (lambda t, k=k, bku=bku, j=j, qoff=qoff: t.matmul(
                            pt[bku][:, :],
                            s_wuq[:, II * k + 128 * j: II * k + 128 * (j + 1)],
                            s_xqT[:, qoff + TC * k:qoff + TC * (k + 1)],
                            start=(k == 0), stop=(k == 7)))
                        for k in range(8)], reads=(wuqb, xqTb[xsl]))
                    kb.op("vector", lambda v, j=j, bku=bku, ssl=ssl:
                          v.scalar_tensor_tensor(
                              s_hC[:, TC * ssl:TC * (ssl + 1)], pt[bku][:, :],
                              s_ucs[:, j:j + 1], s_silC[:, TC * ssl:TC * (ssl + 1)],
                              ALU.mult, ALU.mult),
                          reads=(pb[bku], Bg["ucs"], silCb[ssl]), writes=(hCb[ssl],))
                    kb.op("vector", lambda v, ssl=ssl, col=22 * c8 + j: v.reduce_max(
                        s_max2[:, col:col + 1], s_hC[:, TC * ssl:TC * (ssl + 1)],
                        AX.X, apply_absolute_value=True),
                        reads=(hCb[ssl],), writes=(Bg["max2"],))
                    kb.dma("sync", f"sph{ssl}",
                           d_h.ap()[128 * j:128 * (j + 1), t0:t0 + TC],
                           s_hC[:, TC * ssl:TC * (ssl + 1)],
                           reads=(hCb[ssl],), writes=(Bg[f"dh{ssl}"],))
                    if nxt and j % 2 == 1:
                        nxt.pop(0)()
                while nxt:
                    nxt.pop(0)()
            kb.fence()

        max_collective(s_max2, 22 * NTC, cc2i, cc2o, s_sc2, sc2b, Bg["max2"],
                       Bg["cc2i"], Bg["cc2o"], 1, "tnyB", d_sc2v, Bg["dsc2v"])
        with contextlib.ExitStack() as esD:
            SD = lambda n, s, dt=F32: esD.enter_context(nc.sbuf_tensor(n, s, dt))
            s_wdq = SD("s_wdq", [128, 22 * D], BF16)
            s_sdxr = SD("s_sdxr", [1, D])
            s_sdx = SD("s_sdx", [128, D])

            kb.dma("sync", "tnyB", s_col[:, 2:3],
                   bass.AP(d_sc2v, 2, [[0, 128], [1, 1]]),
                   reads=(Bg["dsc2v"],), writes=(colb,))
            kb.dma("sync", "tnyC", s_sdxr[0:1, :], d_sd.ap()[None, :],
                   reads=(Bg["dsd"],), writes=(Bg["sdxr"],))
            kb.op("vector", lambda v: v.tensor_scalar(
                s_sdxr[0:1, :], s_sdxr[0:1, :], s_sc2[0:1, 1:2], None, ALU.mult),
                reads=(Bg["sdxr"], sc2b), writes=(Bg["sdxr"],))
            kb.dma("sync", "tnyC", d_sdxv.ap(), s_sdxr[0:1, :],
                   reads=(Bg["sdxr"],), writes=(Bg["dsdxv"],))
            kb.dma("sync", "tnyC", s_sdx[:, :],
                   bass.AP(d_sdxv, 0, [[0, 128], [1, D]]),
                   reads=(Bg["dsdxv"],), writes=(Bg["sdx"],))


            s_hin = SD("s_hin", [128, 4 * TC])
            s_t1D = SD("s_t1D", [128, 4 * TC])
            s_hq = SD("s_hq", [128, 2 * 22 * TC], BF16)
            s_y1D = SD("s_y1D", [128, 2 * NSEG * D])
            s_f = SD("s_f", [128, 512])
            s_y2 = SD("s_y2", [128, NSEG * D])
            s_lncD = SD("s_lncD", [128, NSEG * 5])
            s_dumpD = SD("s_dumpD", [128, 1024], BF16)

            wdqb = mkbuf()
            hinb = [mkbuf(), mkbuf(), mkbuf(), mkbuf()]
            t1Db = [mkbuf(), mkbuf(), mkbuf(), mkbuf()]
            hqb = [[mkbuf(f"hq{s}_{j}") for j in range(22)] for s in range(2)]
            y1Db = [mkbuf(), mkbuf()]
            fb, lncDb = mkbuf(), mkbuf()
            y2b = [mkbuf(f"y2{g}") for g in range(NSEG)]

            for k in range(22):
                kb.dma("sync", "wD", s_wdq[:, D * k:D * (k + 1)].rearrange(
                    "p (j c) -> p j c", c=128),
                    g_wdq.ap()[:, k, :, :].rearrange("j p c -> p j c"),
                    reads=(Bg["gwd"],), writes=(wdqb,))

            def d_quant_units(c8):
                t0 = c8 * TC
                xsl = c8 % 2
                yoff = xsl * NSEG * D
                hoff = xsl * 22 * TC
                units = []
                def uy(xsl=xsl, yoff=yoff, t0=t0):
                    kb.dma("sync", f"yD{xsl}",
                           s_y1D[:, yoff:yoff + NSEG * D].rearrange("p (g v) -> p g v", v=D),
                           d_y1.ap()[t0:t0 + TC, :].rearrange("(g p) v -> p g v", p=128),
                           reads=(Bg["dy10"], Bg["dy11"]), writes=(y1Db[xsl],))
                units.append(uy)
                for j in range(22):
                    def uq(j=j, xsl=xsl, hoff=hoff, t0=t0):
                        sl = j % 4
                        kb.dma("sync", f"hD{sl}", s_hin[:, TC * sl:TC * (sl + 1)],
                               d_h.ap()[128 * j:128 * (j + 1), t0:t0 + TC],
                               reads=(Bg["dh0"], Bg["dh1"]), writes=(hinb[sl],))
                        kb.op("vector", lambda v, sl=sl: v.tensor_scalar(
                            s_t1D[:, TC * sl:TC * (sl + 1)], s_hin[:, TC * sl:TC * (sl + 1)],
                            s_col[:, 2:3], 127.0, ALU.mult, ALU.min),
                            reads=(hinb[sl], colb), writes=(t1Db[sl],))
                        kb.op("vector", lambda v, sl=sl: v.tensor_scalar(
                            s_t1D[:, TC * sl:TC * (sl + 1)], s_t1D[:, TC * sl:TC * (sl + 1)],
                            -128.0, MAGIC, ALU.max, ALU.add),
                            reads=(t1Db[sl],), writes=(t1Db[sl],))
                        kb.op("scalar", lambda s, j=j, sl=sl, hoff=hoff: s.activation(
                            s_hq[:, hoff + TC * j:hoff + TC * (j + 1)],
                            s_t1D[:, TC * sl:TC * (sl + 1)],
                            AF.Identity, bias=s_col[:, 3:4]),
                            reads=(t1Db[sl], colb), writes=(hqb[xsl][j],))
                    units.append(uq)
                return units

            def d_mm_epi(c8, g):
                t0 = c8 * TC
                xsl = c8 % 2
                yoff = xsl * NSEG * D
                hoff = xsl * 22 * TC
                for dc in range(2):
                    bk = 4 + dc + 2 * (g % 2)
                    kb.mm_group(pb[bk], [
                        (lambda t, j=j, bk=bk, g=g, dc=dc, hoff=hoff: t.matmul(
                            pt[bk][:, :],
                            s_hq[:, hoff + TC * j + CH * g: hoff + TC * j + CH * (g + 1)],
                            s_wdq[:, D * j + 512 * dc: D * j + 512 * (dc + 1)],
                            start=(j == 0), stop=(j == 21)))
                        for j in range(22)], reads=tuple(hqb[xsl]) + (wdqb,))
                    kb.op("vector", lambda v, dc=dc, bk=bk: v.tensor_tensor(
                        s_f[:, :], pt[bk][:, :], s_sdx[:, 512 * dc:512 * (dc + 1)],
                        ALU.mult), reads=(pb[bk], Bg["sdx"]), writes=(fb,))
                    kb.op("vector", lambda v, g=g, dc=dc, yoff=yoff: v.tensor_tensor(
                        s_y2[:, D * g + 512 * dc: D * g + 512 * (dc + 1)],
                        s_f[:, :],
                        s_y1D[:, yoff + D * g + 512 * dc: yoff + D * g + 512 * (dc + 1)],
                        ALU.add), reads=(fb, y1Db[xsl]), writes=(y2b[g],))
                ysl = s_y2[:, D * g:D * (g + 1)]
                c0 = 4 * g
                kb.op("scalar", lambda s, ysl=ysl, c0=c0: s.activation(
                    s_dumpD[:, :], ysl, AF.Identity,
                    accum_out=s_lncD[:, c0:c0 + 1]),
                    reads=(y2b[g],), writes=(dumpbuf(), lncDb))
                kb.op("scalar", lambda s, ysl=ysl, c0=c0: s.activation(
                    s_dumpD[:, :], ysl, AF.Square,
                    accum_out=s_lncD[:, c0 + 1:c0 + 2]),
                    reads=(y2b[g],), writes=(dumpbuf(), lncDb))

            def d_ln_finish(c8):
                t0 = c8 * TC
                mu = s_lncD[:, 0:16:4]
                ss2 = s_lncD[:, 1:16:4]
                var = s_lncD[:, 2:16:4]
                rst = s_lncD[:, 3:16:4]
                kb.op("vector", lambda v: v.tensor_scalar(
                    mu, mu, 1.0 / D, None, ALU.mult), reads=(lncDb,), writes=(lncDb,))
                kb.op("vector", lambda v: v.tensor_tensor(
                    var, mu, mu, ALU.mult), reads=(lncDb,), writes=(lncDb,))
                kb.op("vector", lambda v: v.tensor_scalar(
                    ss2, ss2, 1.0 / D, EPS, ALU.mult, ALU.add),
                    reads=(lncDb,), writes=(lncDb,))
                kb.op("vector", lambda v: v.tensor_tensor(
                    var, ss2, var, ALU.subtract), reads=(lncDb,), writes=(lncDb,))
                kb.op("scalar", lambda s: s.activation(
                    rst, var, AF.Sqrt), reads=(lncDb,), writes=(lncDb,))
                kb.op("vector", lambda v: v.reciprocal(rst, rst),
                      reads=(lncDb,), writes=(lncDb,))
                kb.op("vector", lambda v, mu=mu, rst=rst: v.scalar_tensor_tensor(
                    s_lncD[:, 16:20], mu, -1.0, rst, ALU.mult, ALU.mult),
                    reads=(lncDb,), writes=(lncDb,))
                for g in range(NSEG):
                    ysl = s_y2[:, D * g:D * (g + 1)]
                    c0 = 4 * g
                    kb.op("scalar", lambda s, ysl=ysl, c0=c0, g=g: s.activation(
                        ysl, ysl, AF.Identity, scale=s_lncD[:, c0 + 3:c0 + 4],
                        bias=s_lncD[:, 16 + g:17 + g]),
                        reads=(y2b[g], lncDb), writes=(y2b[g],))
                    kb.dma("sync", "st_out",
                           out[t0 + CH * g:t0 + CH * (g + 1), :], ysl,
                           reads=(y2b[g],), writes=(Bg["dout"],))

            pend = d_quant_units(0)
            for u in pend:
                u()
            for c8 in range(NTC):
                nxt = d_quant_units(c8 + 1) if c8 + 1 < NTC else []
                while nxt:
                    nxt.pop(0)()
                for g in range(NSEG):
                    d_mm_epi(c8, g)
                d_ln_finish(c8)
            kb.final_wait("sync", ["st_out"])
            kb.fence(("sync",))
        kb.finish()
    return nc




def _in_maps(inputs, T=4096, ncore=NCORE):
    import ml_dtypes
    f32 = np.float32
    x = np.asarray(inputs["L0_attn_x"], f32)[:, :T, :]
    g = lambda n: np.ascontiguousarray(np.asarray(inputs[n], f32))
    wq, wk = g("L0_attn_Wq"), g("L0_attn_Wk")
    wv, wgm = g("L0_attn_Wv"), g("L0_attn_Wg")
    wgk1, wgk2 = g("L0_attn_Wgk1"), g("L0_attn_Wgk2")
    bgk2, wo = g("L0_attn_bgk2"), g("L0_attn_Wo")
    rms_w = g("L0_attn_rms_w")
    w_gate, w_up, w_down = g("L0_ffn_w_gate"), g("L0_ffn_w_up"), g("L0_ffn_w_down")
    clip = np.ascontiguousarray(
        np.stack([np.asarray(inputs["L0_ffn_clip_x"], f32).ravel()[0],
                  np.asarray(inputs["L0_ffn_clip_down"], f32).ravel()[0]])[None, :])
    shared = {
        "wq": wq, "wk": wk, "wv": wv, "wg": wgm,
        "wgk1": wgk1, "wgk2": wgk2,
        "bgk": np.ascontiguousarray(bgk2.reshape(8, 128).T),
        "wo": wo,
        "rmsw": np.ascontiguousarray(np.tile(rms_w[None, :], (128, 4))),
        "wgo": w_gate, "wuo": w_up, "wdo": w_down,

        "clip": clip,
        "ident": np.eye(128, dtype=ml_dtypes.bfloat16),
        "mask": np.ascontiguousarray(np.triu(np.ones((128, 128), f32))),
        "ones1": np.ones((1, 128), f32),
    }
    maps = []
    for c in range(min(ncore, x.shape[0])):
        m = dict(shared)
        m["xf"] = np.ascontiguousarray(x[c].T)
        m["xt"] = np.ascontiguousarray(x[c])
        m["wg_sl"] = np.ascontiguousarray(w_gate[:, 128 * c:128 * (c + 1)])
        m["wu_sl"] = np.ascontiguousarray(w_up[:, 128 * c:128 * (c + 1)])
        m["wd_sl"] = np.ascontiguousarray(w_down[128 * c:128 * (c + 1), :])
        maps.append(m)
    return maps


def kernel(**inputs):
    T = 4096
    maps = _in_maps(inputs, T)
    nc = build(T)
    res = run_bass_kernel_spmd(nc, maps, core_ids=list(range(NCORE)))
    out = np.stack([np.asarray(res.results[c]["out"]) for c in range(NCORE)], 0)
    return np.ascontiguousarray(out.astype(np.float32))


# version 25
# speedup vs baseline: 1.1241x; 1.0321x over previous
import contextlib
import math
import numpy as np
import concourse.bass as bass
import concourse.mybir as mybir
from concourse.bass_utils import run_bass_kernel_spmd

F32 = mybir.dt.float32
BF16 = mybir.dt.bfloat16
I32 = mybir.dt.int32
AF = mybir.ActivationFunctionType
ALU = mybir.AluOpType
AX = mybir.AxisListType

ENGS = ("tensor", "vector", "scalar", "gpsimd", "sync")



class Buf:
    __slots__ = ("name", "last_w", "readers")

    def __init__(self, name=""):
        self.name = name
        self.last_w = None
        self.readers = {}


class KB:
    def __init__(self, nc, es):
        self.nc = nc
        self.es = es
        self.sems = {}
        self.cnt = {}
        self.known = {e: {} for e in ENGS}
        self.q = {e: [] for e in ENGS}
        for e in ENGS:
            self.mk_sem("e_" + e)

    def mk_sem(self, name):
        self.sems[name] = self.es.enter_context(self.nc.semaphore(name))
        self.cnt[name] = 0
        return name

    def wait(self, ename, sem_name, val):
        k = self.known[ename]
        if k.get(sem_name, 0) < val:
            sem = self.sems[sem_name]
            self.q[ename].append(lambda eng, s=sem, v=val: eng.wait_ge(s, v))
            k[sem_name] = val

    def _pre(self, ename, reads, writes, exempt_self_waw=False, dma_sem=None):
        need = {}

        def add(item, waw=False):
            if item is None:
                return
            sn, v = item
            if exempt_self_waw and sn == "e_" + ename:
                return
            if waw and dma_sem is not None and sn == dma_sem:
                return
            if "@" in sn:
                v = max(v, self.cnt[sn])
            if need.get(sn, 0) < v:
                need[sn] = v

        for b in reads:
            add(b.last_w)
        for b in writes:
            add(b.last_w, waw=True)
            for sn, v in b.readers.items():
                add((sn, v))
        for sn in sorted(need):
            self.wait(ename, sn, need[sn])

    def _post(self, ename, f, sem_name, inc, reads, writes):
        self.cnt[sem_name] += inc
        v = self.cnt[sem_name]
        sem = self.sems[sem_name]
        self.q[ename].append(lambda eng, s=sem, i=inc: f(eng).then_inc(s, i))
        for b in writes:
            b.last_w = (sem_name, v)
            b.readers = {}
        for b in reads:
            if b.readers.get(sem_name, 0) < v:
                b.readers[sem_name] = v

    def op(self, ename, f, reads=(), writes=(), accum=False):
        self._pre(ename, reads, writes, exempt_self_waw=accum)
        self._post(ename, f, "e_" + ename, 1, reads, writes)

    def mm(self, psum_buf, out_ap, lhsT_ap, rhs_ap, reads=(), start=True, stop=True):
        self.op("tensor",
                lambda t: t.matmul(out_ap, lhsT_ap, rhs_ap, start=start, stop=stop),
                reads=reads, writes=(psum_buf,), accum=True)

    def mm_group(self, psum_buf, emits, reads=()):
        self._pre("tensor", reads, (psum_buf,), exempt_self_waw=True)
        for f in emits[:-1]:
            self.q["tensor"].append(f)
        self._post("tensor", emits[-1], "e_tensor", 1, reads, (psum_buf,))

    def dma(self, ename, stream, out, in_, reads=(), writes=()):
        key = f"{stream}@{ename}"
        if key not in self.sems:
            self.mk_sem(key)
        self._pre(ename, reads, writes, dma_sem=key)
        self._post(ename, lambda eng, o=out, i=in_: eng.dma_start(out=o, in_=i),
                   key, 16, reads, writes)

    def fence(self, engines=ENGS):
        for e in engines:
            for sn in sorted(self.cnt):
                if self.cnt[sn]:
                    self.wait(e, sn, self.cnt[sn])

    def final_wait(self, ename, streams):
        for sn in sorted(self.cnt):
            if sn.split("@")[0] in streams and self.cnt[sn]:
                self.wait(ename, sn, self.cnt[sn])

    def finish(self):
        with self.nc.Block() as block:
            for ename in ENGS:
                queue = self.q[ename]
                if queue:
                    getattr(block, ename)(lambda eng, qq=queue: [c(eng) for c in qq])



D = 1024
HEADS = 4
HDK = 256
HDV = 512
DV = 2048
LR = 16
II = 2816
CH = 128
TC = 512
LN16 = math.log(1.0 / 16.0)
MAGIC = 12582912.0
EPS = 1e-5
NCORE = 8


def build(T=4096, ncore=NCORE):
    NTC = T // TC
    NSEG = TC // CH
    nc = bass.Bass("TRN2", target_bir_lowering=False)

    dr = lambda n, s, dt=F32: nc.dram_tensor(n, s, dt, kind="ExternalInput").ap()
    xf = dr("xf", [D, T])
    xt = dr("xt", [T, D])
    wq = dr("wq", [D, D])
    wk = dr("wk", [D, D])
    wv = dr("wv", [D, DV])
    wg = dr("wg", [D, DV])
    wgk1 = dr("wgk1", [D, LR])
    wgk2 = dr("wgk2", [LR, D])
    bgk = dr("bgk", [128, 8])
    wo = dr("wo", [DV, D])
    rmsw = dr("rmsw", [128, DV])
    wgo = dr("wgo", [II, D])
    wuo = dr("wuo", [II, D])
    wdo = dr("wdo", [D, II])
    wg_sl = dr("wg_sl", [II, 128])
    wu_sl = dr("wu_sl", [II, 128])
    wd_sl = dr("wd_sl", [128, II])
    clip = dr("clip", [1, 2])
    ident = dr("ident", [128, 128], BF16)
    mask = dr("mask", [128, 128])
    ones1 = dr("ones1", [1, 128])

    out = nc.dram_tensor("out", [T, D], F32, kind="ExternalOutput").ap()

    ds = lambda n, s, dt=F32: nc.dram_tensor(n, s, dt)
    d_o = ds("d_o", [T, DV], BF16)
    d_ss = ds("d_ss", [T // 128, 128, HEADS])
    d_y1 = ds("d_y1", [T, D])
    d_y1T = ds("d_y1T", [D, T])
    d_h = ds("d_h", [II, T])
    d_g_sh = ds("d_g_sh", [22, 128, 128], BF16)
    d_u_sh = ds("d_u_sh", [22, 128, 128], BF16)
    d_d_sh = ds("d_d_sh", [22, 128, 128], BF16)
    assert ncore == NCORE, "sharded weight-quant requires 8 cores"
    g_wgq = ds("g_wgq", [8, 22, 128, 128], BF16)
    g_wuq = ds("g_wuq", [8, 22, 128, 128], BF16)
    g_wdq = ds("g_wdq", [8, 22, 128, 128], BF16)
    d_sd = ds("d_sd", [D])
    d_mx = ds("d_mx", [128])
    d_sc1v = ds("d_sc1v", [8])
    d_sc2v = ds("d_sc2v", [8])
    d_sdxv = ds("d_sdxv", [D])
    cc1i = ds("cc1i", [8])
    cc1o = ds("cc1o", [8])
    cc2i = ds("cc2i", [8])
    cc2o = ds("cc2o", [8])

    es = contextlib.ExitStack()
    with es:
        kb = KB(nc, es)
        SB = lambda n, s, dt=F32: es.enter_context(nc.sbuf_tensor(n, s, dt))
        nb = [0]

        def mkbuf(name=None):
            nb[0] += 1
            return Buf(name or f"b{nb[0]}")

        pt = [es.enter_context(nc.psum_tensor(f"p{i}", [128, 512], F32))
              for i in range(8)]
        pb = [mkbuf(f"p{i}") for i in range(8)]

        s_id = SB("s_id", [128, 128], BF16)
        s_idf = SB("s_idf", [128, 128])
        s_mask = SB("s_mask", [128, 128])
        s_ones = SB("s_ones", [1, 128])
        s_clip = SB("s_clip", [1, 2])
        s_sig = SB("s_sig", [1, 2])
        s_sc1 = SB("s_sc1", [1, 8])
        s_sc2 = SB("s_sc2", [1, 8])
        s_cc8 = SB("s_cc8", [1, 8])
        s_mrow = SB("s_mrow", [1, 128])
        s_col = SB("s_col", [128, 8])
        s_gc = SB("s_gc", [128, 24])
        s_uc = SB("s_uc", [128, 24])
        s_dc = SB("s_dc", [128, 8])
        s_gcs = SB("s_gcs", [128, 24])
        s_ucs = SB("s_ucs", [128, 24])
        s_max1 = SB("s_max1", [128, 34])
        s_max2 = SB("s_max2", [128, 178])

        Bg = {n: mkbuf(n) for n in
              ("id mask ones clip sig sc1 sc2 cc8 mrow col gc uc dc gcs ucs "
               "sdxr sdx dump wgw max1 max2 dmx cc1i cc1o cc2i cc2o "
               "dsd dsc1v dsc2v dsdxv gsh ush dsh gwg gwu gwd "
               "do0 do1 dss dy10 dy11 dh0 dh1 dout").split()}
        dump_bufs = [mkbuf("dump0"), mkbuf("dump1")]
        dump_i = [0]

        def dumpbuf():
            dump_i[0] ^= 1
            return dump_bufs[dump_i[0]]

        cc_sem = kb.mk_sem("cc_done")

        kb.dma("sync", "cst", s_id[:, :], ident, writes=(Bg["id"],))
        kb.dma("sync", "cst", s_mask[:, :], mask, writes=(Bg["mask"],))
        kb.dma("sync", "cst", s_ones[0:1, :], ones1, writes=(Bg["ones"],))
        kb.dma("sync", "cst", s_clip[0:1, :], clip, writes=(Bg["clip"],))
        kb.op("scalar", lambda s: s.activation(s_sig[0:1, :], s_clip[0:1, :], AF.Sigmoid),
              reads=(Bg["clip"],), writes=(Bg["sig"],))
        kb.op("vector", lambda v: v.memset(s_col[:, 3:4], -MAGIC), writes=(Bg["col"],))
        idfb = mkbuf("idf")
        kb.op("scalar", lambda s: s.copy(s_idf[:, :], s_id[:, :]),
              reads=(Bg["id"],), writes=(idfb,))
        kb.op("vector", lambda v: v.memset(s_max1[:, :], 0.0), writes=(Bg["max1"],))
        kb.op("vector", lambda v: v.memset(s_max2[:, :], 0.0), writes=(Bg["max2"],))
        kb.op("vector", lambda v: v.memset(s_cc8[0:1, :], 0.0), writes=(Bg["cc8"],))


        esAB = contextlib.ExitStack()
        s_xf = esAB.enter_context(nc.sbuf_tensor(
            "s_xf", [128, 2 * 8 * TC], BF16))
        s_wg = esAB.enter_context(nc.sbuf_tensor(
            "s_wg", [128, 8 * DV], BF16))
        with contextlib.ExitStack() as esA:
            SA = lambda n, s, dt=F32: esA.enter_context(nc.sbuf_tensor(n, s, dt))
            s_wq = SA("s_wq", [128, 8 * D], BF16)
            s_wk = SA("s_wk", [128, 8 * D], BF16)
            s_wv = SA("s_wv", [128, 8 * DV], BF16)
            s_wgk1 = SA("s_wgk1", [128, 8 * LR], BF16)
            s_wgk2 = SA("s_wgk2", [16, D], BF16)
            s_bgk = SA("s_bgk", [128, 8])
            s_z = SA("s_z", [16, TC], BF16)
            s_sg = SA("s_sg", [128, 2 * TC])
            s_sg8 = SA("s_sg8", [128, 8 * TC], BF16)
            s_B = SA("s_B", [128, 8 * TC])
            s_carry = SA("s_carry", [128, 8])
            s_nbq = SA("s_nbq", [128, 8])
            s_pbc = SA("s_pbc", [128, 8])
            s_basep = SA("s_basep", [128, 8])
            s_resc = SA("s_resc", [128, 8])
            s_Ep = SA("s_Ep", [128, TC], BF16)
            s_Em = SA("s_Em", [128, TC], BF16)
            s_q = SA("s_q", [128, 8 * TC], BF16)
            s_kd = SA("s_kd", [128, 8 * TC], BF16)
            s_v = SA("s_v", [128, 2 * DV], BF16)
            s_At = SA("s_At", [128, 2 * 128], BF16)
            s_kg = SA("s_kg", [128, 4 * 128], BF16)
            s_S = SA("s_S", [128, HEADS * 2 * HDV], BF16)
            s_o = SA("s_o", [128, DV], BF16)
            s_ss = SA("s_ss", [128, NSEG * HEADS])
            s_dumpA = SA("s_dumpA", [128, 512], BF16)

            wunits = []
            HWI = 1408
            s_wsl = SA("s_wsl", [128, II])
            s_wq8 = SA("s_wq8", [128, II], BF16)
            s_evt = SA("s_evt", [128, 512], BF16)
            s_rcol = SA("s_rcol", [128, 48])
            wst_b = [mkbuf("wst0"), mkbuf("wst1")]
            wq8_b = mkbuf("wq8")
            evt_b = mkbuf("evt")
            rcol_b = mkbuf("rcol")

            def w_scales(worig, n_tiles, hwid, sc_sbuf, sc_buf, rbase,
                         srow_d=None, srow_buf=None):
                for j in range(n_tiles):
                    def u(j=j):
                        if hwid <= HWI:
                            sl = j % 2
                            off = sl * HWI
                            kb.dma("sync", f"ws{sl}", s_wsl[:, off:off + hwid],
                                   worig[128 * j:128 * (j + 1), :], writes=(wst_b[sl],))
                            kb.op("vector", lambda v, j=j, off=off, sl=sl: v.reduce_max(
                                sc_sbuf[:, j:j + 1], s_wsl[:, off:off + hwid],
                                AX.X, apply_absolute_value=True),
                                reads=(wst_b[sl],), writes=(sc_buf,))
                        else:
                            nh = (hwid + HWI - 1) // HWI
                            for hh in range(nh):
                                w0 = HWI * hh
                                wl = min(hwid, HWI * (hh + 1)) - w0
                                sl = hh % 2
                                off = sl * HWI
                                kb.dma("sync", f"ws{sl}", s_wsl[:, off:off + wl],
                                       worig[128 * j:128 * (j + 1), w0:w0 + wl],
                                       writes=(wst_b[sl],))
                                kb.op("vector", lambda v, wl=wl, hh=hh, off=off,
                                      sl=sl: v.reduce_max(
                                    s_rcol[:, 45 + hh:46 + hh], s_wsl[:, off:off + wl],
                                    AX.X, apply_absolute_value=True),
                                    reads=(wst_b[sl],), writes=(rcol_b,))
                            for hh in range(1, nh):
                                kb.op("vector", lambda v, hh=hh: v.tensor_tensor(
                                    s_rcol[:, 45:46], s_rcol[:, 45:46],
                                    s_rcol[:, 45 + hh:46 + hh], ALU.max),
                                    reads=(rcol_b,), writes=(rcol_b,))
                            kb.op("vector", lambda v, j=j: v.tensor_copy(
                                sc_sbuf[:, j:j + 1], s_rcol[:, 45:46]),
                                reads=(rcol_b,), writes=(sc_buf,))
                    wunits.append(u)

                def u_fin():
                    kb.op("vector", lambda v: v.tensor_scalar(
                        sc_sbuf[:, 0:n_tiles], sc_sbuf[:, 0:n_tiles],
                        1.0 / 127.0, None, ALU.mult),
                        reads=(sc_buf,), writes=(sc_buf,))
                    if rbase is not None:
                        rc = s_rcol[:, rbase:rbase + n_tiles]
                        kb.op("vector", lambda v: v.reciprocal(
                            rc, sc_sbuf[:, 0:n_tiles]), reads=(sc_buf,), writes=(rcol_b,))
                        kb.op("vector", lambda v: v.tensor_tensor(
                            s_wsl[:, 0:n_tiles], sc_sbuf[:, 0:n_tiles], rc, ALU.mult),
                            reads=(sc_buf, rcol_b), writes=(wst_b[0],))
                        kb.op("vector", lambda v: v.tensor_scalar(
                            s_wsl[:, 0:n_tiles], s_wsl[:, 0:n_tiles], 2.0, -1.0,
                            ALU.subtract, ALU.mult), reads=(wst_b[0],), writes=(wst_b[0],))
                        kb.op("vector", lambda v: v.tensor_tensor(
                            rc, rc, s_wsl[:, 0:n_tiles], ALU.mult),
                            reads=(wst_b[0], rcol_b), writes=(rcol_b,))
                    if srow_d is not None:
                        for j in range(n_tiles):
                            kb.dma("sync", "trow", srow_d.ap()[128 * j:128 * (j + 1)],
                                   sc_sbuf[:, j:j + 1], reads=(sc_buf,), writes=(srow_buf,))
                wunits.append(u_fin)

            def w_shard_quant(wsl_dram, mode, rbase, sh_dram, sh_buf, g_dram, g_buf,
                              own_scale=False):
                def u_load():
                    if mode == "gu":
                        kb.dma("sync", "wsl", s_wsl[:, :].rearrange(
                            "p (it h) -> p it h", h=128),
                            wsl_dram.rearrange("(it p) h -> p it h", p=128),
                            writes=(wst_b[0], wst_b[1]))
                    else:
                        kb.dma("sync", "wsl", s_wsl[:, :], wsl_dram,
                               writes=(wst_b[0], wst_b[1]))
                wunits.append(u_load)
                if own_scale:
                    def u_osc():
                        kb.op("vector", lambda v: v.reduce_max(
                            s_rcol[:, 44:45], s_wsl[:, :], AX.X,
                            apply_absolute_value=True),
                            reads=(wst_b[0], wst_b[1]), writes=(rcol_b,))
                        kb.op("vector", lambda v: v.tensor_scalar(
                            s_rcol[:, 44:45], s_rcol[:, 44:45], 1.0 / 127.0, None,
                            ALU.mult), reads=(rcol_b,), writes=(rcol_b,))
                        kb.op("vector", lambda v: v.reciprocal(
                            s_rcol[:, 46:47], s_rcol[:, 44:45]),
                            reads=(rcol_b,), writes=(rcol_b,))
                        kb.op("vector", lambda v: v.tensor_tensor(
                            s_rcol[:, 47:48], s_rcol[:, 44:45], s_rcol[:, 46:47],
                            ALU.mult), reads=(rcol_b,), writes=(rcol_b,))
                        kb.op("vector", lambda v: v.tensor_scalar(
                            s_rcol[:, 47:48], s_rcol[:, 47:48], 2.0, -1.0,
                            ALU.subtract, ALU.mult), reads=(rcol_b,), writes=(rcol_b,))
                        kb.op("vector", lambda v: v.tensor_tensor(
                            s_rcol[:, 46:47], s_rcol[:, 46:47], s_rcol[:, 47:48],
                            ALU.mult), reads=(rcol_b,), writes=(rcol_b,))
                    wunits.append(u_osc)

                def u_quant():
                    if mode == "gu":
                        rv = bass.AP(s_rcol, rbase, [[48, 128], [1, 22], [0, 128]])
                        kb.op("vector", lambda v, rv=rv: v.tensor_tensor(
                            s_wsl[:, :].rearrange("p (it h) -> p it h", h=128),
                            s_wsl[:, :].rearrange("p (it h) -> p it h", h=128),
                            rv, ALU.mult),
                            reads=(wst_b[0], wst_b[1], rcol_b),
                            writes=(wst_b[0], wst_b[1]))
                    else:
                        kb.op("vector", lambda v: v.tensor_scalar(
                            s_wsl[:, :], s_wsl[:, :], s_rcol[:, 46:47], None,
                            ALU.mult), reads=(wst_b[0], wst_b[1], rcol_b),
                            writes=(wst_b[0], wst_b[1]))
                    kb.op("vector", lambda v: v.tensor_scalar(
                        s_wsl[:, :], s_wsl[:, :], 127.0, -128.0,
                        ALU.min, ALU.max),
                        reads=(wst_b[0], wst_b[1]), writes=(wst_b[0], wst_b[1]))
                    kb.op("vector", lambda v: v.tensor_scalar(
                        s_wsl[:, :], s_wsl[:, :], MAGIC, MAGIC,
                        ALU.add, ALU.subtract),
                        reads=(wst_b[0], wst_b[1]), writes=(wst_b[0], wst_b[1]))
                    kb.op("scalar", lambda s: s.copy(s_wq8[:, :], s_wsl[:, :]),
                          reads=(wst_b[0], wst_b[1]), writes=(wq8_b,))
                wunits.append(u_quant)

                for it0 in range(0, 22, 4):
                    def u_tr(it0=it0):
                        nb4 = min(4, 22 - it0)
                        bk = 2 + (it0 // 4) % 2
                        for z in range(nb4):
                            it = it0 + z
                            kb.mm(pb[bk], pt[bk][:, 128 * z:128 * (z + 1)],
                                  s_wq8[:, 128 * it:128 * (it + 1)], s_id[:, :],
                                  reads=(wq8_b, Bg["id"]), start=True, stop=True)
                        kb.op("vector", lambda v, bk=bk, nb4=nb4: v.tensor_copy(
                            s_evt[:, 0:128 * nb4], pt[bk][:, 0:128 * nb4]),
                            reads=(pb[bk],), writes=(evt_b,))
                        kb.dma("sync", "wsp",
                               sh_dram.ap()[it0:it0 + nb4, :, :].rearrange(
                                   "it p c -> p it c"),
                               s_evt[:, 0:128 * nb4].rearrange(
                                   "p (it c) -> p it c", c=128),
                               reads=(evt_b,), writes=(sh_buf,))
                    wunits.append(u_tr)

                def u_ag():
                    kb._pre("gpsimd", (sh_buf,), (g_buf,))
                    kb._post("gpsimd", lambda g2: g2.collective_compute(
                        "AllGather", ALU.bypass,
                        replica_groups=[list(range(ncore))],
                        ins=[sh_dram.ap()], outs=[g_dram.ap()]),
                        "cc_done", 1, (sh_buf,), (g_buf,))
                wunits.append(u_ag)

            w_scales(wgo, 22, D, s_gc, Bg["gc"], 0)
            w_shard_quant(wg_sl, "gu", 0, d_g_sh, Bg["gsh"], g_wgq, Bg["gwg"])
            w_scales(wuo, 22, D, s_uc, Bg["uc"], 22)
            w_shard_quant(wu_sl, "gu", 22, d_u_sh, Bg["ush"], g_wuq, Bg["gwu"])
            w_scales(wdo, 8, II, s_dc, Bg["dc"], None, d_sd, Bg["dsd"])
            w_shard_quant(wd_sl, "dn", 46, d_d_sh, Bg["dsh"], g_wdq, Bg["gwd"],
                          own_scale=True)
            wu_per_iter = (len(wunits) + NTC - 1) // NTC




            wqb, wkb, wvb = mkbuf("wq"), mkbuf("wk"), mkbuf("wv")
            wgk1b, wgk2b, bgkb = mkbuf(), mkbuf(), mkbuf()
            xfb = [mkbuf("xf0"), mkbuf("xf1")]
            zb = mkbuf()
            sgb = [mkbuf(), mkbuf()]
            sg8b = mkbuf("sg8")
            Bb = [mkbuf(f"B{d}") for d in range(8)]
            carryb = mkbuf("carry")
            nbqb, pbcb = mkbuf(), mkbuf()
            basepb, rescb = mkbuf("basep"), mkbuf("resc")
            Epb = [mkbuf(), mkbuf()]
            Emb = [mkbuf(), mkbuf()]
            qb = [mkbuf(f"q{d}") for d in range(8)]
            kdb = [mkbuf(f"kd{d}") for d in range(8)]
            vb = [mkbuf("v0"), mkbuf("v1")]
            Atb = [mkbuf(), mkbuf()]
            kgb = [mkbuf(), mkbuf()]
            Sb = [mkbuf(f"S{h}") for h in range(HEADS)]
            ob = [mkbuf("o0"), mkbuf("o1")]
            ssb = mkbuf("ss")

            kb.dma("gpsimd", "wA", s_wq[:, :].rearrange("p (k t) -> p k t", t=D),
                   wq.rearrange("(k p) t -> p k t", p=128), writes=(wqb,))
            kb.dma("gpsimd", "wA", s_wk[:, :].rearrange("p (k t) -> p k t", t=D),
                   wk.rearrange("(k p) t -> p k t", p=128), writes=(wkb,))
            kb.dma("gpsimd", "wA", s_wv[:, :].rearrange("p (k t) -> p k t", t=DV),
                   wv.rearrange("(k p) t -> p k t", p=128), writes=(wvb,))
            kb.dma("gpsimd", "wA", s_wgk1[:, :].rearrange("p (k t) -> p k t", t=LR),
                   wgk1.rearrange("(k p) t -> p k t", p=128), writes=(wgk1b,))
            kb.dma("gpsimd", "wA", s_wgk2[:, :], wgk2, writes=(wgk2b,))
            kb.dma("gpsimd", "wB", s_wg[:, :].rearrange("p (k t) -> p k t", t=DV),
                   wg.rearrange("(k p) t -> p k t", p=128), writes=(Bg["wgw"],))
            kb.dma("sync", "wA", s_bgk[:, :], bgk, writes=(bgkb,))
            kb.op("vector", lambda v: v.memset(s_S[:, :], 0.0),
                  writes=tuple(Sb))
            kb.op("vector", lambda v: v.memset(s_carry[:, :], 0.0),
                  writes=(carryb,))
            kb.op("vector", lambda v: v.memset(s_basep[:, :], 0.0),
                  writes=(basepb,))

            def xload(c8):
                xsl = c8 % 2
                xoff = xsl * 8 * TC
                t0 = c8 * TC
                kb.dma("gpsimd", f"xf{xsl}",
                       s_xf[:, xoff:xoff + 8 * TC].rearrange("p (k t) -> p k t", t=TC),
                       xf[:, t0:t0 + TC].rearrange("(k p) t -> p k t", p=128),
                       writes=(xfb[xsl],))

            xload(0)
            for c8 in range(NTC):
                xsl = c8 % 2
                xoff = xsl * 8 * TC
                t0 = c8 * TC
                xload(c8 + 1 if c8 + 1 < NTC else 0)

                kb.mm_group(pb[2], [
                    (lambda t, k=k, xoff=xoff: t.matmul(
                        pt[2][0:16, 0:TC], s_wgk1[:, LR * k:LR * (k + 1)],
                        s_xf[:, xoff + TC * k: xoff + TC * (k + 1)],
                        start=(k == 0), stop=(k == 7)))
                    for k in range(8)], reads=(wgk1b, xfb[xsl]))
                kb.op("vector", lambda v: v.tensor_copy(s_z[:, :], pt[2][0:16, 0:TC]),
                      reads=(pb[2],), writes=(zb,))

                for dt in range(8):
                    bk = 2 + (dt % 2)
                    kb.mm(pb[bk], pt[bk][:, 0:TC], s_wgk2[:, 128 * dt:128 * (dt + 1)],
                          s_z[:, :], reads=(wgk2b, zb))
                    kb.op("scalar", lambda s, bk=bk, dt=dt: s.activation(
                        s_sg8[:, TC * dt:TC * (dt + 1)], pt[bk][:, 0:TC], AF.Sigmoid,
                        bias=s_bgk[:, dt:dt + 1]),
                        reads=(pb[bk], bgkb), writes=(sg8b,))
                kb.op("vector", lambda v: v.tensor_tensor(
                    s_resc[:, :], s_carry[:, :], s_basep[:, :], ALU.subtract),
                    reads=(carryb, basepb), writes=(rescb,))
                kb.op("scalar", lambda s: s.activation(
                    s_resc[:, :], s_resc[:, :], AF.Exp, scale=1.0 / 16.0),
                    reads=(rescb,), writes=(rescb,))
                kb.op("vector", lambda v: v.tensor_copy(s_basep[:, :], s_carry[:, :]),
                      reads=(carryb,), writes=(basepb,))
                kb.op("vector", lambda v: v.tensor_scalar(
                    s_nbq[:, :], s_basep[:, :], -1.0 / 16.0, LN16,
                    ALU.mult, ALU.add), reads=(basepb,), writes=(nbqb,))
                kb.op("vector", lambda v: v.tensor_scalar(
                    s_pbc[:, :], s_basep[:, :], 1.0 / 16.0, None,
                    ALU.mult), reads=(basepb,), writes=(pbcb,))
                if c8 > 0:
                    for h in range(HEADS):
                        for i in range(2):
                            kb.op("vector", lambda v, h=h, i=i: v.tensor_scalar(
                                s_S[:, 2 * HDV * h + HDV * i: 2 * HDV * h + HDV * (i + 1)],
                                s_S[:, 2 * HDV * h + HDV * i: 2 * HDV * h + HDV * (i + 1)],
                                s_resc[:, 2 * h + i:2 * h + i + 1], None, ALU.mult),
                                reads=(rescb, Sb[h]), writes=(Sb[h],))
                for dt in range(8):
                    ssl = dt % 2
                    kb.op("scalar", lambda s, dt=dt, ssl=ssl: s.activation(
                        s_sg[:, TC * ssl:TC * (ssl + 1)],
                        s_sg8[:, TC * dt:TC * (dt + 1)], AF.Ln),
                        reads=(sg8b,), writes=(sgb[ssl],))
                    kb.op("vector", lambda v, dt=dt, ssl=ssl: v.tensor_tensor_scan(
                        s_B[:, TC * dt:TC * (dt + 1)],
                        s_sg[:, TC * ssl:TC * (ssl + 1)],
                        s_sg[:, TC * ssl:TC * (ssl + 1)],
                        s_carry[:, dt:dt + 1], ALU.add, ALU.bypass),
                        reads=(sgb[ssl], carryb), writes=(Bb[dt],))
                kb.op("vector", lambda v: v.tensor_copy(
                    s_carry[:, 0:8], s_B[:, TC - 1:8 * TC:TC]),
                    reads=tuple(Bb), writes=(carryb,))

                for dt in range(8):
                    bk = (0, 1, 4, 5)[dt % 4]
                    esl = 0
                    eo = 0
                    kb.op("scalar", lambda s, dt=dt, eo=eo: s.activation(
                        s_Ep[:, eo:eo + TC], s_B[:, TC * dt:TC * (dt + 1)],
                        AF.Exp, bias=s_nbq[:, dt:dt + 1], scale=1.0 / 16.0),
                        reads=(Bb[dt], nbqb), writes=(Epb[esl],))
                    kb.op("scalar", lambda s, dt=dt, eo=eo: s.activation(
                        s_Em[:, eo:eo + TC], s_B[:, TC * dt:TC * (dt + 1)],
                        AF.Exp, bias=s_pbc[:, dt:dt + 1], scale=-1.0 / 16.0),
                        reads=(Bb[dt], pbcb), writes=(Emb[esl],))
                    kb.mm_group(pb[bk], [
                        (lambda t, k=k, bk=bk, dt=dt, xoff=xoff: t.matmul(
                            pt[bk][:, 0:TC],
                            s_wq[:, D * k + 128 * dt: D * k + 128 * (dt + 1)],
                            s_xf[:, xoff + TC * k: xoff + TC * (k + 1)],
                            start=(k == 0), stop=(k == 7)))
                        for k in range(8)], reads=(wqb, xfb[xsl]))
                    kb.op("vector", lambda v, dt=dt, bk=bk, eo=eo:
                          v.scalar_tensor_tensor(
                        s_q[:, TC * dt:TC * (dt + 1)], pt[bk][:, 0:TC], 1.0,
                        s_Ep[:, eo:eo + TC], ALU.bypass, ALU.mult),
                        reads=(pb[bk], Epb[esl]), writes=(qb[dt],))
                    bk2 = (2, 3, 6, 7)[dt % 4]
                    kb.mm_group(pb[bk2], [
                        (lambda t, k=k, bk2=bk2, dt=dt, xoff=xoff: t.matmul(
                            pt[bk2][:, 0:TC],
                            s_wk[:, D * k + 128 * dt: D * k + 128 * (dt + 1)],
                            s_xf[:, xoff + TC * k: xoff + TC * (k + 1)],
                            start=(k == 0), stop=(k == 7)))
                        for k in range(8)], reads=(wkb, xfb[xsl]))
                    kb.op("vector", lambda v, dt=dt, bk2=bk2, eo=eo:
                          v.scalar_tensor_tensor(
                        s_kd[:, TC * dt:TC * (dt + 1)], pt[bk2][:, 0:TC], 1.0,
                        s_Em[:, eo:eo + TC], ALU.bypass, ALU.mult),
                        reads=(pb[bk2], Emb[esl]), writes=(kdb[dt],))

                def v_proj(g):
                    vsl = g % 2
                    for nuc in range(4):
                        bk = 2 + nuc % 2
                        kb.mm_group(pb[bk], [
                            (lambda t, k=k, bk=bk, g=g, nuc=nuc, xoff=xoff: t.matmul(
                                pt[bk][:, :],
                                s_xf[:, xoff + TC * k + CH * g: xoff + TC * k + CH * (g + 1)],
                                s_wv[:, DV * k + 512 * nuc: DV * k + 512 * (nuc + 1)],
                                start=(k == 0), stop=(k == 7)))
                            for k in range(8)], reads=(xfb[xsl], wvb))
                        kb.op("scalar", lambda s, vsl=vsl, nuc=nuc, bk=bk: s.copy(
                            s_v[:, DV * vsl + 512 * nuc: DV * vsl + 512 * (nuc + 1)],
                            pt[bk][:, :]), reads=(pb[bk],), writes=(vb[vsl],))

                def gla_stage1(it):
                    g, h = divmod(it, HEADS)
                    par = it % 2
                    bA = 4 if par == 0 else 6
                    d0, d1 = 2 * h, 2 * h + 1
                    kb.mm_group(pb[bA], [
                        (lambda t, i=i, dt=dt, g=g, bA=bA: t.matmul(
                            pt[bA][:, 0:128],
                            s_kd[:, TC * dt + CH * g: TC * dt + CH * (g + 1)],
                            s_q[:, TC * dt + CH * g: TC * dt + CH * (g + 1)],
                            start=(i == 0), stop=(i == 1)))
                        for i, dt in enumerate((d0, d1))],
                        reads=(kdb[d0], kdb[d1], qb[d0], qb[d1]))
                    for i, dt in enumerate((d0, d1)):
                        kb.mm_group(pb[bA], [
                            (lambda t, i=i, dt=dt, g=g, bA=bA: t.matmul(
                                pt[bA][:, 128 * (i + 1):128 * (i + 2)],
                                s_kd[:, TC * dt + CH * g: TC * dt + CH * (g + 1)],
                                s_id[:, :],
                                start=True, stop=True))],
                            reads=(kdb[dt], Bg["id"]))
                    kb.op("vector", lambda v, bA=bA, par=par: v.tensor_tensor(
                        s_At[:, 128 * par:128 * (par + 1)], pt[bA][:, 0:128],
                        s_mask[:, :], ALU.mult),
                        reads=(pb[bA], Bg["mask"]), writes=(Atb[par],))
                    kb.op("vector", lambda v, bA=bA, par=par: v.tensor_copy(
                        s_kg[:, 256 * par:256 * (par + 1)], pt[bA][:, 128:384]),
                        reads=(pb[bA],), writes=(kgb[par],))

                def gla_stage2(it):
                    g, h = divmod(it, HEADS)
                    par = it % 2
                    bO = 5 if par == 0 else 0
                    bP = (7, 3) if par == 0 else (1, 2)
                    osl = g % 2
                    d0, d1 = 2 * h, 2 * h + 1
                    vsl = g % 2
                    emits = [
                        (lambda t, vsl=vsl, h=h, par=par, bO=bO: t.matmul(
                            pt[bO][:, :], s_At[:, 128 * par:128 * (par + 1)],
                            s_v[:, DV * vsl + HDV * h: DV * vsl + HDV * (h + 1)],
                            start=True, stop=False))]
                    for i, dt in enumerate((d0, d1)):
                        emits.append(
                            lambda t, i=i, dt=dt, g=g, h=h, bO=bO: t.matmul(
                                pt[bO][:, :],
                                s_q[:, TC * dt + CH * g: TC * dt + CH * (g + 1)],
                                s_S[:, 2 * HDV * h + HDV * i: 2 * HDV * h + HDV * (i + 1)],
                                start=False, stop=(i == 1)))
                    kb.mm_group(pb[bO], emits,
                                reads=(Atb[par], vb[vsl], qb[d0], qb[d1], Sb[h]))
                    for i, dt in enumerate((d0, d1)):
                        kb.mm_group(pb[bP[i]], [
                            (lambda t, i=i, vsl=vsl, h=h, par=par, bk=bP[i]: t.matmul(
                                pt[bk][:, :], s_kg[:, 256 * par + 128 * i:256 * par + 128 * (i + 1)],
                                s_v[:, DV * vsl + HDV * h: DV * vsl + HDV * (h + 1)],
                                start=True, stop=True))],
                            reads=(kgb[par], vb[vsl]))
                    kb.op("scalar", lambda s, g=g, h=h, bO=bO: s.activation(
                        s_dumpA[:, 0:512], pt[bO][:, :], AF.Square,
                        accum_out=s_ss[:, HEADS * g + h:HEADS * g + h + 1]),
                        reads=(pb[bO],), writes=(ssb, dumpbuf()))
                    kb.op("scalar", lambda s, h=h, bO=bO: s.copy(
                        s_o[:, HDV * h: HDV * (h + 1)],
                        pt[bO][:, :]), reads=(pb[bO],), writes=(ob[0],))
                    for i in range(2):
                        kb.op("vector", lambda v, h=h, i=i, bk=bP[i]:
                              v.tensor_tensor(
                                  s_S[:, 2 * HDV * h + HDV * i: 2 * HDV * h + HDV * (i + 1)],
                                  s_S[:, 2 * HDV * h + HDV * i: 2 * HDV * h + HDV * (i + 1)],
                                  pt[bk][:, :], ALU.add),
                              reads=(pb[bP[i]], Sb[h]), writes=(Sb[h],))

                def o_spill(g):
                    osl = g % 2
                    kb.dma("sync", f"sp_o{osl}",
                           d_o.ap()[t0 + CH * g: t0 + CH * (g + 1), :],
                           s_o[:, :], reads=(ob[0],),
                           writes=(Bg[f"do{osl}"],))

                NIT = NSEG * HEADS
                for it in range(NIT):
                    if it % HEADS == 0:
                        v_proj(it // HEADS)
                    gla_stage1(it)
                    if wunits:
                        wunits.pop(0)()
                    if it > 0:
                        gla_stage2(it - 1)
                        if (it - 1) % HEADS == HEADS - 1:
                            o_spill((it - 1) // HEADS)
                gla_stage2(NIT - 1)
                o_spill(NSEG - 1)
                kb.dma("sync", "sp_ss",
                       d_ss.ap()[NSEG * c8:NSEG * (c8 + 1), :, :].rearrange(
                           "g p h -> p g h"),
                       s_ss[:, :], reads=(ssb,), writes=(Bg["dss"],))

            while wunits:
                wunits.pop(0)()
            kb.fence()

        with contextlib.ExitStack() as esB:
            SBp = lambda n, s, dt=F32: esB.enter_context(nc.sbuf_tensor(n, s, dt))
            s_rw = SBp("s_rw", [128, DV], BF16)
            s_oi = SBp("s_oi", [128, 2 * NSEG * DV], BF16)
            s_ssi = SBp("s_ssi", [128, 2 * NSEG * HEADS])
            s_wo = SBp("s_wo", [128, 16 * D], BF16)
            s_rst = SBp("s_rst", [128, NSEG * HEADS])
            s_sil = SBp("s_sil", [128, 4 * 512], BF16)
            s_gg = SBp("s_gg", [128, NSEG * DV], BF16)
            s_oop2 = SBp("s_oop2", [128, 2 * DV], BF16)
            s_oopT = SBp("s_oopT", [128, 2 * 16 * 128], BF16)
            s_xt = SBp("s_xt", [128, NSEG * D])
            s_y1 = SBp("s_y1", [128, 2 * NSEG * D])
            s_dumpB = SBp("s_dumpB", [128, 1024], BF16)
            s_lnc = SBp("s_lnc", [128, 2 * NSEG * 5])

            wob, rwb = mkbuf(), mkbuf()
            oib = [mkbuf(), mkbuf()]
            ssib = [mkbuf(), mkbuf()]
            rstb = mkbuf()
            silb = [mkbuf(), mkbuf(), mkbuf(), mkbuf()]
            ggb = [mkbuf(f"gg{g}") for g in range(NSEG)]
            oopb = [mkbuf(), mkbuf()]
            oopTb = [mkbuf(), mkbuf()]
            xtb = mkbuf("xt")
            y1b = [[mkbuf(f"y1{s}{g}") for g in range(NSEG)] for s in range(2)]
            lncb = mkbuf()

            kb.dma("gpsimd", "wB3",
                   s_wo[:, :].rearrange("p (k t) -> p k t", t=D),
                   wo.rearrange("(k p) t -> p k t", p=128), writes=(wob,))
            kb.dma("gpsimd", "wB2", s_rw[:, :], rmsw, writes=(rwb,))



            def bload(c8):
                t0 = c8 * TC
                xsl = c8 % 2
                kb.dma("sync", f"oB{xsl}",
                       s_oi[:, xsl * NSEG * DV:(xsl + 1) * NSEG * DV].rearrange(
                           "p (g v) -> p g v", v=DV),
                       d_o.ap()[t0:t0 + TC, :].rearrange("(g p) v -> p g v", p=128),
                       reads=(Bg["do0"], Bg["do1"]), writes=(oib[xsl],))
                nsh = NSEG * HEADS
                kb.dma("sync", f"oB{xsl}", s_ssi[:, xsl * nsh:(xsl + 1) * nsh],
                       d_ss.ap()[NSEG * c8:NSEG * (c8 + 1), :, :].rearrange(
                           "g p h -> p g h"),
                       reads=(Bg["dss"],), writes=(ssib[xsl],))

            bload(0)
            for c8 in range(NTC):
                t0 = c8 * TC
                xsl = c8 % 2
                xoffB = xsl * 8 * TC
                ooffB = xsl * NSEG * DV
                if c8 + 1 < NTC:
                    bload(c8 + 1)
                    xload(c8 + 1)
                kb.dma("sync", f"xtB{xsl}",
                       s_xt[:, :].rearrange("p (g v) -> p g v", v=D),
                       xt[t0:t0 + TC, :].rearrange("(g p) v -> p g v", p=128),
                       writes=(xtb,))
                nsh = NSEG * HEADS
                kb.op("vector", lambda v, xsl=xsl, nsh=nsh: v.tensor_scalar(
                    s_rst[:, :], s_ssi[:, xsl * nsh:(xsl + 1) * nsh],
                    1.0 / HDV, EPS, ALU.mult, ALU.add),
                    reads=(ssib[xsl],), writes=(rstb,))
                kb.op("scalar", lambda s: s.activation(
                    s_rst[:, :], s_rst[:, :], AF.Sqrt), reads=(rstb,), writes=(rstb,))
                kb.op("vector", lambda v: v.reciprocal(s_rst[:, :], s_rst[:, :]),
                      reads=(rstb,), writes=(rstb,))
                for g in range(NSEG):
                    for nuc in range(4):
                        bk = nuc % 2
                        ssl = nuc
                        kb.mm_group(pb[bk], [
                            (lambda t, k=k, bk=bk, g=g, nuc=nuc, xoffB=xoffB: t.matmul(
                                pt[bk][:, :],
                                s_xf[:, xoffB + TC * k + CH * g: xoffB + TC * k + CH * (g + 1)],
                                s_wg[:, DV * k + 512 * nuc: DV * k + 512 * (nuc + 1)],
                                start=(k == 0), stop=(k == 7)))
                            for k in range(8)], reads=(xfb[xsl], Bg["wgw"]))
                        kb.op("scalar", lambda s, bk=bk, ssl=ssl: s.activation(
                            s_sil[:, 512 * ssl:512 * (ssl + 1)], pt[bk][:, :], AF.Silu),
                            reads=(pb[bk],), writes=(silb[ssl],))
                        kb.op("vector", lambda v, g=g, nuc=nuc, ssl=ssl: v.tensor_tensor(
                            s_gg[:, DV * g + 512 * nuc:DV * g + 512 * (nuc + 1)],
                            s_sil[:, 512 * ssl:512 * (ssl + 1)],
                            s_rw[:, 512 * nuc:512 * (nuc + 1)], ALU.mult),
                            reads=(silb[ssl], rwb), writes=(ggb[g],))
                def b_oop(g):
                    for h in range(HEADS):
                        col = HEADS * g + h
                        kb.op("vector", lambda v, g=g, h=h, col=col,
                              ooffB=ooffB, xsl=xsl:
                              v.scalar_tensor_tensor(
                                  s_oop2[:, DV * (g % 2) + HDV * h: DV * (g % 2) + HDV * (h + 1)],
                                  s_oi[:, ooffB + DV * g + HDV * h: ooffB + DV * g + HDV * (h + 1)],
                                  s_rst[:, col:col + 1],
                                  s_gg[:, DV * g + HDV * h:DV * g + HDV * (h + 1)],
                                  ALU.mult, ALU.mult),
                              reads=(oib[xsl], rstb, ggb[g]), writes=(oopb[g % 2],))

                def b_tr(g):
                    osl = g % 2
                    for jb in range(4):
                        bk = 2 + jb % 2
                        for j4 in range(4):
                            j = 4 * jb + j4
                            kb.mm(pb[bk], pt[bk][:, 128 * j4:128 * (j4 + 1)],
                                  s_oop2[:, DV * osl + 128 * j: DV * osl + 128 * (j + 1)],
                                  s_id[:, :],
                                  reads=(oopb[osl], Bg["id"]), start=True, stop=True)
                        kb.op("vector", lambda v, jb=jb, bk=bk, osl=osl: v.tensor_copy(
                            s_oopT[:, 2048 * osl + 512 * jb:2048 * osl + 512 * (jb + 1)],
                            pt[bk][:, :]),
                            reads=(pb[bk],), writes=(oopTb[osl],))

                def b_out(g):
                    osl = g % 2
                    for dc in range(2):
                        bk = 4 + dc
                        kb.mm_group(pb[bk], [
                            (lambda t, k=k, bk=bk, dc=dc, osl=osl: t.matmul(
                                pt[bk][:, :],
                                s_oopT[:, 2048 * osl + 128 * k:2048 * osl + 128 * (k + 1)],
                                s_wo[:, D * k + 512 * dc: D * k + 512 * (dc + 1)],
                                start=(k == 0), stop=(k == 15)))
                            for k in range(16)], reads=(oopTb[osl], wob))
                        kb.op("vector", lambda v, g=g, dc=dc, bk=bk, xsl=xsl:
                              v.tensor_tensor(
                            s_y1[:, xsl * NSEG * D + D * g + 512 * dc:
                                 xsl * NSEG * D + D * g + 512 * (dc + 1)],
                            pt[bk][:, :],
                            s_xt[:, D * g + 512 * dc: D * g + 512 * (dc + 1)],
                            ALU.add), reads=(pb[bk], xtb), writes=(y1b[xsl][g],))
                    ysl = s_y1[:, xsl * NSEG * D + D * g:xsl * NSEG * D + D * (g + 1)]
                    c0 = 20 * xsl + 4 * g
                    kb.op("scalar", lambda s, ysl=ysl, c0=c0: s.activation(
                        s_dumpB[:, :], ysl, AF.Identity,
                        accum_out=s_lnc[:, c0:c0 + 1]),
                        reads=(y1b[xsl][g],), writes=(dumpbuf(), lncb))
                    kb.op("scalar", lambda s, ysl=ysl, c0=c0: s.activation(
                        s_dumpB[:, :], ysl, AF.Square,
                        accum_out=s_lnc[:, c0 + 1:c0 + 2]),
                        reads=(y1b[xsl][g],), writes=(dumpbuf(), lncb))

                def b_ln_a(c8):
                    t0 = c8 * TC
                    xsl = c8 % 2
                    yo = xsl * NSEG * D
                    lb = 20 * xsl
                    mu = s_lnc[:, lb + 0:lb + 16:4]
                    ss2 = s_lnc[:, lb + 1:lb + 16:4]
                    var = s_lnc[:, lb + 2:lb + 16:4]
                    rst = s_lnc[:, lb + 3:lb + 16:4]
                    kb.op("vector", lambda v: v.tensor_scalar(
                        mu, mu, 1.0 / D, None, ALU.mult), reads=(lncb,), writes=(lncb,))
                    kb.op("vector", lambda v: v.tensor_tensor(
                        var, mu, mu, ALU.mult), reads=(lncb,), writes=(lncb,))
                    kb.op("vector", lambda v: v.tensor_scalar(
                        ss2, ss2, 1.0 / D, EPS, ALU.mult, ALU.add),
                        reads=(lncb,), writes=(lncb,))
                    kb.op("vector", lambda v: v.tensor_tensor(
                        var, ss2, var, ALU.subtract), reads=(lncb,), writes=(lncb,))
                    kb.op("scalar", lambda s: s.activation(
                        rst, var, AF.Sqrt), reads=(lncb,), writes=(lncb,))
                    kb.op("vector", lambda v: v.reciprocal(rst, rst),
                          reads=(lncb,), writes=(lncb,))
                    kb.op("vector", lambda v, mu=mu, rst=rst, lb=lb:
                          v.scalar_tensor_tensor(
                        s_lnc[:, lb + 16:lb + 20], mu, -1.0, rst, ALU.mult, ALU.mult),
                        reads=(lncb,), writes=(lncb,))
                    for g in range(NSEG):
                        ysl = s_y1[:, yo + D * g:yo + D * (g + 1)]
                        c0 = lb + 4 * g
                        kb.op("scalar", lambda s, ysl=ysl, c0=c0, g=g, lb=lb, xsl=xsl:
                              s.activation(
                            ysl, ysl, AF.Identity, scale=s_lnc[:, c0 + 3:c0 + 4],
                            bias=s_lnc[:, lb + 16 + g:lb + 17 + g]),
                            reads=(y1b[xsl][g], lncb), writes=(y1b[xsl][g],))
                        kb.dma("sync", f"spy{xsl}",
                               d_y1.ap()[t0 + CH * g:t0 + CH * (g + 1), :],
                               ysl, reads=(y1b[xsl][g],), writes=(Bg[f"dy1{xsl}"],))

                def b_ln_b(c8):
                    xsl = c8 % 2
                    yo = xsl * NSEG * D
                    for g in range(NSEG):
                        ysl = s_y1[:, yo + D * g:yo + D * (g + 1)]
                        kb.op("vector", lambda v, ysl=ysl, col=NSEG * c8 + g, g=g,
                              xsl=xsl: v.reduce_max(
                            s_max1[:, col:col + 1], ysl, AX.X, apply_absolute_value=True),
                            reads=(y1b[xsl][g],), writes=(Bg["max1"],))

                b_oop(0)
                b_tr(0)
                b_oop(1)
                b_tr(1)
                b_out(0)
                if c8 > 0:
                    b_ln_a(c8 - 1)
                b_oop(2)
                b_tr(2)
                b_out(1)
                b_oop(3)
                if c8 > 0:
                    b_ln_b(c8 - 1)
                b_tr(3)
                b_out(2)
                b_out(3)
                if c8 == NTC - 1:
                    b_ln_a(c8)
                    b_ln_b(c8)
            kb.fence()

        esAB.close()
        def max_collective(s_maxt, ncols, cci, cco, s_scn, scn_buf, maxt_buf,
                           cc_in_b, cc_out_b, sig_col, tny, dscv, dscv_b):
            kb.op("vector", lambda v: v.reduce_max(
                s_maxt[:, ncols:ncols + 1], s_maxt[:, 0:ncols], AX.X),
                reads=(maxt_buf,), writes=(maxt_buf,))
            kb.dma("gpsimd", tny + "g", d_mx.ap(), s_maxt[:, ncols:ncols + 1],
                   reads=(maxt_buf,), writes=(Bg["dmx"],))
            kb.dma("gpsimd", tny + "g", s_mrow[0:1, :], d_mx.ap()[None, :],
                   reads=(Bg["dmx"],), writes=(Bg["mrow"],))
            kb.op("vector", lambda v: v.reduce_max(
                s_cc8[0:1, 0:1], s_mrow[0:1, :], AX.X),
                reads=(Bg["mrow"],), writes=(Bg["cc8"],))
            kb.dma("gpsimd", tny + "g", cci.ap(), s_cc8[0:1, :],
                   reads=(Bg["cc8"],), writes=(cc_in_b,))
            kb._pre("gpsimd", (cc_in_b,), (cc_out_b,))
            kb._post("gpsimd", lambda g: g.collective_compute(
                "AllReduce", ALU.max, replica_groups=[list(range(ncore))],
                ins=[cci.ap()], outs=[cco.ap()]),
                "cc_done", 1, (cc_in_b,), (cc_out_b,))
            kb.dma("gpsimd", tny + "g", s_scn[0:1, 0:1], cco.ap()[None, 0:1],
                   reads=(cc_out_b,), writes=(scn_buf,))
            kb.op("vector", lambda v: v.tensor_tensor(
                s_scn[0:1, 1:2], s_scn[0:1, 0:1], s_sig[0:1, sig_col:sig_col + 1],
                ALU.mult), reads=(scn_buf, Bg["sig"]), writes=(scn_buf,))
            kb.op("vector", lambda v: v.tensor_scalar(
                s_scn[0:1, 1:2], s_scn[0:1, 1:2], 1.0 / 127.0, None, ALU.mult),
                reads=(scn_buf,), writes=(scn_buf,))
            kb.op("vector", lambda v: v.reciprocal(
                s_scn[0:1, 2:3], s_scn[0:1, 1:2]), reads=(scn_buf,), writes=(scn_buf,))
            kb.op("vector", lambda v: v.tensor_tensor(
                s_scn[0:1, 3:4], s_scn[0:1, 1:2], s_scn[0:1, 2:3], ALU.mult),
                reads=(scn_buf,), writes=(scn_buf,))
            kb.op("vector", lambda v: v.tensor_scalar(
                s_scn[0:1, 3:4], s_scn[0:1, 3:4], 2.0, -1.0,
                ALU.subtract, ALU.mult), reads=(scn_buf,), writes=(scn_buf,))
            kb.op("vector", lambda v: v.tensor_tensor(
                s_scn[0:1, 2:3], s_scn[0:1, 2:3], s_scn[0:1, 3:4], ALU.mult),
                reads=(scn_buf,), writes=(scn_buf,))
            kb.dma("gpsimd", tny + "g", dscv.ap()[0:4], s_scn[0:1, 0:4],
                   reads=(scn_buf,), writes=(dscv_b,))

        sc1b, sc2b = Bg["sc1"], Bg["sc2"]
        max_collective(s_max1, NSEG * NTC, cc1i, cc1o, s_sc1, sc1b, Bg["max1"],
                       Bg["cc1i"], Bg["cc1o"], 0, "tnyA", d_sc1v, Bg["dsc1v"])
        colb = Bg["col"]
        kb.dma("gpsimd", "tnyAg", s_col[:, 0:1],
               bass.AP(d_sc1v, 1, [[0, 128], [1, 1]]),
               reads=(Bg["dsc1v"],), writes=(colb,))
        kb.dma("gpsimd", "tnyAg", s_col[:, 1:2],
               bass.AP(d_sc1v, 2, [[0, 128], [1, 1]]),
               reads=(Bg["dsc1v"],), writes=(colb,))
        kb.op("vector", lambda v: v.tensor_scalar(
            s_gcs[:, 0:22], s_gc[:, 0:22], s_col[:, 0:1], None, ALU.mult),
            reads=(Bg["gc"], colb), writes=(Bg["gcs"],))
        kb.op("vector", lambda v: v.tensor_scalar(
            s_ucs[:, 0:22], s_uc[:, 0:22], s_col[:, 0:1], None, ALU.mult),
            reads=(Bg["uc"], colb), writes=(Bg["ucs"],))

        dy1Tb = [mkbuf("dy1T0"), mkbuf("dy1T1")]
        with contextlib.ExitStack() as esT:
            ST = lambda n, s, dt=F32: esT.enter_context(nc.sbuf_tensor(n, s, dt))
            s_ty = ST("s_ty", [128, 2 * NSEG * D])
            s_tyT = ST("s_tyT", [128, 2 * NSEG * D])
            tyb = [mkbuf(), mkbuf()]
            tyTb = [mkbuf(), mkbuf()]

            def tld(c8):
                xsl = c8 % 2
                t0 = c8 * TC
                kb.dma("sync", f"ty{xsl}",
                       s_ty[:, xsl * NSEG * D:(xsl + 1) * NSEG * D].rearrange(
                           "p (g v) -> p g v", v=D),
                       d_y1.ap()[t0:t0 + TC, :].rearrange("(g p) v -> p g v", p=128),
                       reads=(Bg["dy10"], Bg["dy11"]), writes=(tyb[xsl],))

            tld(0)
            tld(1)
            for c8 in range(NTC):
                xsl = c8 % 2
                yo = xsl * NSEG * D
                t0 = c8 * TC
                for db in range(8):
                    bk = db % 4
                    for g in range(NSEG):
                        kb.op("tensor", lambda t, bk=bk, g=g, db=db, yo=yo:
                              t.transpose(
                                  pt[bk][:, 128 * g:128 * (g + 1)],
                                  s_ty[:, yo + D * g + 128 * db:
                                       yo + D * g + 128 * (db + 1)],
                                  s_idf[:, :]),
                              reads=(tyb[xsl], idfb), writes=(pb[bk],), accum=True)
                    kb.op("scalar", lambda s, bk=bk, db=db, yo=yo: s.copy(
                        s_tyT[:, yo + 512 * db:yo + 512 * (db + 1)],
                        pt[bk][:, 0:512]), reads=(pb[bk],), writes=(tyTb[xsl],))
                kb.dma("sync", f"tyo{xsl}",
                       d_y1T.ap()[:, t0:t0 + TC].rearrange(
                           "(db p) t -> p db t", p=128),
                       s_tyT[:, yo:yo + NSEG * D].rearrange(
                           "p (db t) -> p db t", t=TC),
                       reads=(tyTb[xsl],), writes=(dy1Tb[xsl],))
                if c8 + 2 < NTC:
                    tld(c8 + 2)
        kb.final_wait("sync", ["tyo0", "tyo1"])
        kb.wait("sync", "e_tensor", kb.cnt["e_tensor"])
        kb.wait("sync", "e_scalar", kb.cnt["e_scalar"])

        with contextlib.ExitStack() as esC:
            SC = lambda n, s, dt=F32: esC.enter_context(nc.sbuf_tensor(n, s, dt))
            s_wgq = SC("s_wgq", [128, 8 * II], BF16)
            s_wuq = SC("s_wuq", [128, 8 * II], BF16)
            s_y1C = SC("s_y1C", [128, 2 * 8 * TC])
            s_t1 = SC("s_t1", [128, 2 * 2 * TC])
            s_xqT = SC("s_xqT", [128, 2 * 8 * TC], BF16)
            s_silC = SC("s_silC", [128, 2 * TC])
            s_hC = SC("s_hC", [128, 2 * TC])

            wgqb, wuqb = mkbuf(), mkbuf()
            y1Cb = [mkbuf(), mkbuf()]
            xqTb = [mkbuf(), mkbuf()]
            t1b = [mkbuf(), mkbuf()]
            silCb = [mkbuf(), mkbuf()]
            hCb = [mkbuf(), mkbuf()]

            for k in range(8):
                kb.dma("sync", "wCg", s_wgq[:, II * k:II * (k + 1)].rearrange(
                    "p (it c) -> p it c", c=128),
                    g_wgq.ap()[k].rearrange("it p c -> p it c"),
                    reads=(Bg["gwg"],), writes=(wgqb,))
                kb.dma("sync", "wCu", s_wuq[:, II * k:II * (k + 1)].rearrange(
                    "p (it c) -> p it c", c=128),
                    g_wuq.ap()[k].rearrange("it p c -> p it c"),
                    reads=(Bg["gwu"],), writes=(wuqb,))

            def c_quant_units(c8):
                t0 = c8 * TC
                xsl = c8 % 2
                qoff = xsl * 8 * TC
                units = []

                def u_load():
                    kb.dma("sync", f"yC{xsl}",
                           s_y1C[:, qoff:qoff + 8 * TC].rearrange(
                               "p (k t) -> p k t", t=TC),
                           d_y1T.ap()[:, t0:t0 + TC].rearrange(
                               "(k p) t -> p k t", p=128),
                           reads=tuple(dy1Tb), writes=(y1Cb[xsl],))
                units.append(u_load)
                for u in range(4):
                    def u_q(u=u, xsl=xsl, qoff=qoff):
                        usl = u % 2
                        uo = 2 * TC * usl
                        ysl = s_y1C[:, qoff + 2 * TC * u:qoff + 2 * TC * (u + 1)]
                        kb.op("vector", lambda v, ysl=ysl, uo=uo: v.tensor_scalar(
                            s_t1[:, uo:uo + 2 * TC], ysl,
                            s_col[:, 1:2], 127.0, ALU.mult, ALU.min),
                            reads=(y1Cb[xsl], colb), writes=(t1b[usl],))
                        kb.op("vector", lambda v, uo=uo: v.tensor_scalar(
                            s_t1[:, uo:uo + 2 * TC], s_t1[:, uo:uo + 2 * TC],
                            -128.0, MAGIC,
                            ALU.max, ALU.add), reads=(t1b[usl],), writes=(t1b[usl],))
                        kb.op("scalar", lambda s, u=u, uo=uo, qoff=qoff: s.activation(
                            s_xqT[:, qoff + 2 * TC * u:qoff + 2 * TC * (u + 1)],
                            s_t1[:, uo:uo + 2 * TC],
                            AF.Identity, bias=s_col[:, 3:4]),
                            reads=(t1b[usl], colb), writes=(xqTb[xsl],))
                    units.append(u_q)
                return units

            for u in c_quant_units(0):
                u()
            for c8 in range(NTC):
                t0 = c8 * TC
                xsl = c8 % 2
                qoff = xsl * 8 * TC
                nxt = c_quant_units(c8 + 1) if c8 + 1 < NTC else []
                for j in range(22):
                    bkg = j % 2
                    ssl = j % 2
                    kb.mm_group(pb[bkg], [
                        (lambda t, k=k, bkg=bkg, j=j, qoff=qoff: t.matmul(
                            pt[bkg][:, :],
                            s_wgq[:, II * k + 128 * j: II * k + 128 * (j + 1)],
                            s_xqT[:, qoff + TC * k:qoff + TC * (k + 1)],
                            start=(k == 0), stop=(k == 7)))
                        for k in range(8)], reads=(wgqb, xqTb[xsl]))
                    kb.op("scalar", lambda s, j=j, bkg=bkg, ssl=ssl: s.activation(
                        s_silC[:, TC * ssl:TC * (ssl + 1)], pt[bkg][:, :], AF.Silu,
                        scale=s_gcs[:, j:j + 1]),
                        reads=(pb[bkg], Bg["gcs"]), writes=(silCb[ssl],))
                    bku = 4 + j % 2
                    kb.mm_group(pb[bku], [
                        (lambda t, k=k, bku=bku, j=j, qoff=qoff: t.matmul(
                            pt[bku][:, :],
                            s_wuq[:, II * k + 128 * j: II * k + 128 * (j + 1)],
                            s_xqT[:, qoff + TC * k:qoff + TC * (k + 1)],
                            start=(k == 0), stop=(k == 7)))
                        for k in range(8)], reads=(wuqb, xqTb[xsl]))
                    kb.op("vector", lambda v, j=j, bku=bku, ssl=ssl:
                          v.scalar_tensor_tensor(
                              s_hC[:, TC * ssl:TC * (ssl + 1)], pt[bku][:, :],
                              s_ucs[:, j:j + 1], s_silC[:, TC * ssl:TC * (ssl + 1)],
                              ALU.mult, ALU.mult),
                          reads=(pb[bku], Bg["ucs"], silCb[ssl]), writes=(hCb[ssl],))
                    kb.op("vector", lambda v, ssl=ssl, col=22 * c8 + j: v.reduce_max(
                        s_max2[:, col:col + 1], s_hC[:, TC * ssl:TC * (ssl + 1)],
                        AX.X, apply_absolute_value=True),
                        reads=(hCb[ssl],), writes=(Bg["max2"],))
                    kb.dma("sync", f"sph{ssl}",
                           d_h.ap()[128 * j:128 * (j + 1), t0:t0 + TC],
                           s_hC[:, TC * ssl:TC * (ssl + 1)],
                           reads=(hCb[ssl],), writes=(Bg[f"dh{ssl}"],))
                    if nxt and j % 2 == 1:
                        nxt.pop(0)()
                while nxt:
                    nxt.pop(0)()
            kb.fence()

        max_collective(s_max2, 22 * NTC, cc2i, cc2o, s_sc2, sc2b, Bg["max2"],
                       Bg["cc2i"], Bg["cc2o"], 1, "tnyB", d_sc2v, Bg["dsc2v"])
        with contextlib.ExitStack() as esD:
            SD = lambda n, s, dt=F32: esD.enter_context(nc.sbuf_tensor(n, s, dt))
            s_wdq = SD("s_wdq", [128, 22 * D], BF16)
            s_sdxr = SD("s_sdxr", [1, D])
            s_sdx = SD("s_sdx", [128, D])

            kb.dma("sync", "tnyB", s_col[:, 2:3],
                   bass.AP(d_sc2v, 2, [[0, 128], [1, 1]]),
                   reads=(Bg["dsc2v"],), writes=(colb,))
            kb.dma("sync", "tnyC", s_sdxr[0:1, :], d_sd.ap()[None, :],
                   reads=(Bg["dsd"],), writes=(Bg["sdxr"],))
            kb.op("vector", lambda v: v.tensor_scalar(
                s_sdxr[0:1, :], s_sdxr[0:1, :], s_sc2[0:1, 1:2], None, ALU.mult),
                reads=(Bg["sdxr"], sc2b), writes=(Bg["sdxr"],))
            kb.dma("sync", "tnyC", d_sdxv.ap(), s_sdxr[0:1, :],
                   reads=(Bg["sdxr"],), writes=(Bg["dsdxv"],))
            kb.dma("sync", "tnyC", s_sdx[:, :],
                   bass.AP(d_sdxv, 0, [[0, 128], [1, D]]),
                   reads=(Bg["dsdxv"],), writes=(Bg["sdx"],))


            s_hin = SD("s_hin", [128, 4 * TC])
            s_t1D = SD("s_t1D", [128, 4 * TC])
            s_hq = SD("s_hq", [128, 2 * 22 * TC], BF16)
            s_y1D = SD("s_y1D", [128, 2 * NSEG * D])
            s_f = SD("s_f", [128, 512])
            s_y2 = SD("s_y2", [128, NSEG * D])
            s_lncD = SD("s_lncD", [128, NSEG * 5])
            s_dumpD = SD("s_dumpD", [128, 1024], BF16)

            wdqb = mkbuf()
            hinb = [mkbuf(), mkbuf(), mkbuf(), mkbuf()]
            t1Db = [mkbuf(), mkbuf(), mkbuf(), mkbuf()]
            hqb = [[mkbuf(f"hq{s}_{j}") for j in range(22)] for s in range(2)]
            y1Db = [mkbuf(), mkbuf()]
            fb, lncDb = mkbuf(), mkbuf()
            y2b = [mkbuf(f"y2{g}") for g in range(NSEG)]

            for k in range(22):
                kb.dma("sync", "wD", s_wdq[:, D * k:D * (k + 1)].rearrange(
                    "p (j c) -> p j c", c=128),
                    g_wdq.ap()[:, k, :, :].rearrange("j p c -> p j c"),
                    reads=(Bg["gwd"],), writes=(wdqb,))

            def d_quant_units(c8):
                t0 = c8 * TC
                xsl = c8 % 2
                yoff = xsl * NSEG * D
                hoff = xsl * 22 * TC
                units = []
                def uy(xsl=xsl, yoff=yoff, t0=t0):
                    kb.dma("sync", f"yD{xsl}",
                           s_y1D[:, yoff:yoff + NSEG * D].rearrange("p (g v) -> p g v", v=D),
                           d_y1.ap()[t0:t0 + TC, :].rearrange("(g p) v -> p g v", p=128),
                           reads=(Bg["dy10"], Bg["dy11"]), writes=(y1Db[xsl],))
                units.append(uy)
                for j in range(22):
                    def uq(j=j, xsl=xsl, hoff=hoff, t0=t0):
                        sl = j % 4
                        kb.dma("sync", f"hD{sl}", s_hin[:, TC * sl:TC * (sl + 1)],
                               d_h.ap()[128 * j:128 * (j + 1), t0:t0 + TC],
                               reads=(Bg["dh0"], Bg["dh1"]), writes=(hinb[sl],))
                        kb.op("vector", lambda v, sl=sl: v.tensor_scalar(
                            s_t1D[:, TC * sl:TC * (sl + 1)], s_hin[:, TC * sl:TC * (sl + 1)],
                            s_col[:, 2:3], 127.0, ALU.mult, ALU.min),
                            reads=(hinb[sl], colb), writes=(t1Db[sl],))
                        kb.op("vector", lambda v, sl=sl: v.tensor_scalar(
                            s_t1D[:, TC * sl:TC * (sl + 1)], s_t1D[:, TC * sl:TC * (sl + 1)],
                            -128.0, MAGIC, ALU.max, ALU.add),
                            reads=(t1Db[sl],), writes=(t1Db[sl],))
                        kb.op("scalar", lambda s, j=j, sl=sl, hoff=hoff: s.activation(
                            s_hq[:, hoff + TC * j:hoff + TC * (j + 1)],
                            s_t1D[:, TC * sl:TC * (sl + 1)],
                            AF.Identity, bias=s_col[:, 3:4]),
                            reads=(t1Db[sl], colb), writes=(hqb[xsl][j],))
                    units.append(uq)
                return units

            def d_mm_epi(c8, g):
                t0 = c8 * TC
                xsl = c8 % 2
                yoff = xsl * NSEG * D
                hoff = xsl * 22 * TC
                for dc in range(2):
                    bk = 4 + dc
                    kb.mm_group(pb[bk], [
                        (lambda t, j=j, bk=bk, g=g, dc=dc, hoff=hoff: t.matmul(
                            pt[bk][:, :],
                            s_hq[:, hoff + TC * j + CH * g: hoff + TC * j + CH * (g + 1)],
                            s_wdq[:, D * j + 512 * dc: D * j + 512 * (dc + 1)],
                            start=(j == 0), stop=(j == 21)))
                        for j in range(22)], reads=tuple(hqb[xsl]) + (wdqb,))
                    kb.op("vector", lambda v, dc=dc, bk=bk: v.tensor_tensor(
                        s_f[:, :], pt[bk][:, :], s_sdx[:, 512 * dc:512 * (dc + 1)],
                        ALU.mult), reads=(pb[bk], Bg["sdx"]), writes=(fb,))
                    kb.op("vector", lambda v, g=g, dc=dc, yoff=yoff: v.tensor_tensor(
                        s_y2[:, D * g + 512 * dc: D * g + 512 * (dc + 1)],
                        s_f[:, :],
                        s_y1D[:, yoff + D * g + 512 * dc: yoff + D * g + 512 * (dc + 1)],
                        ALU.add), reads=(fb, y1Db[xsl]), writes=(y2b[g],))
                ysl = s_y2[:, D * g:D * (g + 1)]
                c0 = 4 * g
                kb.op("scalar", lambda s, ysl=ysl, c0=c0: s.activation(
                    s_dumpD[:, :], ysl, AF.Identity,
                    accum_out=s_lncD[:, c0:c0 + 1]),
                    reads=(y2b[g],), writes=(dumpbuf(), lncDb))
                kb.op("scalar", lambda s, ysl=ysl, c0=c0: s.activation(
                    s_dumpD[:, :], ysl, AF.Square,
                    accum_out=s_lncD[:, c0 + 1:c0 + 2]),
                    reads=(y2b[g],), writes=(dumpbuf(), lncDb))

            def d_ln_finish(c8):
                t0 = c8 * TC
                mu = s_lncD[:, 0:16:4]
                ss2 = s_lncD[:, 1:16:4]
                var = s_lncD[:, 2:16:4]
                rst = s_lncD[:, 3:16:4]
                kb.op("vector", lambda v: v.tensor_scalar(
                    mu, mu, 1.0 / D, None, ALU.mult), reads=(lncDb,), writes=(lncDb,))
                kb.op("vector", lambda v: v.tensor_tensor(
                    var, mu, mu, ALU.mult), reads=(lncDb,), writes=(lncDb,))
                kb.op("vector", lambda v: v.tensor_scalar(
                    ss2, ss2, 1.0 / D, EPS, ALU.mult, ALU.add),
                    reads=(lncDb,), writes=(lncDb,))
                kb.op("vector", lambda v: v.tensor_tensor(
                    var, ss2, var, ALU.subtract), reads=(lncDb,), writes=(lncDb,))
                kb.op("scalar", lambda s: s.activation(
                    rst, var, AF.Sqrt), reads=(lncDb,), writes=(lncDb,))
                kb.op("vector", lambda v: v.reciprocal(rst, rst),
                      reads=(lncDb,), writes=(lncDb,))
                kb.op("vector", lambda v, mu=mu, rst=rst: v.scalar_tensor_tensor(
                    s_lncD[:, 16:20], mu, -1.0, rst, ALU.mult, ALU.mult),
                    reads=(lncDb,), writes=(lncDb,))
                for g in range(NSEG):
                    ysl = s_y2[:, D * g:D * (g + 1)]
                    c0 = 4 * g
                    kb.op("scalar", lambda s, ysl=ysl, c0=c0, g=g: s.activation(
                        ysl, ysl, AF.Identity, scale=s_lncD[:, c0 + 3:c0 + 4],
                        bias=s_lncD[:, 16 + g:17 + g]),
                        reads=(y2b[g], lncDb), writes=(y2b[g],))
                    kb.dma("sync", "st_out",
                           out[t0 + CH * g:t0 + CH * (g + 1), :], ysl,
                           reads=(y2b[g],), writes=(Bg["dout"],))

            pend = d_quant_units(0)
            for u in pend:
                u()
            for c8 in range(NTC):
                nxt = d_quant_units(c8 + 1) if c8 + 1 < NTC else []
                for g in range(NSEG):
                    d_mm_epi(c8, g)
                    for _ in range(7):
                        if nxt:
                            nxt.pop(0)()
                d_ln_finish(c8)
                while nxt:
                    nxt.pop(0)()
            kb.final_wait("sync", ["st_out"])
            kb.fence(("sync",))
        kb.finish()
    return nc




def _in_maps(inputs, T=4096, ncore=NCORE):
    import ml_dtypes
    f32 = np.float32
    x = np.asarray(inputs["L0_attn_x"], f32)[:, :T, :]
    g = lambda n: np.ascontiguousarray(np.asarray(inputs[n], f32))
    wq, wk = g("L0_attn_Wq"), g("L0_attn_Wk")
    wv, wgm = g("L0_attn_Wv"), g("L0_attn_Wg")
    wgk1, wgk2 = g("L0_attn_Wgk1"), g("L0_attn_Wgk2")
    bgk2, wo = g("L0_attn_bgk2"), g("L0_attn_Wo")
    rms_w = g("L0_attn_rms_w")
    w_gate, w_up, w_down = g("L0_ffn_w_gate"), g("L0_ffn_w_up"), g("L0_ffn_w_down")
    clip = np.ascontiguousarray(
        np.stack([np.asarray(inputs["L0_ffn_clip_x"], f32).ravel()[0],
                  np.asarray(inputs["L0_ffn_clip_down"], f32).ravel()[0]])[None, :])
    shared = {
        "wq": wq, "wk": wk, "wv": wv, "wg": wgm,
        "wgk1": wgk1, "wgk2": wgk2,
        "bgk": np.ascontiguousarray(bgk2.reshape(8, 128).T),
        "wo": wo,
        "rmsw": np.ascontiguousarray(np.tile(rms_w[None, :], (128, 4))),
        "wgo": w_gate, "wuo": w_up, "wdo": w_down,

        "clip": clip,
        "ident": np.eye(128, dtype=ml_dtypes.bfloat16),
        "mask": np.ascontiguousarray(np.triu(np.ones((128, 128), f32))),
        "ones1": np.ones((1, 128), f32),
    }
    maps = []
    for c in range(min(ncore, x.shape[0])):
        m = dict(shared)
        m["xf"] = np.ascontiguousarray(x[c].T)
        m["xt"] = np.ascontiguousarray(x[c])
        m["wg_sl"] = np.ascontiguousarray(w_gate[:, 128 * c:128 * (c + 1)])
        m["wu_sl"] = np.ascontiguousarray(w_up[:, 128 * c:128 * (c + 1)])
        m["wd_sl"] = np.ascontiguousarray(w_down[128 * c:128 * (c + 1), :])
        maps.append(m)
    return maps


def kernel(**inputs):
    T = 4096
    maps = _in_maps(inputs, T)
    nc = build(T)
    res = run_bass_kernel_spmd(nc, maps, core_ids=list(range(NCORE)))
    out = np.stack([np.asarray(res.results[c]["out"]) for c in range(NCORE)], 0)
    return np.ascontiguousarray(out.astype(np.float32))
